# version 12
# baseline (speedup 1.0000x reference)
import numpy as np
import ml_dtypes
from contextlib import ExitStack

import concourse.bass as bass
import concourse.mybir as mybir
from concourse.bass_utils import run_bass_kernel_spmd

F32 = mybir.dt.float32
BF16 = mybir.dt.bfloat16
I32 = mybir.dt.int32
AF = mybir.ActivationFunctionType
ALU = mybir.AluOpType
AX = mybir.AxisListType
BF = ml_dtypes.bfloat16

NCORES = 8
D = 2048
FF0 = 5632
FF1 = 8192
NH = 16
HD = 128
SEQ = 4096
BATCH = 2
TOK = 1024
CH = 128
NCHUNK = TOK // CH


class Sched:

    def __init__(self, nc, block, st):
        self.nc = nc
        self.block = block
        self.dsem = {"sync": st.enter_context(nc.semaphore("dsemS")),
                     "gpsimd": st.enter_context(nc.semaphore("dsemG"))}
        self.pfsem = {"sync": st.enter_context(nc.semaphore("pfsemS")),
                      "gpsimd": st.enter_context(nc.semaphore("pfsemG"))}
        self.csem = st.enter_context(nc.semaphore("csem"))
        self.tsem = st.enter_context(nc.semaphore("tsem"))
        self.d = {"sync": 0, "gpsimd": 0}
        self.p = {"sync": 0, "gpsimd": 0}
        self.c = 0
        self.t = 0
        self.floor = (0, 0, 0, 0, 0)
        self.floor_t = 0
        self.seen = {}
        self.pend_eng = None
        self.pend = []

    def _emit(self, engname, body):
        if self.pend_eng != engname:
            self.flush()
            self.pend_eng = engname
        self.pend.append(body)
        if len(self.pend) >= 64:
            self.flush()

    def flush(self):
        if self.pend:
            bodies = self.pend
            self.pend = []
            getattr(self.block, self.pend_eng)(
                lambda eng: [b(eng) for b in bodies])
        self.pend_eng = None

    def fence(self):
        self.flush()
        self.floor = (self.c, self.d["sync"], self.d["gpsimd"],
                      self.p["sync"], self.p["gpsimd"])
        self.floor_t = self.t

    def _w(self, eng, name, c_t, ds_t, dg_t, ps_t=0, pg_t=0, t_t=0):
        s = self.seen.setdefault(name, {"c": 0, "s": 0, "g": 0, "ps": 0, "pg": 0, "t": 0})
        t_t = max(t_t, self.floor_t)
        if ds_t > s["s"]:
            eng.wait_ge(self.dsem["sync"], ds_t)
            s["s"] = ds_t
        if dg_t > s["g"]:
            eng.wait_ge(self.dsem["gpsimd"], dg_t)
            s["g"] = dg_t
        if ps_t > s["ps"]:
            eng.wait_ge(self.pfsem["sync"], ps_t)
            s["ps"] = ps_t
        if pg_t > s["pg"]:
            eng.wait_ge(self.pfsem["gpsimd"], pg_t)
            s["pg"] = pg_t
        if t_t > s["t"]:
            eng.wait_ge(self.tsem, t_t)
            s["t"] = t_t
        if c_t > s["c"]:
            eng.wait_ge(self.csem, c_t)
            s["c"] = c_t

    def _dneed(self, dneed):
        if dneed == "all":
            return self.d["sync"], self.d["gpsimd"], 0, 0
        ds = dg = ps = pg = 0
        for t in (dneed or ()):
            q, v = t
            if q == "sync":
                ds = max(ds, v)
            elif q == "gpsimd":
                dg = max(dg, v)
            elif q == "ps":
                ps = max(ps, v)
            else:
                pg = max(pg, v)
        return ds, dg, ps, pg

    def dma(self, engname, out, in_, war=None, detach=False, twar=0, **kw):
        fc, fds, fdg, fps, fpg = self.floor
        c_t = max(self.c if war is None else war, fc)
        sem = self.pfsem[engname] if detach else self.dsem[engname]
        prev = (self.p if detach else self.d)[engname]
        def body(eng):
            self._w(eng, engname, c_t, fds, fdg, fps, fpg, twar)
            if prev:
                eng.wait_ge(sem, prev)
            eng.dma_start(out=out, in_=in_, **kw).then_inc(sem, 16)
        self._emit(engname, body)
        if detach:
            self.p[engname] += 16
            return (("ps" if engname == "sync" else "pg"), self.p[engname])
        self.d[engname] += 16
        return (engname, self.d[engname])

    def op(self, engname, fn, dneed="all", tneed=0):
        ds_t, dg_t, ps_t, pg_t = self._dneed(dneed)
        c_t = self.c
        def body(eng):
            self._w(eng, engname, c_t, ds_t, dg_t, ps_t, pg_t, tneed)
            fn(eng).then_inc(self.csem, 1)
        self._emit(engname, body)
        self.c += 1
        return self.c

    def top_group(self, fns, cneed=0, dneed=None):
        ds_t, dg_t, ps_t, pg_t = self._dneed(dneed)
        c_t = max(cneed, self.floor[0])
        n = len(fns)

        def body(eng):
            self._w(eng, "tensor", c_t, ds_t, dg_t, ps_t, pg_t, 0)
            for i, fn in enumerate(fns):
                ins = fn(eng)
                if i == n - 1:
                    ins.then_inc(self.tsem, 1)
        self._emit("tensor", body)
        self.t += 1
        return self.t

    def ops(self, engname, fns):
        for fn in fns:
            self.op(engname, fn)

    def mm_group(self, fns, dneed="all"):
        n = len(fns)
        ds_t, dg_t, ps_t, pg_t = self._dneed(dneed)
        c_t = self.c

        def body(eng):
            self._w(eng, "tensor", c_t, ds_t, dg_t, ps_t, pg_t)
            for i, fn in enumerate(fns):
                ins = fn(eng)
                if i == n - 1:
                    ins.then_inc(self.csem, 1)
        self._emit("tensor", body)
        self.c += 1

    def cc(self, kind, op, ins, outs):
        from concourse.aot_env import allow_ici_all_to_all
        ds_t, dg_t, ps_t, pg_t = self._dneed("all")
        c_t = self.c

        t_t = self.t

        def body(eng):
            self._w(eng, "gpsimd", c_t, ds_t, dg_t, self.p["sync"], self.p["gpsimd"], t_t)
            with allow_ici_all_to_all():
                eng.collective_compute(
                    kind, op, replica_groups=[list(range(NCORES))],
                    ins=[a.opt() for a in ins], outs=[a.opt() for a in outs],
                ).then_inc(self.csem, 1)
        self.flush()
        self.block.gpsimd(body)
        self.c += 1

    def finish(self, engname="sync"):
        self.flush()
        def body(eng):
            eng.wait_ge(self.dsem["sync"], self.d["sync"])
            eng.wait_ge(self.dsem["gpsimd"], self.d["gpsimd"])
            if self.p["sync"]:
                eng.wait_ge(self.pfsem["sync"], self.p["sync"])
            if self.p["gpsimd"]:
                eng.wait_ge(self.pfsem["gpsimd"], self.p["gpsimd"])
            if self.t:
                eng.wait_ge(self.tsem, self.t)
            eng.wait_ge(self.csem, self.c)
        getattr(self.block, engname)(body)


def _ceil(a, b):
    return (a + b - 1) // b



def build_A():
    nc = bass.Bass("TRN2", target_bir_lowering=False)
    R1 = FF0 // 8
    R3 = D // 8
    RG = FF1 // 8
    RD = D // 8

    w1s = nc.dram_tensor("w1s", [R1, D], F32, kind="ExternalInput").ap()
    w2s = nc.dram_tensor("w2s", [R1, D], F32, kind="ExternalInput").ap()
    w3s = nc.dram_tensor("w3s", [R3, FF0], F32, kind="ExternalInput").ap()
    gws = nc.dram_tensor("gws", [RG, D], I32, kind="ExternalInput").ap()
    uws = nc.dram_tensor("uws", [RG, D], I32, kind="ExternalInput").ap()
    dws = nc.dram_tensor("dws", [RD, FF1], I32, kind="ExternalInput").ap()
    gsc = nc.dram_tensor("gsc", [RG], F32, kind="ExternalInput").ap()
    usc = nc.dram_tensor("usc", [RG], F32, kind="ExternalInput").ap()
    dsc = nc.dram_tensor("dsc", [RD], F32, kind="ExternalInput").ap()

    gsum = nc.dram_tensor("gsum", [1, 4], F32, kind="ExternalOutput").ap()
    gwq = nc.dram_tensor("gwq", [RG, D], BF16, kind="ExternalOutput").ap()
    uwq = nc.dram_tensor("uwq", [RG, D], BF16, kind="ExternalOutput").ap()
    dwq = nc.dram_tensor("dwq", [RD, FF1], BF16, kind="ExternalOutput").ap()

    with ExitStack() as st:
        sb = lambda n, s, t: st.enter_context(nc.sbuf_tensor(n, s, t))
        ps = lambda n, s, t: st.enter_context(nc.psum_tensor(n, s, t))
        s_w = sb("s_w", [128, FF1], F32)
        s_col = sb("s_col", [128, 1], F32)
        s_ones = sb("s_ones", [128, 1], F32)
        s_red = sb("s_red", [128, 1], F32)
        s_q = sb("s_q", [128, FF1], BF16)
        s_gs = sb("s_gs", [1, 4], F32)
        p_gs = ps("p_gs", [1, 4], F32)
        block = st.enter_context(nc.Block())
        S = Sched(nc, block, st)

        S.op("gpsimd", lambda g: g.memset(s_ones[:], 1.0))
        S.op("gpsimd", lambda g: g.memset(s_gs[:], 0.0))

        def abssum(src, rows, cols, k):
            nt = _ceil(rows, 128)
            for t in range(nt):
                pc = min(128, rows - t * 128)
                S.dma("sync", s_w[0:pc, 0:cols], src[t * 128:t * 128 + pc, :])
                S.op("vector", lambda v, pc=pc, cols=cols: v.tensor_reduce(
                    s_red[0:pc, :], s_w[0:pc, 0:cols], axis=AX.X, op=ALU.add,
                    apply_absolute_value=True))
                S.op("tensor", lambda t_, pc=pc: t_.matmul(
                    p_gs[0:1, 0:1], s_red[0:pc, :], s_ones[0:pc, :],
                    start=True, stop=True))
                S.op("vector", lambda v, k=k: v.tensor_tensor(
                    s_gs[0:1, k:k + 1], s_gs[0:1, k:k + 1], p_gs[0:1, 0:1], ALU.add))

        abssum(w1s, R1, D, 0)
        abssum(w2s, R1, D, 1)
        abssum(w3s, R3, FF0, 2)
        S.dma("sync", gsum, s_gs[:])

        def dequant(srcw, srcs, dst, rows, cols):
            nt = _ceil(rows, 128)
            for t in range(nt):
                pc = min(128, rows - t * 128)
                S.dma("gpsimd", s_w[0:pc, 0:cols], srcw[t * 128:t * 128 + pc, :])
                S.dma("sync", s_col[0:pc, :], srcs[t * 128:t * 128 + pc].unsqueeze(1))
                S.op("vector", lambda v, pc=pc, cols=cols: v.tensor_scalar(
                    s_q[0:pc, 0:cols], s_w[0:pc, 0:cols], -128.0, s_col[0:pc, 0:1],
                    ALU.add, ALU.mult))
                S.dma("sync", dst[t * 128:t * 128 + pc, :], s_q[0:pc, 0:cols])

        dequant(gws, gsc, gwq, RG, D)
        dequant(uws, usc, uwq, RG, D)
        dequant(dws, dsc, dwq, RD, FF1)

        S.finish()
    return nc



def build_B():
    nc = bass.Bass("TRN2", target_bir_lowering=False)
    R1 = FF0 // 8
    R3 = D // 8
    w1s = nc.dram_tensor("w1s", [R1, D], F32, kind="ExternalInput").ap()
    w2s = nc.dram_tensor("w2s", [R1, D], F32, kind="ExternalInput").ap()
    w3s = nc.dram_tensor("w3s", [R3, FF0], F32, kind="ExternalInput").ap()
    gsums = nc.dram_tensor("gsums", [8, 4], F32, kind="ExternalInput").ap()
    w1q = nc.dram_tensor("w1q", [R1, D], BF16, kind="ExternalOutput").ap()
    w2q = nc.dram_tensor("w2q", [R1, D], BF16, kind="ExternalOutput").ap()
    w3q = nc.dram_tensor("w3q", [R3, FF0], BF16, kind="ExternalOutput").ap()

    nelem = [FF0 * D, FF0 * D, D * FF0]

    with ExitStack() as st:
        sb = lambda n, s, t: st.enter_context(nc.sbuf_tensor(n, s, t))
        ps = lambda n, s, t: st.enter_context(nc.psum_tensor(n, s, t))
        s_gp = sb("s_gp", [8, 4], F32)
        s_ones8 = sb("s_ones8", [8, 1], F32)
        s_ones1 = sb("s_ones1", [1, 128], F32)
        s_thr_row = sb("s_thr_row", [1, 4], F32)
        s_thr = sb("s_thr", [128, 4], F32)
        s_nthr = sb("s_nthr", [128, 4], F32)
        s_w = sb("s_w", [128, FF0], F32)
        s_a = sb("s_a", [128, FF0], F32)
        s_b = sb("s_b", [128, FF0], F32)
        s_q = sb("s_q", [128, FF0], BF16)
        p_sum = ps("p_sum", [1, 4], F32)
        p_bc = ps("p_bc", [128, 4], F32)
        block = st.enter_context(nc.Block())
        S = Sched(nc, block, st)

        S.op("gpsimd", lambda g: g.memset(s_ones8[:], 1.0))
        S.op("gpsimd", lambda g: g.memset(s_ones1[:], 1.0))
        S.dma("sync", s_gp[:], gsums)
        S.op("tensor", lambda t: t.matmul(p_sum[:], s_ones8[:], s_gp[:], start=True, stop=True))
        S.op("vector", lambda v: v.tensor_copy(s_thr_row[:], p_sum[:]))
        for k in range(3):
            S.op("vector", lambda v, k=k: v.tensor_scalar(
                s_thr_row[0:1, k:k + 1], s_thr_row[0:1, k:k + 1],
                0.5 / nelem[k], 0.5e-8, ALU.mult, ALU.add))
        S.op("tensor", lambda t: t.matmul(p_bc[:], s_ones1[:], s_thr_row[:], start=True, stop=True))
        S.op("vector", lambda v: v.tensor_copy(s_thr[:], p_bc[:]))
        S.op("vector", lambda v: v.tensor_scalar_mul(s_nthr[:], s_thr[:], -1.0))

        def quant(src, dst, rows, cols, k):
            nt = _ceil(rows, 128)
            for t in range(nt):
                pc = min(128, rows - t * 128)
                S.dma("sync", s_w[0:pc, 0:cols], src[t * 128:t * 128 + pc, :])
                S.op("vector", lambda v, pc=pc, cols=cols, k=k: v.tensor_scalar(
                    s_a[0:pc, 0:cols], s_w[0:pc, 0:cols], s_thr[0:pc, k:k + 1], None, ALU.is_ge))
                S.op("vector", lambda v, pc=pc, cols=cols, k=k: v.tensor_scalar(
                    s_b[0:pc, 0:cols], s_w[0:pc, 0:cols], s_nthr[0:pc, k:k + 1], None, ALU.is_le))
                S.op("vector", lambda v, pc=pc, cols=cols: v.tensor_tensor(
                    s_q[0:pc, 0:cols], s_a[0:pc, 0:cols], s_b[0:pc, 0:cols], ALU.subtract))
                S.dma("sync", dst[t * 128:t * 128 + pc, :], s_q[0:pc, 0:cols])

        quant(w1s, w1q, R1, D, 0)
        quant(w2s, w2q, R1, D, 1)
        quant(w3s, w3q, R3, FF0, 2)
        S.finish()
    return nc



def build_C(debug=False):
    nc = bass.Bass("TRN2", target_bir_lowering=False)
    x_s = nc.dram_tensor("x_s", [TOK, D], F32, kind="ExternalInput").ap()
    wqT = nc.dram_tensor("wqT", [D, D], F32, kind="ExternalInput").ap()
    wkT = nc.dram_tensor("wkT", [D, D], F32, kind="ExternalInput").ap()
    wvT = nc.dram_tensor("wvT", [D, D], F32, kind="ExternalInput").ap()
    w1qT = nc.dram_tensor("w1qT", [D, FF0], BF16, kind="ExternalInput").ap()
    w2qT = nc.dram_tensor("w2qT", [D, FF0], BF16, kind="ExternalInput").ap()
    w3qT = nc.dram_tensor("w3qT", [FF0, D], BF16, kind="ExternalInput").ap()
    triu = nc.dram_tensor("triu", [128, 128], BF16, kind="ExternalInput").ap()
    ident = nc.dram_tensor("ident", [128, 128], BF16, kind="ExternalInput").ap()

    sq_s = nc.dram_tensor("sq_s", [TOK, D], BF16, kind="ExternalOutput").ap()
    num_s = nc.dram_tensor("num_s", [TOK, D], F32, kind="ExternalOutput").ap()
    den_s = nc.dram_tensor("den_s", [TOK, NH], F32, kind="ExternalOutput").ap()
    kvz_s = nc.dram_tensor("kvz_s", [NH, HD, 129], F32, kind="ExternalOutput").ap()
    qsum_s = nc.dram_tensor("qsum_s", [1, D], F32, kind="ExternalOutput").ap()
    g_s = nc.dram_tensor("g_s", [TOK, D], F32, kind="ExternalOutput").ap()
    dbg_q = nc.dram_tensor("dbg_q", [TOK, 512], F32, kind="ExternalOutput").ap() if debug else None
    dbg_xnT = nc.dram_tensor("dbg_xnT", [128, 16 * TOK], BF16, kind="ExternalOutput").ap() if debug else None

    with ExitStack() as st:
        sb = lambda n, s, t: st.enter_context(nc.sbuf_tensor(n, s, t))
        s_xt = sb("s_xt", [128, D], F32)
        s_sqr = sb("s_sqr", [128, D], F32)
        s_mean = sb("s_mean", [128, 1], F32)
        s_sumsq = sb("s_sumsq", [128, 1], F32)
        s_var = sb("s_var", [128, 1], F32)
        s_m2 = sb("s_m2", [128, 1], F32)
        s_rstd = sb("s_rstd", [128, 1], F32)
        s_xnt = sb("s_xnt", [128, D], BF16)
        s_xnT = sb("s_xnT", [128, 16 * TOK], BF16)
        s_id = sb("s_id", [128, 128], BF16)
        s_triu = sb("s_triu", [128, 128], BF16)
        s_ones = sb("s_ones", [128, 1], F32)

        block = st.enter_context(nc.Block())
        S = Sched(nc, block, st)

        S.op("gpsimd", lambda g: g.memset(s_ones[:], 1.0))
        S.dma("sync", s_id[:], ident)
        S.dma("sync", s_triu[:], triu)

        with ExitStack() as ps1:
            sbl = lambda n, s, t: ps1.enter_context(nc.sbuf_tensor(n, s, t))
            s_wq = sbl("s_wq", [128, 16 * 512], BF16)
            s_wk = sbl("s_wk", [128, 16 * 512], BF16)
            s_wv = sbl("s_wv", [128, 16 * 512], BF16)
            s_sqb = sbl("s_sqb", [128, NCHUNK * 512], BF16)
            s_skb = sbl("s_skb", [128, NCHUNK * 512], BF16)
            s_vb = sbl("s_vb", [128, 4 * NCHUNK * 132], BF16)
            s_t1 = sbl("s_t1", [128, 512], F32)
            s_t2 = sbl("s_t2", [128, 512], F32)
            s_qf = sbl("s_qf", [128, 512], F32)
            s_state = sbl("s_state", [128, 4 * 132], F32)
            s_spb = sbl("s_spb", [128, 132], BF16)
            s_qpT = sbl("s_qpT", [128, 128], BF16)
            s_kpT = sbl("s_kpT", [128, 128], BF16)
            s_ms = sbl("s_ms", [128, 128], BF16)
            s_numst = sbl("s_numst", [128, 512], F32)
            s_denst = sbl("s_denst", [128, NCHUNK * NH], F32)
            s_qsumst = sbl("s_qsumst", [1, D], F32)
            psc = lambda n, s, t: ps1.enter_context(nc.psum_tensor(n, s, t))
            pS = psc("pS", [1, 512], F32)
            pA = psc("pA", [128, 512], F32)
            pB = psc("pB", [128, 512], F32)
            pC = psc("pC", [128, 512], F32)
            pD = psc("pD", [128, 129], F32)
            pE = psc("pE", [128, 129], F32)
            pF = psc("pF", [128, 128], F32)
            pT = psc("pT", [128, 128], BF16)

            for tt in range(NCHUNK):
                S.dma("sync", s_xt[:], x_s[tt * 128:(tt + 1) * 128, :])
                S.op("vector", lambda v: v.reduce_sum(s_mean[:], s_xt[:], axis=AX.X))
                S.op("vector", lambda v: v.tensor_scalar_mul(s_mean[:], s_mean[:], 1.0 / D))
                S.op("scalar", lambda sc: sc.activation(s_sqr[:], s_xt[:], AF.Square, accum_out=s_sumsq[:]))
                S.op("vector", lambda v: v.tensor_scalar_mul(s_var[:], s_sumsq[:], 1.0 / D))
                S.op("vector", lambda v: v.scalar_tensor_tensor(s_m2[:], s_mean[:], -1.0, s_mean[:], ALU.mult, ALU.mult))
                S.op("vector", lambda v: v.tensor_tensor(s_var[:], s_var[:], s_m2[:], ALU.add))
                S.op("vector", lambda v: v.tensor_scalar_add(s_var[:], s_var[:], 1e-5))
                S.op("scalar", lambda sc: sc.activation(s_rstd[:], s_var[:], AF.Ln))
                S.op("scalar", lambda sc: sc.activation(s_rstd[:], s_rstd[:], AF.Exp, scale=-0.5))
                S.op("vector", lambda v: v.tensor_scalar(s_xnt[:], s_xt[:], s_mean[:, 0:1], s_rstd[:, 0:1], ALU.subtract, ALU.mult))
                for dc in range(16):
                    S.op("tensor", lambda t, dc=dc: t.transpose(pT[:], s_xnt[:, dc * 128:(dc + 1) * 128], s_id[:]))
                    S.op("vector", lambda v, dc=dc, tt=tt: v.tensor_copy(
                        s_xnT[:, dc * TOK + tt * 128: dc * TOK + (tt + 1) * 128], pT[:]))

            for blk in range(4):
                cols = slice(blk * 512, (blk + 1) * 512)
                for (wsrc, wdst) in ((wqT, s_wq), (wkT, s_wk), (wvT, s_wv)):
                    S.dma("gpsimd",
                          wdst[:].rearrange("p (dc e) -> p dc e", e=512),
                          wsrc[:, cols].rearrange("(dc p) e -> p dc e", p=128))
                S.op("gpsimd", lambda g: g.memset(s_vb[:], 0.0))
                S.op("gpsimd", lambda g: g.memset(
                    s_vb[:].rearrange("p (t c) -> p t c", c=132)[:, :, 128:129], 1.0))
                for tt in range(NCHUNK):
                    xn_t = lambda dc, tt=tt: s_xnT[:, dc * TOK + tt * 128: dc * TOK + (tt + 1) * 128]
                    for (wdst, pdst) in ((s_wq, pA), (s_wk, pB), (s_wv, pC)):
                        for dc in range(16):
                            S.op("tensor", lambda t, dc=dc, wdst=wdst, pdst=pdst, tt=tt: t.matmul(
                                pdst[:], xn_t(dc), wdst[:, dc * 512:(dc + 1) * 512],
                                start=(dc == 0), stop=(dc == 15)))
                    S.op("scalar", lambda sc: sc.copy(s_qf[:], pA[:]))
                    S.op("tensor", lambda t, tt=tt: t.matmul(
                        pS[:], s_ones[:], s_qf[:], start=(tt == 0), stop=(tt == NCHUNK - 1)))
                    if debug and blk == 0:
                        S.dma("sync", dbg_q[tt * 128:(tt + 1) * 128, :], s_qf[:])
                    for (psrc, dstbuf) in ((pA, s_sqb), (pB, s_skb)):
                        S.op("vector", lambda v, psrc=psrc: v.tensor_scalar_min(s_t1[:], psrc[:], 0.0))
                        S.op("scalar", lambda sc: sc.activation(s_t2[:], s_t1[:], AF.Exp))
                        S.op("vector", lambda v, psrc=psrc: v.tensor_scalar_max(s_t1[:], psrc[:], 0.0))
                        S.op("vector", lambda v, dstbuf=dstbuf, tt=tt: v.tensor_tensor(
                            dstbuf[:, tt * 512:(tt + 1) * 512], s_t2[:], s_t1[:], ALU.add))
                    for h in range(4):
                        S.op("vector", lambda v, h=h, tt=tt: v.tensor_copy(
                            s_vb[:, (h * NCHUNK + tt) * 132:(h * NCHUNK + tt) * 132 + 128],
                            pC[:, h * 128:(h + 1) * 128]))
                    S.dma("sync", sq_s[tt * 128:(tt + 1) * 128, cols], s_sqb[:, tt * 512:(tt + 1) * 512])
                S.op("vector", lambda v, blk=blk: v.tensor_copy(s_qsumst[0:1, blk * 512:(blk + 1) * 512], pS[:]))
                S.op("gpsimd", lambda g: g.memset(s_state[:], 0.0))
                for n in range(NCHUNK):
                    for h in range(4):
                        qp = s_sqb[:, n * 512 + h * 128: n * 512 + (h + 1) * 128]
                        kp = s_skb[:, n * 512 + h * 128: n * 512 + (h + 1) * 128]
                        vt = s_vb[:, (h * NCHUNK + n) * 132:(h * NCHUNK + n) * 132 + 129]
                        stt = s_state[:, h * 132:h * 132 + 129]
                        S.op("vector", lambda v, stt=stt: v.tensor_copy(s_spb[:, 0:129], stt))
                        S.op("tensor", lambda t, qp=qp: t.transpose(pT[:], qp, s_id[:]))
                        S.op("vector", lambda v: v.tensor_copy(s_qpT[:], pT[:]))
                        S.op("tensor", lambda t, kp=kp: t.transpose(pT[:], kp, s_id[:]))
                        S.op("vector", lambda v: v.tensor_copy(s_kpT[:], pT[:]))
                        S.op("tensor", lambda t, kp=kp, vt=vt: t.matmul(pD[:], kp, vt, start=True, stop=True))
                        S.op("vector", lambda v, stt=stt: v.tensor_tensor(stt, stt, pD[:], ALU.add))
                        S.op("tensor", lambda t: t.matmul(pF[:], s_kpT[:], s_qpT[:], start=True, stop=True))
                        S.op("vector", lambda v: v.tensor_tensor(s_ms[:], pF[:], s_triu[:], ALU.mult))
                        S.op("tensor", lambda t: t.matmul(pE[:], s_qpT[:], s_spb[:, 0:129], start=True, stop=False))
                        S.op("tensor", lambda t, vt=vt: t.matmul(pE[:], s_ms[:], vt, start=False, stop=True))
                        S.op("vector", lambda v, h=h: v.tensor_copy(
                            s_numst[:, h * 128:(h + 1) * 128], pE[:, 0:128]))
                        S.op("vector", lambda v, n=n, h=h, blk=blk: v.tensor_copy(
                            s_denst[:, n * NH + blk * 4 + h: n * NH + blk * 4 + h + 1], pE[:, 128:129]))
                    S.dma("sync", num_s[n * 128:(n + 1) * 128, cols], s_numst[:])
                for h in range(4):
                    S.dma("sync", kvz_s[blk * 4 + h], s_state[:, h * 132:h * 132 + 129])
            if debug:
                S.dma("sync", dbg_xnT, s_xnT[:])
            S.dma("sync", qsum_s, s_qsumst[:])
            S.dma("sync",
                  den_s.rearrange("(n t) h -> t n h", t=128),
                  s_denst[:].rearrange("p (n h) -> p n h", h=NH))

        with ExitStack() as ps2:
            sbl = lambda n, s, t: ps2.enter_context(nc.sbuf_tensor(n, s, t))
            s_w1b = sbl("s_w1b", [128, 16 * 512], BF16)
            s_w2b = sbl("s_w2b", [128, 16 * 512], BF16)
            s_hT = sbl("s_hT", [128, 44 * 512], BF16)
            s_w3r = sbl("s_w3r", [128, D], BF16)
            s_sg = sbl("s_sg", [128, 512], F32)
            s_gst = sbl("s_gst", [128, D], F32)
            psc = lambda n, s, t: ps2.enter_context(nc.psum_tensor(n, s, t))
            pf = [psc(f"pf{i}", [128, 512], F32) for i in range(8)]
            for half in range(2):
                hoff = half * 512
                for hb in range(11):
                    for (wsrc, wdst) in ((w1qT, s_w1b), (w2qT, s_w2b)):
                        S.dma("sync",
                              wdst[:].rearrange("p (dc e) -> p dc e", e=512),
                              wsrc[:, hb * 512:(hb + 1) * 512].rearrange("(dc p) e -> p dc e", p=128))
                    for hc in range(4):
                        for (wdst, pdst) in ((s_w1b, pf[0]), (s_w2b, pf[1])):
                            for dc in range(16):
                                S.op("tensor", lambda t, dc=dc, wdst=wdst, pdst=pdst, hc=hc: t.matmul(
                                    pdst[:], wdst[:, dc * 512 + hc * 128: dc * 512 + (hc + 1) * 128],
                                    s_xnT[:, dc * TOK + hoff: dc * TOK + hoff + 512],
                                    start=(dc == 0), stop=(dc == 15)))
                        S.op("scalar", lambda sc: sc.activation(s_sg[:], pf[0][:], AF.Sigmoid))
                        S.op("vector", lambda v, hb=hb, hc=hc: v.tensor_tensor(
                            s_hT[:, (hb * 4 + hc) * 512:(hb * 4 + hc + 1) * 512],
                            s_sg[:], pf[1][:], ALU.mult))
                for tp in range(2):
                    for hc44 in range(44):
                        S.dma("sync", s_w3r[:], w3qT[hc44 * 128:(hc44 + 1) * 128, :])
                        for t2 in range(2):
                            for eb in range(4):
                                S.op("tensor", lambda t, hc44=hc44, t2=t2, eb=eb, tp=tp: t.matmul(
                                    pf[t2 * 4 + eb][:],
                                    s_hT[:, hc44 * 512 + (tp * 2 + t2) * 128: hc44 * 512 + (tp * 2 + t2 + 1) * 128],
                                    s_w3r[:, eb * 512:(eb + 1) * 512],
                                    start=(hc44 == 0), stop=(hc44 == 43)))
                    for t2 in range(2):
                        for eb in range(4):
                            S.op("scalar", lambda sc, t2=t2, eb=eb: sc.activation(
                                s_gst[:, eb * 512:(eb + 1) * 512], pf[t2 * 4 + eb][:], AF.Silu))
                        row0 = (half * 4 + tp * 2 + t2) * 128
                        S.dma("sync", g_s[row0:row0 + 128, :], s_gst[:])

        S.finish()
    return nc


def run_C(inputs, w1q, w2q, w3q):
    nc = build_C()
    x = np.ascontiguousarray(inputs["L0_attn_hidden_states"].reshape(SEQ * BATCH, D))
    wqT = np.ascontiguousarray(inputs["L0_attn_w_q"].T)
    wkT = np.ascontiguousarray(inputs["L0_attn_w_k"].T)
    wvT = np.ascontiguousarray(inputs["L0_attn_w_v"].T)
    w1qT = np.ascontiguousarray(w1q.T)
    w2qT = np.ascontiguousarray(w2q.T)
    w3qT = np.ascontiguousarray(w3q.T)
    triu = np.triu(np.ones((128, 128), BF))
    ident = np.eye(128, dtype=BF)
    in_maps = []
    for r in range(NCORES):
        in_maps.append({
            "x_s": np.ascontiguousarray(x[r * TOK:(r + 1) * TOK]),
            "wqT": wqT, "wkT": wkT, "wvT": wvT,
            "w1qT": w1qT, "w2qT": w2qT, "w3qT": w3qT,
            "triu": triu, "ident": ident,
        })
    res = run_bass_kernel_spmd(nc, in_maps, core_ids=list(range(NCORES)))
    rs = res.results
    return {k: np.stack([rs[r][k] for r in range(NCORES)], axis=0)
            for k in ("sq_s", "num_s", "den_s", "kvz_s", "qsum_s", "g_s")}



def build_D():
    nc = bass.Bass("TRN2", target_bir_lowering=False)
    x_s = nc.dram_tensor("x_s", [TOK, D], F32, kind="ExternalInput").ap()
    sq_s = nc.dram_tensor("sq_s", [TOK, D], BF16, kind="ExternalInput").ap()
    num_s = nc.dram_tensor("num_s", [TOK, D], F32, kind="ExternalInput").ap()
    den_s = nc.dram_tensor("den_s", [TOK, NH], F32, kind="ExternalInput").ap()
    pkvz = nc.dram_tensor("pkvz", [3, NH, HD, 129], F32, kind="ExternalInput").ap()
    qsum4 = nc.dram_tensor("qsum4", [4, D], F32, kind="ExternalInput").ap()
    memh = nc.dram_tensor("memh", [NH, HD, 512], F32, kind="ExternalInput").ap()
    mnT = nc.dram_tensor("mnT", [NH, HD, 4], F32, kind="ExternalInput").ap()
    mnm = nc.dram_tensor("mnm", [4, D], F32, kind="ExternalInput").ap()
    gate_c = nc.dram_tensor("gate_c", [NH, 1], F32, kind="ExternalInput").ap()
    gate_r = nc.dram_tensor("gate_r", [1, NH], F32, kind="ExternalInput").ap()
    w_oT = nc.dram_tensor("w_oT", [D, D], F32, kind="ExternalInput").ap()
    g_s = nc.dram_tensor("g_s", [TOK, D], F32, kind="ExternalInput").ap()
    ident = nc.dram_tensor("ident", [128, 128], BF16, kind="ExternalInput").ap()
    x1_s = nc.dram_tensor("x1_s", [TOK, D], F32, kind="ExternalOutput").ap()

    scr_qbar = nc.dram_tensor("scr_qbar", [1, D], F32, kind="Internal").ap()
    scr4 = nc.dram_tensor("scr4", [4, 1], F32, kind="Internal").ap()
    scrwg = nc.dram_tensor("scrwg", [NH, 4], F32, kind="Internal").ap()

    with ExitStack() as st:
        sb = lambda n, s, t: st.enter_context(nc.sbuf_tensor(n, s, t))
        psc = lambda n, s, t: st.enter_context(nc.psum_tensor(n, s, t))
        p_am = psc("p_am", [128, 512], F32)
        p_row = psc("p_row", [1, 512], F32)
        pO = psc("pO", [128, 512], F32)
        pE = psc("pE", [128, 129], F32)
        p_nrm = psc("p_nrm", [128, 4], F32)
        pT = psc("pT", [128, 128], BF16)
        p_b16 = psc("p_b16", [128, 4], F32)
        p_b128 = psc("p_b128", [128, NH], F32)

        s_id = sb("s_id", [128, 128], BF16)
        s_ones1 = sb("s_ones1", [1, 128], F32)
        s_ones4 = sb("s_ones4", [4, 1], F32)
        s_omg = sb("s_omg", [128, NH], F32)
        s_wb = sb("s_wb", [128, NH * 4], F32)
        s_attn = sb("s_attn", [128, NCHUNK * D], BF16)

        S = None
        block = st.enter_context(nc.Block())
        S = Sched(nc, block, st)
        S.dma("sync", s_id[:], ident)
        S.op("gpsimd", lambda g: g.memset(s_ones1[:], 1.0))
        S.op("gpsimd", lambda g: g.memset(s_ones4[:], 1.0))

        with ExitStack() as ph1:
            sbl = lambda n, s, t: ph1.enter_context(nc.sbuf_tensor(n, s, t))
            s_sq = sbl("s_sq", [128, NCHUNK * D], BF16)
            s_mem = sbl("s_mem", [128, NH * 512], BF16)
            s_mnT = sbl("s_mnT", [128, NH * 4], BF16)
            s_sbf = sbl("s_sbf", [128, NH * 132], F32)
            s_sbb = sbl("s_sbb", [128, NH * 132], BF16)
            s_t129 = sbl("s_t129", [128, 129], F32)
            s_q4 = sbl("s_q4", [4, D], F32)
            s_qbar_row = sbl("s_qbar_row", [1, D], F32)
            s_qbar16 = sbl("s_qbar16", [NH, 128], F32)
            s_mnm16 = sbl("s_mnm16", [NH, 128], F32)
            s_tmp16 = sbl("s_tmp16", [NH, 128], F32)
            s_rel = sbl("s_rel", [NH, 4], F32)
            s_mnm4 = sbl("s_mnm4", [4, D], F32)
            s_mns = sbl("s_mns", [4, 1], F32)
            s_mrow = sbl("s_mrow", [1, 4], F32)
            s_e16 = sbl("s_e16", [NH, 4], F32)
            s_ke16 = sbl("s_ke16", [NH, 4], F32)
            s_mx = sbl("s_mx", [NH, 1], F32)
            s_sum = sbl("s_sum", [NH, 1], F32)
            s_gc = sbl("s_gc", [NH, 1], F32)
            s_gr = sbl("s_gr", [1, NH], F32)
            s_wgrow = sbl("s_wgrow", [1, 4], F32)
            s_numt = sbl("s_numt", [128, 128], F32)
            s_dent = sbl("s_dent", [128, NH], F32)
            s_numtot = sbl("s_numtot", [128, 128], F32)
            s_dtot = sbl("s_dtot", [128, 1], F32)
            s_acc = sbl("s_acc", [128, 128], F32)
            s_cn = sbl("s_cn", [128, 4], F32)
            s_qpT = sbl("s_qpT", [128, 128], BF16)

            S.dma("gpsimd", s_mem[:].rearrange("p (h e) -> p h e", e=512),
                  memh.rearrange("h d e -> d h e"))
            S.dma("gpsimd", s_mnT[:].rearrange("p (h m) -> p h m", m=4),
                  mnT.rearrange("h d m -> d h m"))
            S.dma("sync", s_sq[:].rearrange("p (n e) -> p n e", e=D),
                  sq_s.rearrange("(n p) e -> p n e", p=128))
            for h in range(NH):
                sl = s_sbf[:, h * 132:h * 132 + 129]
                S.dma("sync", s_t129[:], pkvz[0, h])
                S.op("vector", lambda v, sl=sl: v.tensor_copy(sl, s_t129[:]))
                for j in (1, 2):
                    S.dma("sync", s_t129[:], pkvz[j, h])
                    S.op("vector", lambda v, sl=sl: v.tensor_tensor(sl, sl, s_t129[:], ALU.add))
                S.op("vector", lambda v, h=h: v.tensor_copy(
                    s_sbb[:, h * 132:h * 132 + 129], s_sbf[:, h * 132:h * 132 + 129]))
            S.dma("sync", s_q4[:], qsum4)
            for c in range(4):
                S.op("tensor", lambda t, c=c: t.matmul(
                    p_row[:], s_ones4[:], s_q4[:, c * 512:(c + 1) * 512], start=True, stop=True))
                S.op("vector", lambda v, c=c: v.tensor_scalar_mul(
                    s_qbar_row[0:1, c * 512:(c + 1) * 512], p_row[:], 1.0 / SEQ))
            S.dma("sync", scr_qbar, s_qbar_row[:])
            S.dma("sync", s_qbar16[:], scr_qbar.rearrange("o (h d) -> (o h) d", d=128))
            for m in range(4):
                S.dma("sync", s_mnm16[:], mnm[m:m + 1, :].rearrange("o (h d) -> (o h) d", d=128))
                S.op("vector", lambda v: v.tensor_tensor(s_tmp16[:], s_qbar16[:], s_mnm16[:], ALU.mult))
                S.op("vector", lambda v, m=m: v.reduce_sum(s_rel[:, m:m + 1], s_tmp16[:], axis=AX.X))
            S.op("vector", lambda v: v.tensor_scalar_mul(s_rel[:], s_rel[:], HD ** -0.5))
            S.dma("sync", s_mnm4[:], mnm)
            S.op("vector", lambda v: v.reduce_sum(s_mns[:], s_mnm4[:], axis=AX.X))
            S.dma("sync", scr4, s_mns[:])
            S.dma("sync", s_mrow[:], scr4.rearrange("a b -> b a"))
            S.op("vector", lambda v: v.tensor_single_scalar(s_mrow[:], s_mrow[:], 1e-6, ALU.is_lt))
            S.op("tensor", lambda t: t.matmul(p_b16[0:NH, 0:4], s_ones1[0:1, 0:NH], s_mrow[:], start=True, stop=True))
            S.op("vector", lambda v: v.tensor_copy(s_e16[:], p_b16[0:NH, 0:4]))
            S.op("vector", lambda v: v.tensor_scalar(s_ke16[:], s_e16[:], -1.0, 1.0, ALU.mult, ALU.add))
            S.op("vector", lambda v: v.tensor_tensor(s_rel[:], s_rel[:], s_ke16[:], ALU.mult))
            S.op("vector", lambda v: v.scalar_tensor_tensor(s_rel[:], s_e16[:], -1e30, s_rel[:], ALU.mult, ALU.add))
            S.op("vector", lambda v: v.reduce_max(s_mx[:], s_rel[:], axis=AX.X))
            S.op("vector", lambda v: v.tensor_scalar(s_rel[:], s_rel[:], s_mx[:, 0:1], None, ALU.subtract))
            S.op("scalar", lambda sc: sc.activation(s_rel[:], s_rel[:], AF.Exp))
            S.op("vector", lambda v: v.reduce_sum(s_sum[:], s_rel[:], axis=AX.X))
            S.op("vector", lambda v: v.reciprocal(s_sum[:], s_sum[:]))
            S.op("vector", lambda v: v.tensor_scalar(s_rel[:], s_rel[:], s_sum[:, 0:1], None, ALU.mult))
            S.op("vector", lambda v: v.tensor_tensor(s_rel[:], s_rel[:], s_ke16[:], ALU.mult))
            S.dma("sync", s_gc[:], gate_c)
            S.op("scalar", lambda sc: sc.activation(s_gc[:], s_gc[:], AF.Sigmoid))
            S.op("vector", lambda v: v.tensor_scalar(s_rel[:], s_rel[:], s_gc[:, 0:1], None, ALU.mult))
            S.dma("sync", s_gr[:], gate_r)
            S.op("scalar", lambda sc: sc.activation(s_gr[:], s_gr[:], AF.Sigmoid))
            S.op("vector", lambda v: v.tensor_scalar(s_gr[:], s_gr[:], -1.0, 1.0, ALU.mult, ALU.add))
            S.op("tensor", lambda t: t.matmul(p_b128[:], s_ones1[:], s_gr[:], start=True, stop=True))
            S.op("vector", lambda v: v.tensor_copy(s_omg[:], p_b128[:]))
            S.dma("sync", scrwg, s_rel[:])
            for h in range(NH):
                S.dma("sync", s_wgrow[:], scrwg[h:h + 1, :])
                S.op("tensor", lambda t: t.matmul(p_b16[0:128, 0:4], s_ones1[:], s_wgrow[:], start=True, stop=True))
                S.op("vector", lambda v, h=h: v.tensor_copy(s_wb[:, h * 4:(h + 1) * 4], p_b16[0:128, 0:4]))

            for n in range(NCHUNK):
                S.dma("sync", s_dent[:], den_s[n * 128:(n + 1) * 128, :])
                for h in range(NH):
                    sqt = s_sq[:, n * D + h * 128: n * D + (h + 1) * 128]
                    S.op("tensor", lambda t, sqt=sqt: t.transpose(pT[:], sqt, s_id[:]))
                    S.op("vector", lambda v: v.tensor_copy(s_qpT[:], pT[:]))
                    S.op("tensor", lambda t, h=h: t.matmul(
                        pE[:], s_qpT[:], s_sbb[:, h * 132:h * 132 + 129], start=True, stop=True))
                    S.dma("sync", s_numt[:], num_s[n * 128:(n + 1) * 128, h * 128:(h + 1) * 128])
                    S.op("vector", lambda v: v.tensor_tensor(s_numtot[:], s_numt[:], pE[:, 0:128], ALU.add))
                    S.op("vector", lambda v, h=h: v.tensor_tensor(
                        s_dtot[:], s_dent[:, h:h + 1], pE[:, 128:129], ALU.add))
                    S.op("vector", lambda v: v.tensor_scalar_max(s_dtot[:], s_dtot[:], 1e-6))
                    S.op("vector", lambda v: v.reciprocal(s_dtot[:], s_dtot[:]))
                    S.op("vector", lambda v, h=h: v.tensor_tensor(s_dtot[:], s_dtot[:], s_omg[:, h:h + 1], ALU.mult))
                    S.op("vector", lambda v: v.tensor_scalar(s_acc[:], s_numtot[:], s_dtot[:, 0:1], None, ALU.mult))
                    S.op("tensor", lambda t, h=h: t.matmul(
                        p_nrm[:], s_qpT[:], s_mnT[:, h * 4:(h + 1) * 4], start=True, stop=True))
                    S.op("tensor", lambda t, h=h: t.matmul(
                        p_am[:], s_qpT[:], s_mem[:, h * 512:(h + 1) * 512], start=True, stop=True))
                    S.op("vector", lambda v: v.tensor_scalar_max(s_cn[:], p_nrm[:], 1e-6))
                    S.op("vector", lambda v: v.reciprocal(s_cn[:], s_cn[:]))
                    S.op("vector", lambda v, h=h: v.tensor_tensor(s_cn[:], s_cn[:], s_wb[:, h * 4:(h + 1) * 4], ALU.mult))
                    for m in range(4):
                        S.op("vector", lambda v, m=m: v.scalar_tensor_tensor(
                            s_acc[:], p_am[:, m * 128:(m + 1) * 128], s_cn[:, m:m + 1],
                            s_acc[:], ALU.mult, ALU.add))
                    S.op("vector", lambda v, n=n, h=h: v.tensor_copy(
                        s_attn[:, n * D + h * 128: n * D + (h + 1) * 128], s_acc[:]))

        with ExitStack() as ph2:
            sbl = lambda n, s, t: ph2.enter_context(nc.sbuf_tensor(n, s, t))
            s_wo = sbl("s_wo", [128, 16 * D], BF16)
            s_aT = sbl("s_aT", [128, 16 * 128], BF16)
            s_xt = sbl("s_xt", [128, D], F32)
            s_gt = sbl("s_gt", [128, D], F32)
            s_tmp = sbl("s_tmp", [128, 512], F32)
            s_out = sbl("s_out", [128, D], F32)
            S.dma("gpsimd", s_wo[:].rearrange("p (ec d) -> p ec d", d=D),
                  w_oT.rearrange("(ec p) d -> p ec d", p=128))
            for tt in range(NCHUNK):
                for ec in range(16):
                    S.op("tensor", lambda t, tt=tt, ec=ec: t.transpose(
                        pT[:], s_attn[:, tt * D + ec * 128: tt * D + (ec + 1) * 128], s_id[:]))
                    S.op("vector", lambda v, ec=ec: v.tensor_copy(s_aT[:, ec * 128:(ec + 1) * 128], pT[:]))
                S.dma("sync", s_xt[:], x_s[tt * 128:(tt + 1) * 128, :])
                S.dma("sync", s_gt[:], g_s[tt * 128:(tt + 1) * 128, :])
                for eb in range(4):
                    for ec in range(16):
                        S.op("tensor", lambda t, ec=ec, eb=eb: t.matmul(
                            pO[:], s_aT[:, ec * 128:(ec + 1) * 128],
                            s_wo[:, ec * D + eb * 512: ec * D + eb * 512 + 512],
                            start=(ec == 0), stop=(ec == 15)))
                    S.op("vector", lambda v, eb=eb: v.tensor_tensor(
                        s_tmp[:], pO[:], s_gt[:, eb * 512:(eb + 1) * 512], ALU.mult))
                    S.op("vector", lambda v, eb=eb: v.tensor_tensor(
                        s_out[:, eb * 512:(eb + 1) * 512], s_tmp[:], s_xt[:, eb * 512:(eb + 1) * 512], ALU.add))
                S.dma("sync", x1_s[tt * 128:(tt + 1) * 128, :], s_out[:])

        S.finish()
    return nc


def run_D(inputs, Cout):
    nc = build_D()
    x = np.ascontiguousarray(inputs["L0_attn_hidden_states"].reshape(SEQ * BATCH, D))
    memories = inputs["L0_attn_memories"]
    memh = np.ascontiguousarray(memories.transpose(1, 2, 0, 3).reshape(NH, HD, 512))
    mnT = np.ascontiguousarray(inputs["L0_attn_memory_norms"].transpose(1, 2, 0))
    mnm = np.ascontiguousarray(inputs["L0_attn_memory_norms"].reshape(4, D))
    gate_c = np.ascontiguousarray(inputs["L0_attn_gate"].reshape(NH, 1))
    gate_r = np.ascontiguousarray(inputs["L0_attn_gate"].reshape(1, NH))
    w_oT = np.ascontiguousarray(inputs["L0_attn_w_o"].T)
    ident = np.eye(128, dtype=BF)
    in_maps = []
    for r in range(NCORES):
        b = r // 4
        pk = np.zeros((3, NH, HD, 129), np.float32)
        for j, rp in enumerate(range(4 * b, r)):
            pk[j] = Cout["kvz_s"][rp]
        q4 = np.ascontiguousarray(Cout["qsum_s"][4 * b:4 * b + 4, 0])
        in_maps.append({
            "x_s": np.ascontiguousarray(x[r * TOK:(r + 1) * TOK]),
            "sq_s": Cout["sq_s"][r], "num_s": Cout["num_s"][r],
            "den_s": Cout["den_s"][r], "pkvz": pk, "qsum4": q4,
            "memh": memh, "mnT": mnT, "mnm": mnm,
            "gate_c": gate_c, "gate_r": gate_r, "w_oT": w_oT,
            "g_s": Cout["g_s"][r], "ident": ident,
        })
    res = run_bass_kernel_spmd(nc, in_maps, core_ids=list(range(NCORES)))
    rs = res.results
    return np.stack([rs[r]["x1_s"] for r in range(NCORES)], axis=0)



def build_E():
    nc = bass.Bass("TRN2", target_bir_lowering=False)
    x1_s = nc.dram_tensor("x1_s", [TOK, D], F32, kind="ExternalInput").ap()
    W_attn = nc.dram_tensor("W_attn", [D, 3 * D], F32, kind="ExternalInput").ap()
    b_attn = nc.dram_tensor("b_attn", [1, 3 * D], F32, kind="ExternalInput").ap()
    gwqT = nc.dram_tensor("gwqT", [D, FF1], BF16, kind="ExternalInput").ap()
    upqT = nc.dram_tensor("upqT", [D, FF1], BF16, kind="ExternalInput").ap()
    dwqe = nc.dram_tensor("dwqe", [16, 128, FF1], BF16, kind="ExternalInput").ap()
    ident = nc.dram_tensor("ident", [128, 128], BF16, kind="ExternalInput").ap()

    q_s = nc.dram_tensor("q_s", [TOK, D], BF16, kind="ExternalOutput").ap()
    k_s = nc.dram_tensor("k_s", [TOK, D], BF16, kind="ExternalOutput").ap()
    v_s = nc.dram_tensor("v_s", [TOK, D], BF16, kind="ExternalOutput").ap()
    g1T_s = nc.dram_tensor("g1T_s", [D, TOK], F32, kind="ExternalOutput").ap()

    with ExitStack() as st:
        sb = lambda n, s, t: st.enter_context(nc.sbuf_tensor(n, s, t))
        psc = lambda n, s, t: st.enter_context(nc.psum_tensor(n, s, t))
        pQ = psc("pQ", [128, 512], F32)
        pG = psc("pG", [128, 512], F32)
        pU = psc("pU", [128, 512], F32)
        pDn = psc("pDn", [128, 512], F32)
        pT = psc("pT", [128, 128], BF16)

        s_xt = sb("s_xt", [128, D], F32)
        s_sqr = sb("s_sqr", [128, D], F32)
        s_mean = sb("s_mean", [128, 1], F32)
        s_sumsq = sb("s_sumsq", [128, 1], F32)
        s_var = sb("s_var", [128, 1], F32)
        s_m2 = sb("s_m2", [128, 1], F32)
        s_rstd = sb("s_rstd", [128, 1], F32)
        s_xnt = sb("s_xnt", [128, D], BF16)
        s_xnT = sb("s_xnT", [128, 16 * TOK], BF16)
        s_id = sb("s_id", [128, 128], BF16)
        s_ones1 = sb("s_ones1", [1, 128], BF16)
        s_ba = sb("s_ba", [1, 3 * D], BF16)

        block = st.enter_context(nc.Block())
        S = Sched(nc, block, st)
        S.dma("sync", s_id[:], ident)
        S.op("gpsimd", lambda g: g.memset(s_ones1[:], 1.0))
        S.dma("gpsimd", s_ba[:], b_attn)

        for tt in range(NCHUNK):
            S.dma("sync", s_xt[:], x1_s[tt * 128:(tt + 1) * 128, :])
            S.op("vector", lambda v: v.reduce_sum(s_mean[:], s_xt[:], axis=AX.X))
            S.op("vector", lambda v: v.tensor_scalar_mul(s_mean[:], s_mean[:], 1.0 / D))
            S.op("scalar", lambda sc: sc.activation(s_sqr[:], s_xt[:], AF.Square, accum_out=s_sumsq[:]))
            S.op("vector", lambda v: v.tensor_scalar_mul(s_var[:], s_sumsq[:], 1.0 / D))
            S.op("vector", lambda v: v.scalar_tensor_tensor(s_m2[:], s_mean[:], -1.0, s_mean[:], ALU.mult, ALU.mult))
            S.op("vector", lambda v: v.tensor_tensor(s_var[:], s_var[:], s_m2[:], ALU.add))
            S.op("vector", lambda v: v.tensor_scalar_add(s_var[:], s_var[:], 1e-5))
            S.op("scalar", lambda sc: sc.activation(s_rstd[:], s_var[:], AF.Ln))
            S.op("scalar", lambda sc: sc.activation(s_rstd[:], s_rstd[:], AF.Exp, scale=-0.5))
            S.op("vector", lambda v: v.tensor_scalar(s_xnt[:], s_xt[:], s_mean[:, 0:1], s_rstd[:, 0:1], ALU.subtract, ALU.mult))
            for dc in range(16):
                S.op("tensor", lambda t, dc=dc: t.transpose(pT[:], s_xnt[:, dc * 128:(dc + 1) * 128], s_id[:]))
                S.op("vector", lambda v, dc=dc, tt=tt: v.tensor_copy(
                    s_xnT[:, dc * TOK + tt * 128: dc * TOK + (tt + 1) * 128], pT[:]))

        with ExitStack() as ph1:
            sbl = lambda n, s, t: ph1.enter_context(nc.sbuf_tensor(n, s, t))
            s_wb = sbl("s_wb", [128, 16 * 512], BF16)
            s_qkv = sbl("s_qkv", [128, 512], BF16)
            outs = [q_s, q_s, q_s, q_s, k_s, k_s, k_s, k_s, v_s, v_s, v_s, v_s]
            for ob in range(12):
                S.dma("gpsimd",
                      s_wb[:].rearrange("p (dc e) -> p dc e", e=512),
                      W_attn[:, ob * 512:(ob + 1) * 512].rearrange("(dc p) e -> p dc e", p=128))
                for tt in range(NCHUNK):
                    for dc in range(16):
                        S.op("tensor", lambda t, dc=dc, tt=tt: t.matmul(
                            pQ[:], s_xnT[:, dc * TOK + tt * 128: dc * TOK + (tt + 1) * 128],
                            s_wb[:, dc * 512:(dc + 1) * 512], start=(dc == 0), stop=False))
                    S.op("tensor", lambda t, ob=ob: t.matmul(
                        pQ[:], s_ones1[:], s_ba[:, ob * 512:(ob + 1) * 512], start=False, stop=True))
                    S.op("vector", lambda v: v.tensor_copy(s_qkv[:], pQ[:]))
                    S.dma("sync", outs[ob][tt * 128:(tt + 1) * 128, (ob % 4) * 512:(ob % 4) * 512 + 512], s_qkv[:])

        with ExitStack() as ph2:
            sbl = lambda n, s, t: ph2.enter_context(nc.sbuf_tensor(n, s, t))
            s_gw = sbl("s_gw", [128, 16 * 512], BF16)
            s_uw = sbl("s_uw", [128, 16 * 512], BF16)
            s_hT = sbl("s_hT", [128, 64 * 512], BF16)
            s_dw = sbl("s_dw", [128, FF1], BF16)
            s_sg = sbl("s_sg", [128, 512], F32)
            s_g1 = sbl("s_g1", [128, 512], F32)
            for half in range(2):
                hoff = half * 512
                for hb in range(16):
                    for (wsrc, wdst) in ((gwqT, s_gw), (upqT, s_uw)):
                        S.dma("sync",
                              wdst[:].rearrange("p (dc e) -> p dc e", e=512),
                              wsrc[:, hb * 512:(hb + 1) * 512].rearrange("(dc p) e -> p dc e", p=128))
                    for hc in range(4):
                        for (wdst, pdst) in ((s_gw, pG), (s_uw, pU)):
                            for dc in range(16):
                                S.op("tensor", lambda t, dc=dc, wdst=wdst, pdst=pdst, hc=hc: t.matmul(
                                    pdst[:], wdst[:, dc * 512 + hc * 128: dc * 512 + (hc + 1) * 128],
                                    s_xnT[:, dc * TOK + hoff: dc * TOK + hoff + 512],
                                    start=(dc == 0), stop=(dc == 15)))
                        S.op("scalar", lambda sc: sc.activation(s_sg[:], pG[:], AF.Silu))
                        S.op("vector", lambda v, hb=hb, hc=hc: v.tensor_tensor(
                            s_hT[:, (hb * 4 + hc) * 512:(hb * 4 + hc + 1) * 512],
                            s_sg[:], pU[:], ALU.mult))
                for ec in range(16):
                    S.dma("sync", s_dw[:], dwqe[ec])
                    for ic in range(64):
                        S.op("tensor", lambda t, ic=ic: t.matmul(
                            pDn[:], s_dw[:, ic * 128:(ic + 1) * 128],
                            s_hT[:, ic * 512:(ic + 1) * 512],
                            start=(ic == 0), stop=(ic == 63)))
                    S.op("scalar", lambda sc: sc.activation(s_g1[:], pDn[:], AF.Silu))
                    S.dma("sync", g1T_s[ec * 128:(ec + 1) * 128, hoff:hoff + 512], s_g1[:])

        S.finish()
    return nc


def run_E(inputs, x1, gwq, uwq, dwq):
    nc = build_E()
    b_attn = np.ascontiguousarray(inputs["L1_attn_b_attn"].reshape(1, 3 * D))
    gwqT = np.ascontiguousarray(gwq.T)
    upqT = np.ascontiguousarray(uwq.T)
    dwqT = dwq.T
    dwqe = np.ascontiguousarray(
        dwqT.reshape(64, 128, 16, 128).transpose(2, 1, 0, 3).reshape(16, 128, FF1))
    ident = np.eye(128, dtype=BF)
    in_maps = []
    for r in range(NCORES):
        in_maps.append({
            "x1_s": np.ascontiguousarray(x1[r]),
            "W_attn": inputs["L1_attn_W_attn"], "b_attn": b_attn,
            "gwqT": gwqT, "upqT": upqT, "dwqe": dwqe, "ident": ident,
        })
    res = run_bass_kernel_spmd(nc, in_maps, core_ids=list(range(NCORES)))
    rs = res.results
    return {k: np.stack([rs[r][k] for r in range(NCORES)], axis=0)
            for k in ("q_s", "k_s", "v_s", "g1T_s")}



def build_F():
    nc = bass.Bass("TRN2", target_bir_lowering=False)
    Qf = nc.dram_tensor("Qf", [4, HD, SEQ], BF16, kind="ExternalInput").ap()
    Kf = nc.dram_tensor("Kf", [4, HD, SEQ], BF16, kind="ExternalInput").ap()
    Vt = nc.dram_tensor("Vt", [4, SEQ, HD], BF16, kind="ExternalInput").ap()
    dm = nc.dram_tensor("dm", [4, 128, 512], BF16, kind="ExternalInput").ap()
    identf = nc.dram_tensor("identf", [128, 128], F32, kind="ExternalInput").ap()
    y_s = nc.dram_tensor("y_s", [4, SEQ, HD], BF16, kind="ExternalOutput").ap()

    SC = HD ** -0.5

    with ExitStack() as st:
        sb = lambda n, s, t: st.enter_context(nc.sbuf_tensor(n, s, t))
        psc = lambda n, s, t: st.enter_context(nc.psum_tensor(n, s, t))
        pS1 = psc("pS1", [128, 512], F32)
        pST = psc("pST", [128, 512], F32)
        pY = [psc(f"pY{i}", [128, 512], F32) for i in range(4)]
        pTm = psc("pTm", [1, 128], F32)

        s_q = sb("s_q", [128, SEQ], BF16)
        s_k = sb("s_k", [128, SEQ], BF16)
        s_v = sb("s_v", [128, 32 * 132], BF16)
        s_dm = sb("s_dm", [128, 4 * 512], BF16)
        s_idf = sb("s_idf", [128, 128], F32)
        s_ones1 = sb("s_ones1", [1, 128], BF16)
        s_nm = sb("s_nm", [128, 4], F32)
        s_mt = sb("s_mt", [128, 1], F32)
        s_negm_row = sb("s_negm_row", [1, 512], F32)
        s_negm_bf = sb("s_negm_bf", [1, 512], BF16)
        s_pt = sb("s_pt", [128, 512], BF16)
        s_l = sb("s_l", [128, 1], F32)
        s_yst = sb("s_yst", [128, 512], BF16)

        block = st.enter_context(nc.Block())
        S = Sched(nc, block, st)
        S.dma("sync", s_idf[:], identf)
        S.dma("sync", s_dm[:].rearrange("p (r w) -> p r w", w=512), dm.rearrange("r p w -> p r w"))
        S.op("gpsimd", lambda g: g.memset(s_ones1[:], 1.0))

        for pr in range(4):
            S.dma("sync", s_q[:], Qf[pr])
            S.dma("sync", s_k[:], Kf[pr])
            S.op("gpsimd", lambda g: g.memset(s_v[:], 0.0))
            S.op("gpsimd", lambda g: g.memset(
                s_v[:].rearrange("p (c w) -> p c w", w=132)[:, :, 128:129], 1.0))
            S.dma("sync",
                  s_v[:].rearrange("p (c w) -> p c w", w=132)[:, :, 0:128],
                  Vt[pr].rearrange("(c p) e -> p c e", p=128))
            for ig in range(8):
                S.op("gpsimd", lambda g: g.memset(s_nm[:], 0.0))
                for j in range(4):
                    qt = ig * 4 + j
                    for bb in range(qt // 4):
                        S.op("tensor", lambda t, qt=qt, bb=bb: t.matmul(
                            pS1[:], s_q[:, qt * 128:(qt + 1) * 128],
                            s_k[:, bb * 512:(bb + 1) * 512], start=True, stop=True))
                        S.op("vector", lambda v: v.reduce_max(s_mt[:], pS1[:], axis=AX.X, negate=True))
                        S.op("vector", lambda v, j=j: v.tensor_tensor(
                            s_nm[:, j:j + 1], s_nm[:, j:j + 1], s_mt[:], ALU.min))
                for j in range(4):
                    S.op("tensor", lambda t, j=j: t.transpose(pTm[:], s_nm[:, j:j + 1], s_idf[:]))
                    S.op("vector", lambda v, j=j: v.tensor_copy(
                        s_negm_row[0:1, j * 128:(j + 1) * 128], pTm[:]))
                S.op("vector", lambda v: v.tensor_copy(s_negm_bf[:], s_negm_row[:]))
                for c in range(4 * ig + 4):
                    S.op("tensor", lambda t, c=c, ig=ig: t.matmul(
                        pST[:], s_k[:, c * 128:(c + 1) * 128],
                        s_q[:, ig * 512:(ig + 1) * 512], start=True, stop=False))
                    S.op("tensor", lambda t: t.matmul(
                        pST[:], s_ones1[:], s_negm_bf[:], start=False, stop=True))
                    S.op("scalar", lambda sc: sc.activation(s_pt[:], pST[:], AF.Exp, scale=SC))
                    if c >= 4 * ig:
                        rel = c - 4 * ig
                        S.op("vector", lambda v, rel=rel: v.tensor_tensor(
                            s_pt[:], s_pt[:], s_dm[:, rel * 512:(rel + 1) * 512], ALU.mult))
                    for sub in range(4):
                        if 4 * ig + sub >= c:
                            S.op("tensor", lambda t, sub=sub, c=c, ig=ig: t.matmul(
                                pY[sub][0:128, 0:129], s_pt[:, sub * 128:(sub + 1) * 128],
                                s_v[:, c * 132:c * 132 + 129],
                                start=(c == 0), stop=(c == 4 * ig + sub)))
                for sub in range(4):
                    S.op("vector", lambda v, sub=sub: v.reciprocal(s_l[:], pY[sub][:, 128:129]))
                    S.op("vector", lambda v, sub=sub: v.tensor_scalar(
                        s_yst[:, sub * 128:(sub + 1) * 128], pY[sub][:, 0:128],
                        s_l[:, 0:1], None, ALU.mult))
                S.dma("sync",
                      y_s[pr, ig * 512:(ig + 1) * 512, :].rearrange("(s q) e -> q s e", q=128),
                      s_yst[:].rearrange("p (s e) -> p s e", e=128))
        S.finish()
    return nc


def run_F(inputs, Eout):
    nc = build_F()
    q = np.ascontiguousarray(Eout["q_s"].reshape(BATCH, SEQ, NH, HD))
    k = np.ascontiguousarray(Eout["k_s"].reshape(BATCH, SEQ, NH, HD))
    v = np.ascontiguousarray(Eout["v_s"].reshape(BATCH, SEQ, NH, HD))
    dmm = np.zeros((4, 128, 512), BF)
    for r in range(4):
        for tk in range(128):
            dmm[r, tk, r * 128 + tk:] = 1.0
    identf = np.eye(128, dtype=np.float32)
    in_maps = []
    for c in range(NCORES):
        qf = np.zeros((4, HD, SEQ), BF)
        kf = np.zeros((4, HD, SEQ), BF)
        vt = np.zeros((4, SEQ, HD), BF)
        for i in range(4):
            pair = 4 * c + i
            b, h = pair // NH, pair % NH
            qf[i] = q[b, :, h, :].T
            kf[i] = k[b, :, h, :].T
            vt[i] = v[b, :, h, :]
        in_maps.append({"Qf": qf, "Kf": kf, "Vt": vt, "dm": dmm, "identf": identf})
    res = run_bass_kernel_spmd(nc, in_maps, core_ids=list(range(NCORES)))
    rs = res.results
    return np.stack([rs[r]["y_s"] for r in range(NCORES)], axis=0)



def build_G():
    nc = bass.Bass("TRN2", target_bir_lowering=False)
    x1_s = nc.dram_tensor("x1_s", [TOK, D], F32, kind="ExternalInput").ap()
    yT_s = nc.dram_tensor("yT_s", [D, TOK], BF16, kind="ExternalInput").ap()
    W_proj = nc.dram_tensor("W_proj", [D, D], F32, kind="ExternalInput").ap()
    b_proj = nc.dram_tensor("b_proj", [1, D], F32, kind="ExternalInput").ap()
    g1_s = nc.dram_tensor("g1_s", [TOK, D], F32, kind="ExternalInput").ap()
    out_s = nc.dram_tensor("out_s", [TOK, D], F32, kind="ExternalOutput").ap()

    with ExitStack() as st:
        sb = lambda n, s, t: st.enter_context(nc.sbuf_tensor(n, s, t))
        pG = st.enter_context(nc.psum_tensor("pG", [128, 512], F32))
        s_yT = sb("s_yT", [128, 16 * TOK], BF16)
        s_wp = sb("s_wp", [128, 16 * D], BF16)
        s_bp = sb("s_bp", [1, D], BF16)
        s_ones1 = sb("s_ones1", [1, 128], BF16)
        s_xt = sb("s_xt", [128, D], F32)
        s_gt = sb("s_gt", [128, D], F32)
        s_tmp = sb("s_tmp", [128, 512], F32)
        s_out = sb("s_out", [128, D], F32)
        block = st.enter_context(nc.Block())
        S = Sched(nc, block, st)
        S.op("gpsimd", lambda g: g.memset(s_ones1[:], 1.0))
        S.dma("gpsimd", s_bp[:], b_proj)
        S.dma("sync", s_yT[:].rearrange("p (cc t) -> p cc t", t=TOK),
              yT_s.rearrange("(cc p) t -> p cc t", p=128))
        S.dma("gpsimd", s_wp[:].rearrange("p (cc d) -> p cc d", d=D),
              W_proj.rearrange("(cc p) d -> p cc d", p=128))
        for tt in range(NCHUNK):
            S.dma("sync", s_xt[:], x1_s[tt * 128:(tt + 1) * 128, :])
            S.dma("sync", s_gt[:], g1_s[tt * 128:(tt + 1) * 128, :])
            for eb in range(4):
                for cc in range(16):
                    S.op("tensor", lambda t, cc=cc, eb=eb, tt=tt: t.matmul(
                        pG[:], s_yT[:, cc * TOK + tt * 128: cc * TOK + (tt + 1) * 128],
                        s_wp[:, cc * D + eb * 512: cc * D + eb * 512 + 512],
                        start=(cc == 0), stop=False))
                S.op("tensor", lambda t, eb=eb: t.matmul(
                    pG[:], s_ones1[:], s_bp[:, eb * 512:(eb + 1) * 512], start=False, stop=True))
                S.op("vector", lambda v, eb=eb: v.tensor_tensor(
                    s_tmp[:], pG[:], s_gt[:, eb * 512:(eb + 1) * 512], ALU.mult))
                S.op("vector", lambda v, eb=eb: v.tensor_tensor(
                    s_out[:, eb * 512:(eb + 1) * 512], s_tmp[:], s_xt[:, eb * 512:(eb + 1) * 512], ALU.add))
            S.dma("sync", out_s[tt * 128:(tt + 1) * 128, :], s_out[:])
        S.finish()
    return nc


def run_G(inputs, x1, y, g1T):
    nc = build_G()
    yy = y.reshape(BATCH, NH, SEQ, HD).transpose(0, 2, 1, 3).reshape(SEQ * BATCH, D)
    g1 = np.concatenate([g1T[r].T for r in range(NCORES)], axis=0)
    b_proj = np.ascontiguousarray(inputs["L1_attn_b_proj"].reshape(1, D))
    in_maps = []
    for r in range(NCORES):
        in_maps.append({
            "x1_s": np.ascontiguousarray(x1[r]),
            "yT_s": np.ascontiguousarray(yy[r * TOK:(r + 1) * TOK].T),
            "W_proj": inputs["L1_attn_W_proj"], "b_proj": b_proj,
            "g1_s": np.ascontiguousarray(g1[r * TOK:(r + 1) * TOK]),
        })
    res = run_bass_kernel_spmd(nc, in_maps, core_ids=list(range(NCORES)))
    rs = res.results
    return np.concatenate([rs[r]["out_s"] for r in range(NCORES)], axis=0)



def build_mk(parts="L0L1", dbg=False):
    nc = bass.Bass("TRN2", target_bir_lowering=False)
    R1 = FF0 // 8
    R3 = D // 8
    has0 = "L0" in parts
    has1 = "L1" in parts
    IK = "ExternalOutput" if dbg else "Internal"

    if has0:
        x_s = nc.dram_tensor("x_s", [TOK, D], F32, kind="ExternalInput").ap()
        w1s = nc.dram_tensor("w1s", [R1, D], F32, kind="ExternalInput").ap()
        w2s = nc.dram_tensor("w2s", [R1, D], F32, kind="ExternalInput").ap()
        w3s = nc.dram_tensor("w3s", [R3, FF0], F32, kind="ExternalInput").ap()
        wq_h = nc.dram_tensor("wq_h", [4, 128, 16, 512], F32, kind="ExternalInput").ap()
        wk_h = nc.dram_tensor("wk_h", [4, 128, 16, 512], F32, kind="ExternalInput").ap()
        wv_h = nc.dram_tensor("wv_h", [4, 128, 16, 512], F32, kind="ExternalInput").ap()
        wo_h = nc.dram_tensor("wo_h", [128, 16, D], F32, kind="ExternalInput").ap()
        memh2 = nc.dram_tensor("memh2", [HD, NH, 512], F32, kind="ExternalInput").ap()
        mnT2 = nc.dram_tensor("mnT2", [HD, NH, 4], F32, kind="ExternalInput").ap()
        mnm = nc.dram_tensor("mnm", [4, D], F32, kind="ExternalInput").ap()
        gate_c = nc.dram_tensor("gate_c", [NH, 1], F32, kind="ExternalInput").ap()
        gate_r = nc.dram_tensor("gate_r", [1, NH], F32, kind="ExternalInput").ap()
        triu = nc.dram_tensor("triu", [128, 128], BF16, kind="ExternalInput").ap()
        pmask = nc.dram_tensor("pmask", [1, 8], F32, kind="ExternalInput").ap()
        bmask = nc.dram_tensor("bmask", [8, 1], F32, kind="ExternalInput").ap()
        x1_s = nc.dram_tensor("x1_s", [TOK, D], F32, kind="ExternalOutput").ap()
    else:
        x1_s = nc.dram_tensor("x1_s", [TOK, D], F32, kind="ExternalInput").ap()
    if has1:
        wa_h = nc.dram_tensor("wa_h", [12, 128, 16, 512], F32, kind="ExternalInput").ap()
        b_attn = nc.dram_tensor("b_attn", [1, 3 * D], F32, kind="ExternalInput").ap()
        gw_h = nc.dram_tensor("gw_h", [32, 128, 16, 256], I32, kind="ExternalInput").ap()
        up_h = nc.dram_tensor("up_h", [32, 128, 16, 256], I32, kind="ExternalInput").ap()
        dwe32 = nc.dram_tensor("dwe32", [16, 128, FF1], I32, kind="ExternalInput").ap()
        gsc2 = nc.dram_tensor("gsc2", [64, 128], F32, kind="ExternalInput").ap()
        usc2 = nc.dram_tensor("usc2", [64, 128], F32, kind="ExternalInput").ap()
        dsc2 = nc.dram_tensor("dsc2", [16, 128], F32, kind="ExternalInput").ap()
        wp_h = nc.dram_tensor("wp_h", [128, 16, D], F32, kind="ExternalInput").ap()
        b_proj = nc.dram_tensor("b_proj", [1, D], F32, kind="ExternalInput").ap()
        dm = nc.dram_tensor("dm", [4, 128, 512], BF16, kind="ExternalInput").ap()
        identf = nc.dram_tensor("identf", [128, 128], F32, kind="ExternalInput").ap()
        out_s = nc.dram_tensor("out_s", [TOK, D], F32, kind="ExternalOutput").ap()
    ident = nc.dram_tensor("ident", [128, 128], BF16, kind="ExternalInput").ap()

    if has0:
        ag_g_in = nc.dram_tensor("ag_g_in", [1, 4], F32, kind="Internal").ap()
        ag_g_out = nc.dram_tensor("ag_g_out", [8, 4], F32, kind="Internal").ap()
        ag_w12_in = nc.dram_tensor("ag_w12_in", [2 * D, R1], BF16, kind="Internal").ap()
        ag_w12_out = nc.dram_tensor("ag_w12_out", [8 * 2 * D, R1], BF16, kind="Internal").ap()
        ag_w3_in = nc.dram_tensor("ag_w3_in", [FF0, R3], BF16, kind="Internal").ap()
        ag_w3_out = nc.dram_tensor("ag_w3_out", [8 * FF0, R3], BF16, kind="Internal").ap()
        ag_kvz_in = nc.dram_tensor("ag_kvz_in", [NH * HD, 129], F32, kind="Internal").ap()
        ag_kvz_out = nc.dram_tensor("ag_kvz_out", [8 * NH * HD, 129], F32, kind="Internal").ap()
        ag_qs_in = nc.dram_tensor("ag_qs_in", [1, D], F32, kind="Internal").ap()
        ag_qs_out = nc.dram_tensor("ag_qs_out", [8, D], F32, kind="Internal").ap()
        sq_d = nc.dram_tensor("sq_d", [TOK, D], BF16, kind=IK).ap()
        num_d = nc.dram_tensor("num_d", [TOK, D], F32, kind=IK).ap()
        den_d = nc.dram_tensor("den_d", [TOK, NH], F32, kind=IK).ap()
        g_d = nc.dram_tensor("g_d", [TOK, D], F32, kind=IK).ap()
        scr_qbar = nc.dram_tensor("scr_qbar", [1, D], F32, kind="Internal").ap()
        scrwg = nc.dram_tensor("scrwg", [NH, 4], F32, kind="Internal").ap()
    if has1:
        qkvc = nc.dram_tensor("qkvc", [8 * 3 * 2 * TOK, HD], BF16, kind="Internal").ap()
        qkvg = nc.dram_tensor("qkvg", [8 * 3 * 2 * TOK, HD], BF16, kind="Internal").ap()
        yc = nc.dram_tensor("yc", [8 * 2 * TOK, HD], BF16, kind="Internal").ap()
        yg = nc.dram_tensor("yg", [8 * 2 * TOK, HD], BF16, kind="Internal").ap()
        g1_d = nc.dram_tensor("g1_d", [TOK, D], F32, kind=IK).ap()
        scr4m = nc.dram_tensor("scr4m", [4, 1], F32, kind="Internal").ap()
    else:
        scr4m = nc.dram_tensor("scr4m", [4, 1], F32, kind="Internal").ap()

    with ExitStack() as st:
        sb = lambda n, s, t: st.enter_context(nc.sbuf_tensor(n, s, t))
        s_id = sb("s_id", [128, 128], BF16)
        s_ones1b = sb("s_ones1b", [1, 128], BF16)
        s_ones1f = sb("s_ones1f", [1, 128], F32)
        s_onescf = sb("s_onescf", [128, 1], F32)
        s_onescb = sb("s_onescb", [128, 1], BF16)
        s_xnT = sb("s_xnT", [128, 16 * TOK], BF16)
        s_xt = sb("s_xt", [128, D], F32)
        s_sqr = sb("s_sqr", [128, D], F32)
        s_mean = sb("s_mean", [128, 1], F32)
        s_sumsq = sb("s_sumsq", [128, 1], F32)
        s_var = sb("s_var", [128, 1], F32)
        s_m2 = sb("s_m2", [128, 1], F32)
        s_rstd = sb("s_rstd", [128, 1], F32)
        s_xnt = sb("s_xnt", [128, D], BF16)

        block = st.enter_context(nc.Block())
        S = Sched(nc, block, st)
        S.dma("sync", s_id[:], ident)
        S.op("gpsimd", lambda g: g.memset(s_ones1b[:], 1.0))
        S.op("gpsimd", lambda g: g.memset(s_ones1f[:], 1.0))
        S.op("gpsimd", lambda g: g.memset(s_onescf[:], 1.0))
        S.op("gpsimd", lambda g: g.memset(s_onescb[:], 1.0))

        def emit_ln(src, pT):
            for tt in range(NCHUNK):
                S.dma("sync", s_xt[:], src[tt * 128:(tt + 1) * 128, :])
                S.op("vector", lambda v: v.reduce_sum(s_mean[:], s_xt[:], axis=AX.X))
                S.op("vector", lambda v: v.tensor_scalar_mul(s_mean[:], s_mean[:], 1.0 / D))
                S.op("scalar", lambda sc: sc.activation(s_sqr[:], s_xt[:], AF.Square, accum_out=s_sumsq[:]))
                S.op("vector", lambda v: v.tensor_scalar_mul(s_var[:], s_sumsq[:], 1.0 / D))
                S.op("vector", lambda v: v.scalar_tensor_tensor(s_m2[:], s_mean[:], -1.0, s_mean[:], ALU.mult, ALU.mult))
                S.op("vector", lambda v: v.tensor_tensor(s_var[:], s_var[:], s_m2[:], ALU.add))
                S.op("vector", lambda v: v.tensor_scalar_add(s_var[:], s_var[:], 1e-5))
                S.op("scalar", lambda sc: sc.activation(s_rstd[:], s_var[:], AF.Ln))
                S.op("scalar", lambda sc: sc.activation(s_rstd[:], s_rstd[:], AF.Exp, scale=-0.5))
                S.op("vector", lambda v: v.tensor_scalar(s_xnt[:], s_xt[:], s_mean[:, 0:1], s_rstd[:, 0:1], ALU.subtract, ALU.mult))
                for dc in range(16):
                    S.op("tensor", lambda t, dc=dc: t.transpose(pT[:], s_xnt[:, dc * 128:(dc + 1) * 128], s_id[:]))
                    S.op("vector", lambda v, dc=dc, tt=tt: v.tensor_copy(
                        s_xnT[:, dc * TOK + tt * 128: dc * TOK + (tt + 1) * 128], pT[:]))

        if has0:
            with ExitStack() as ph:
                S.fence()
                sbl = lambda n, s, t: ph.enter_context(nc.sbuf_tensor(n, s, t))
                psc = lambda n, s, t: ph.enter_context(nc.psum_tensor(n, s, t))
                s_w = sbl("s_w", [128, 2 * FF0], F32)
                s_red = sbl("s_red", [128, 1], F32)
                s_gs = sbl("s_gs", [1, 4], F32)
                s_gp = sbl("s_gp", [8, 4], F32)
                s_ones8 = sbl("s_ones8", [8, 1], F32)
                s_thr_row = sbl("s_thr_row", [1, 4], F32)
                s_thr = sbl("s_thr", [128, 4], F32)
                s_nthr = sbl("s_nthr", [128, 4], F32)
                s_a = sbl("s_a", [128, FF0], F32)
                s_b = sbl("s_b", [128, FF0], F32)
                s_q = sbl("s_q", [128, FF0], BF16)
                s_wc = sbl("s_wc", [128, 16 * R1], BF16)
                p_gs = psc("p_gs", [1, 4], F32)
                p_bc = psc("p_bc", [128, 4], F32)
                pT = psc("pTw", [128, 128], BF16)

                S.op("gpsimd", lambda g: g.memset(s_ones8[:], 1.0))
                S.op("gpsimd", lambda g: g.memset(s_gs[:], 0.0))
                nelem = [FF0 * D, FF0 * D, D * FF0]
                tiles = []
                for k, (src, rows, cols) in enumerate(((w1s, R1, D), (w2s, R1, D), (w3s, R3, FF0))):
                    for t in range(_ceil(rows, 128)):
                        pc = min(128, rows - t * 128)
                        tiles.append((k, src, t, pc, cols))
                wq_war = [0, 0]
                wq_tk = {}

                def load_tile(i):
                    k, src, t, pc, cols = tiles[i]
                    par = (i % 2) * FF0
                    return [S.dma("sync", s_w[0:pc, par:par + cols], src[t * 128:t * 128 + pc, :],
                                  war=wq_war[i % 2], detach=True)]

                wq_tk[0] = load_tile(0)
                for i, (k, src, t, pc, cols) in enumerate(tiles):
                    par = (i % 2) * FF0
                    if i + 1 < len(tiles):
                        wq_tk[i + 1] = load_tile(i + 1)
                    S.op("vector", lambda v, pc=pc, cols=cols, par=par: v.tensor_reduce(
                        s_red[0:pc, :], s_w[0:pc, par:par + cols], axis=AX.X, op=ALU.add,
                        apply_absolute_value=True), dneed=wq_tk[i])
                    wq_war[i % 2] = S.c
                    S.op("tensor", lambda t_, pc=pc: t_.matmul(
                        p_gs[0:1, 0:1], s_red[0:pc, :], s_onescf[0:pc, :], start=True, stop=True))
                    S.op("vector", lambda v, k=k: v.tensor_tensor(
                        s_gs[0:1, k:k + 1], s_gs[0:1, k:k + 1], p_gs[0:1, 0:1], ALU.add))
                S.dma("sync", ag_g_in, s_gs[:])
                S.cc("AllGather", ALU.bypass, [ag_g_in], [ag_g_out])
                S.dma("sync", s_gp[:], ag_g_out)
                S.op("tensor", lambda t: t.matmul(p_gs[:], s_ones8[:], s_gp[:], start=True, stop=True))
                S.op("vector", lambda v: v.tensor_copy(s_thr_row[:], p_gs[:]))
                for k in range(3):
                    S.op("vector", lambda v, k=k: v.tensor_scalar(
                        s_thr_row[0:1, k:k + 1], s_thr_row[0:1, k:k + 1],
                        0.5 / nelem[k], 0.5e-8, ALU.mult, ALU.add))
                S.op("tensor", lambda t: t.matmul(p_bc[:], s_ones1f[:], s_thr_row[:], start=True, stop=True))
                S.op("vector", lambda v: v.tensor_copy(s_thr[:], p_bc[:]))
                S.op("vector", lambda v: v.tensor_scalar_mul(s_nthr[:], s_thr[:], -1.0))

                qt_state = {"i": 0}

                def quant_t(src, rows, cols, k, dst, dstw):
                    for t in range(_ceil(rows, 128)):
                        pc = min(128, rows - t * 128)
                        i = qt_state["i"]
                        qt_state["i"] += 1
                        par = (i % 2) * FF0
                        tk = S.dma("sync", s_w[0:pc, par:par + cols], src[t * 128:t * 128 + pc, :],
                                   war=wq_war[i % 2], detach=True)
                        S.op("vector", lambda v, pc=pc, cols=cols, k=k, par=par: v.tensor_scalar(
                            s_a[0:pc, 0:cols], s_w[0:pc, par:par + cols], s_thr[0:pc, k:k + 1], None, ALU.is_ge),
                            dneed=[tk])
                        S.op("vector", lambda v, pc=pc, cols=cols, k=k, par=par: v.tensor_scalar(
                            s_b[0:pc, 0:cols], s_w[0:pc, par:par + cols], s_nthr[0:pc, k:k + 1], None, ALU.is_le))
                        wq_war[i % 2] = S.c
                        S.op("vector", lambda v, pc=pc, cols=cols: v.tensor_tensor(
                            s_q[0:pc, 0:cols], s_a[0:pc, 0:cols], s_b[0:pc, 0:cols], ALU.subtract))
                        for cc in range(cols // 128):
                            S.op("tensor", lambda t_, cc=cc, pc=pc: t_.transpose(
                                pT[0:128, 0:pc], s_q[0:pc, cc * 128:(cc + 1) * 128], s_id[0:pc, 0:pc]))
                            S.op("vector", lambda v, cc=cc, t=t, pc=pc: v.tensor_copy(
                                s_wc[:, cc * rows + t * 128: cc * rows + t * 128 + pc], pT[0:128, 0:pc]))
                    S.dma("sync",
                          dst.rearrange("(cc p) r -> p cc r", p=128),
                          s_wc[:, 0:(cols // 128) * rows].rearrange("p (cc r) -> p cc r", r=rows))

                quant_t(w1s, R1, D, 0, ag_w12_in[0:D, :], R1)
                quant_t(w2s, R1, D, 1, ag_w12_in[D:2 * D, :], R1)
                quant_t(w3s, R3, FF0, 2, ag_w3_in, R3)
                S.cc("AllGather", ALU.bypass, [ag_w12_in], [ag_w12_out])
                S.cc("AllGather", ALU.bypass, [ag_w3_in], [ag_w3_out])

            with ExitStack() as ph:
                S.fence()
                sbl = lambda n, s, t: ph.enter_context(nc.sbuf_tensor(n, s, t))
                psc = lambda n, s, t: ph.enter_context(nc.psum_tensor(n, s, t))
                pT = psc("pT0", [128, 128], BF16)
                pS = psc("pS0", [1, 512], F32)
                pA = psc("pA0", [128, 512], F32)
                pB = psc("pB0", [128, 512], F32)
                pC = psc("pC0", [128, 512], F32)
                pD = psc("pD0", [128, 129], F32)
                pE = psc("pE0", [128, 129], F32)
                pF = psc("pF0", [128, 128], F32)
                s_triu = sbl("s_triu", [128, 128], BF16)
                s_wq = sbl("s_wq", [128, 2 * 16 * 512], BF16)
                s_wk = sbl("s_wk", [128, 2 * 16 * 512], BF16)
                s_wv = sbl("s_wv", [128, 2 * 16 * 512], BF16)
                s_sqb = sbl("s_sqb", [128, NCHUNK * 512], BF16)
                s_skb = sbl("s_skb", [128, NCHUNK * 512], BF16)
                s_vb = sbl("s_vb", [128, 4 * NCHUNK * 132], BF16)
                s_t1 = sbl("s_t1", [128, 512], F32)
                s_t2 = sbl("s_t2", [128, 512], F32)
                s_qf = sbl("s_qf", [128, 512], F32)
                s_state = sbl("s_state", [128, 4 * 132], F32)
                s_spb = sbl("s_spb", [128, 132], BF16)
                s_qpT = sbl("s_qpT", [128, 128], BF16)
                s_kpT = sbl("s_kpT", [128, 128], BF16)
                s_ms = sbl("s_ms", [128, 128], BF16)
                s_numst = sbl("s_numst", [128, 512], F32)
                s_denst = sbl("s_denst", [128, NCHUNK * NH], F32)
                s_qsumst = sbl("s_qsumst", [1, D], F32)

                S.dma("sync", s_triu[:], triu)
                emit_ln(x_s, pT)
                BW = 16 * 512

                def ld_wblk(blk, war, twar=0):
                    par = (blk % 2) * BW
                    tks = []
                    for (wsrc, wdst) in ((wq_h, s_wq), (wk_h, s_wk), (wv_h, s_wv)):
                        tks.append(S.dma(
                            "gpsimd",
                            wdst[:, par:par + BW].rearrange("p (dc e) -> p dc e", e=512),
                            wsrc[blk],
                            war=war, twar=twar, detach=True))
                    return tks

                wt_war = [0, 0]
                wt_tw = [0, 0]
                wtk = [None] * 4
                wtk[0] = ld_wblk(0, 0)
                xn_ready = S.c
                q_rd, k_rd, v_rd, qs_rd = [xn_ready], [xn_ready], [xn_ready], [0]
                for blk in range(4):
                    par = (blk % 2) * BW
                    if blk + 1 < 4:
                        wtk[blk + 1] = ld_wblk(blk + 1, wt_war[(blk + 1) % 2], wt_tw[(blk + 1) % 2])
                    S.op("gpsimd", lambda g: g.memset(s_vb[:], 0.0), tneed=S.t)
                    S.op("gpsimd", lambda g: g.memset(
                        s_vb[:].rearrange("p (t c) -> p t c", c=132)[:, :, 128:129], 1.0))
                    tqs = 0
                    for tt in range(NCHUNK):
                        tq = S.top_group([
                            (lambda t, dc=dc, tt=tt, par=par: t.matmul(
                                pA[:], s_xnT[:, dc * TOK + tt * 128: dc * TOK + (tt + 1) * 128],
                                s_wq[:, par + dc * 512: par + (dc + 1) * 512],
                                start=(dc == 0), stop=(dc == 15)))
                            for dc in range(16)], cneed=q_rd[0], dneed=wtk[blk])
                        tk2 = S.top_group([
                            (lambda t, dc=dc, tt=tt, par=par: t.matmul(
                                pB[:], s_xnT[:, dc * TOK + tt * 128: dc * TOK + (tt + 1) * 128],
                                s_wk[:, par + dc * 512: par + (dc + 1) * 512],
                                start=(dc == 0), stop=(dc == 15)))
                            for dc in range(16)], cneed=k_rd[0], dneed=wtk[blk])
                        tv = S.top_group([
                            (lambda t, dc=dc, tt=tt, par=par: t.matmul(
                                pC[:], s_xnT[:, dc * TOK + tt * 128: dc * TOK + (tt + 1) * 128],
                                s_wv[:, par + dc * 512: par + (dc + 1) * 512],
                                start=(dc == 0), stop=(dc == 15)))
                            for dc in range(16)], cneed=v_rd[0], dneed=wtk[blk])
                        cp = S.op("scalar", lambda sc: sc.copy(s_qf[:], pA[:]), tneed=tq)
                        tqs = S.top_group([
                            (lambda t, tt=tt: t.matmul(
                                pS[:], s_onescf[:], s_qf[:], start=(tt == 0), stop=(tt == NCHUNK - 1)))],
                            cneed=max(cp, qs_rd[0]))
                        for (psrc, dstbuf, tkk) in ((pA, s_sqb, tq), (pB, s_skb, tk2)):
                            S.op("vector", lambda v, psrc=psrc: v.tensor_scalar_min(s_t1[:], psrc[:], 0.0),
                                 tneed=tkk)
                            S.op("scalar", lambda sc: sc.activation(s_t2[:], s_t1[:], AF.Exp))
                            rdt = S.op("vector", lambda v, psrc=psrc: v.tensor_scalar_max(s_t1[:], psrc[:], 0.0))
                            S.op("vector", lambda v, dstbuf=dstbuf, tt=tt: v.tensor_tensor(
                                dstbuf[:, tt * 512:(tt + 1) * 512], s_t2[:], s_t1[:], ALU.add))
                            if psrc is pA:
                                q_rd[0] = rdt
                            else:
                                k_rd[0] = rdt
                        for h in range(4):
                            v_rd[0] = S.op("vector", lambda v, h=h, tt=tt: v.tensor_copy(
                                s_vb[:, (h * NCHUNK + tt) * 132:(h * NCHUNK + tt) * 132 + 128],
                                pC[:, h * 128:(h + 1) * 128]), tneed=tv)
                        S.dma("sync", sq_d[tt * 128:(tt + 1) * 128, blk * 512:(blk + 1) * 512],
                              s_sqb[:, tt * 512:(tt + 1) * 512])
                    wt_war[blk % 2] = S.c
                    wt_tw[blk % 2] = tqs
                    qs_rd[0] = S.op("vector", lambda v, blk=blk: v.tensor_copy(
                        s_qsumst[0:1, blk * 512:(blk + 1) * 512], pS[:]), tneed=tqs)
                    S.op("gpsimd", lambda g: g.memset(s_state[:], 0.0))
                    for n in range(NCHUNK):
                        for h in range(4):
                            qp = s_sqb[:, n * 512 + h * 128: n * 512 + (h + 1) * 128]
                            kp = s_skb[:, n * 512 + h * 128: n * 512 + (h + 1) * 128]
                            vt = s_vb[:, (h * NCHUNK + n) * 132:(h * NCHUNK + n) * 132 + 129]
                            stt = s_state[:, h * 132:h * 132 + 129]
                            S.op("vector", lambda v, stt=stt: v.tensor_copy(s_spb[:, 0:129], stt))
                            S.op("tensor", lambda t, qp=qp: t.transpose(pT[:], qp, s_id[:]))
                            S.op("vector", lambda v: v.tensor_copy(s_qpT[:], pT[:]))
                            S.op("tensor", lambda t, kp=kp: t.transpose(pT[:], kp, s_id[:]))
                            S.op("vector", lambda v: v.tensor_copy(s_kpT[:], pT[:]))
                            S.op("tensor", lambda t, kp=kp, vt=vt: t.matmul(pD[:], kp, vt, start=True, stop=True))
                            S.op("vector", lambda v, stt=stt: v.tensor_tensor(stt, stt, pD[:], ALU.add))
                            S.op("tensor", lambda t: t.matmul(pF[:], s_kpT[:], s_qpT[:], start=True, stop=True))
                            S.op("vector", lambda v: v.tensor_tensor(s_ms[:], pF[:], s_triu[:], ALU.mult))
                            S.mm_group([
                                lambda t: t.matmul(pE[:], s_qpT[:], s_spb[:, 0:129], start=True, stop=False),
                                lambda t, vt=vt: t.matmul(pE[:], s_ms[:], vt, start=False, stop=True)])
                            S.op("vector", lambda v, h=h: v.tensor_copy(
                                s_numst[:, h * 128:(h + 1) * 128], pE[:, 0:128]))
                            S.op("vector", lambda v, n=n, h=h, blk=blk: v.tensor_copy(
                                s_denst[:, n * NH + blk * 4 + h: n * NH + blk * 4 + h + 1], pE[:, 128:129]))
                        S.dma("sync", num_d[n * 128:(n + 1) * 128, blk * 512:(blk + 1) * 512], s_numst[:])
                    for h in range(4):
                        S.dma("sync", ag_kvz_in[(blk * 4 + h) * HD:(blk * 4 + h + 1) * HD, :],
                              s_state[:, h * 132:h * 132 + 129])
                S.dma("sync", ag_qs_in, s_qsumst[:])
                S.dma("sync",
                      den_d.rearrange("(n t) h -> t n h", t=128),
                      s_denst[:].rearrange("p (n h) -> p n h", h=NH))
                S.cc("AllGather", ALU.bypass, [ag_kvz_in], [ag_kvz_out])
                S.cc("AllGather", ALU.bypass, [ag_qs_in], [ag_qs_out])

            with ExitStack() as ph:
                S.fence()
                sbl = lambda n, s, t: ph.enter_context(nc.sbuf_tensor(n, s, t))
                psc = lambda n, s, t: ph.enter_context(nc.psum_tensor(n, s, t))
                s_w1b = sbl("s_w1b", [128, 2 * 16 * 512], BF16)
                s_w2b = sbl("s_w2b", [128, 2 * 16 * 512], BF16)
                s_hT = sbl("s_hT0", [128, 44 * 512], BF16)
                s_w3r = sbl("s_w3r", [128, 2 * D], BF16)
                s_sg = sbl("s_sg", [128, 512], F32)
                s_gst = sbl("s_gst", [128, D], F32)
                pf = [psc(f"pff{i}", [128, 512], F32) for i in range(8)]
                BW = 16 * 512

                def load_w12(it, war):
                    hb = it % 11
                    par = (it % 2) * BW
                    tks = []
                    for (woff, dstbuf) in ((0, s_w1b), (D, s_w2b)):
                        h0 = hb * 512
                        done = 0
                        while done < 512:
                            h = h0 + done
                            r = h // R1
                            lo = h - r * R1
                            w = min(512 - done, R1 - lo)
                            tks.append(S.dma(
                                "sync",
                                dstbuf[:, par:par + BW].rearrange("p (dc e) -> p dc e", e=512)[:, :, done:done + w],
                                ag_w12_out[r * 2 * D + woff: r * 2 * D + woff + D, lo:lo + w]
                                .rearrange("(dc p) e -> p dc e", p=128),
                                war=war, detach=True))
                            done += w
                    return tks

                def load_w3(jt, war):
                    hc44 = jt % 44
                    par = (jt % 2) * D
                    tks = []
                    for r8 in range(8):
                        tks.append(S.dma(
                            "sync", s_w3r[:, par + r8 * 256: par + (r8 + 1) * 256],
                            ag_w3_out[r8 * FF0 + hc44 * 128: r8 * FF0 + (hc44 + 1) * 128, :],
                            war=war, detach=True))
                    return tks

                w12_war = [0, 0]
                w12_tk = {}
                w12_tk[0] = load_w12(0, 0)
                w3_war = [0, 0]
                w3_tk = {}
                for half in range(2):
                    hoff = half * 512
                    for hb in range(11):
                        it = half * 11 + hb
                        par = (it % 2) * BW
                        if it + 1 < 22:
                            w12_tk[it + 1] = load_w12(it + 1, w12_war[(it + 1) % 2])
                        for hc in range(4):
                            for (wdst, pdst) in ((s_w1b, pf[0]), (s_w2b, pf[1])):
                                S.mm_group([
                                    (lambda t, dc=dc, wdst=wdst, pdst=pdst, hc=hc, par=par: t.matmul(
                                        pdst[:], wdst[:, par + dc * 512 + hc * 128: par + dc * 512 + (hc + 1) * 128],
                                        s_xnT[:, dc * TOK + hoff: dc * TOK + hoff + 512],
                                        start=(dc == 0), stop=(dc == 15)))
                                    for dc in range(16)], dneed=w12_tk[it])
                            S.op("scalar", lambda sc: sc.activation(s_sg[:], pf[0][:], AF.Sigmoid))
                            S.op("vector", lambda v, hb=hb, hc=hc: v.tensor_tensor(
                                s_hT[:, (hb * 4 + hc) * 512:(hb * 4 + hc + 1) * 512],
                                s_sg[:], pf[1][:], ALU.mult))
                        w12_war[it % 2] = S.c
                    for tp in range(2):
                        for hc44 in range(44):
                            jt = (half * 2 + tp) * 44 + hc44
                            parw = (jt % 2) * D
                            if jt not in w3_tk:
                                w3_tk[jt] = load_w3(jt, w3_war[jt % 2])
                            if hc44 + 1 < 44:
                                w3_tk[jt + 1] = load_w3(jt + 1, w3_war[(jt + 1) % 2])
                            for t2 in range(2):
                                S.mm_group([
                                    (lambda t, hc44=hc44, t2=t2, eb=eb, tp=tp, parw=parw: t.matmul(
                                        pf[t2 * 4 + eb][:],
                                        s_hT[:, hc44 * 512 + (tp * 2 + t2) * 128: hc44 * 512 + (tp * 2 + t2 + 1) * 128],
                                        s_w3r[:, parw + eb * 512: parw + (eb + 1) * 512],
                                        start=(hc44 == 0), stop=(hc44 == 43)))
                                    for eb in range(4)], dneed=w3_tk[jt])
                            w3_war[jt % 2] = S.c
                        for t2 in range(2):
                            for eb in range(4):
                                S.op("scalar", lambda sc, t2=t2, eb=eb: sc.activation(
                                    s_gst[:, eb * 512:(eb + 1) * 512], pf[t2 * 4 + eb][:], AF.Silu))
                            row0 = (half * 4 + tp * 2 + t2) * 128
                            S.dma("sync", g_d[row0:row0 + 128, :], s_gst[:])

            with ExitStack() as ph:
                S.fence()
                sbl = lambda n, s, t: ph.enter_context(nc.sbuf_tensor(n, s, t))
                pha = ph.enter_context(ExitStack())
                psc = lambda n, s, t: pha.enter_context(nc.psum_tensor(n, s, t))
                p_row = psc("p_row", [1, 512], F32)
                p_b16 = psc("p_b16", [128, 4], F32)
                p_b128 = psc("p_b128", [128, NH], F32)
                s_omg = sbl("s_omg", [128, NH], F32)
                s_wb = sbl("s_wb", [128, NH * 4], F32)
                s_attn = sbl("s_attn", [128, NCHUNK * D], BF16)
                phd1 = ph.enter_context(ExitStack())
                sbl = lambda n, s, t: phd1.enter_context(nc.sbuf_tensor(n, s, t))
                s_sq = sbl("s_sq2", [128, NCHUNK * D], BF16)
                s_mem = sbl("s_mem", [128, NH * 512], BF16)
                s_mnT = sbl("s_mnT", [128, NH * 4], BF16)
                s_sbf = sbl("s_sbf", [128, NH * 132], F32)
                s_sbb = sbl("s_sbb", [128, NH * 132], BF16)
                s_t129 = sbl("s_t129", [128, 2 * 132], F32)
                s_pm = sbl("s_pm", [1, 8], F32)
                s_pm128 = sbl("s_pm128", [128, 8], F32)
                s_bm = sbl("s_bm", [8, 1], F32)
                s_q8 = sbl("s_q8", [8, D], F32)
                s_qbar_row = sbl("s_qbar_row", [1, D], F32)
                s_qbar16 = sbl("s_qbar16", [NH, 128], F32)
                s_mnm16 = sbl("s_mnm16", [NH, 128], F32)
                s_tmp16 = sbl("s_tmp16", [NH, 128], F32)
                s_rel = sbl("s_rel", [NH, 4], F32)
                s_mnm4 = sbl("s_mnm4", [4, D], F32)
                s_mns = sbl("s_mns", [4, 1], F32)
                s_mrow = sbl("s_mrow", [1, 4], F32)
                s_e16 = sbl("s_e16", [NH, 4], F32)
                s_ke16 = sbl("s_ke16", [NH, 4], F32)
                s_mx = sbl("s_mx", [NH, 1], F32)
                s_sum = sbl("s_sum", [NH, 1], F32)
                s_gc = sbl("s_gc", [NH, 1], F32)
                s_gr = sbl("s_gr", [1, NH], F32)
                s_wgrow = sbl("s_wgrow", [1, 4], F32)
                s_numt = sbl("s_numt", [128, 2 * 128], F32)
                s_dent = sbl("s_dent", [128, NH], F32)
                s_numtot = sbl("s_numtot", [128, 128], F32)
                s_dtot = sbl("s_dtot", [128, 1], F32)
                s_acc = sbl("s_acc", [128, 128], F32)
                s_cn = sbl("s_cn", [128, 4], F32)
                s_qpT = sbl("s_qpT1", [128, 2 * 128], BF16)

                S.dma("gpsimd", s_mem[:].rearrange("p (h e) -> p h e", e=512), memh2)
                S.dma("gpsimd", s_mnT[:].rearrange("p (h m) -> p h m", m=4), mnT2)
                S.dma("sync", s_sq[:].rearrange("p (n e) -> p n e", e=D),
                      sq_d.rearrange("(n p) e -> p n e", p=128))
                S.dma("sync", s_pm[:], pmask)
                S.op("tensor", lambda t: t.matmul(p_b128[0:128, 0:8], s_ones1f[:], s_pm[:], start=True, stop=True))
                S.op("vector", lambda v: v.tensor_copy(s_pm128[:], p_b128[0:128, 0:8]))
                kv_war = [0, 0]
                kv_tk = {}

                def load_kvz(i):
                    r, h = i % 8, i // 8
                    par = (i % 2) * 132
                    tk = S.dma("sync", s_t129[:, par:par + 129],
                               ag_kvz_out[(r * NH + h) * HD:(r * NH + h + 1) * HD, :],
                               war=kv_war[i % 2], detach=True)
                    return [tk]

                kv_tk[0] = load_kvz(0)
                for h in range(NH):
                    sl = s_sbf[:, h * 132:h * 132 + 129]
                    S.op("gpsimd", lambda g, sl=sl: g.memset(sl, 0.0))
                    for r in range(8):
                        i = h * 8 + r
                        par = (i % 2) * 132
                        if i + 1 < NH * 8:
                            kv_tk[i + 1] = load_kvz(i + 1)
                        S.op("vector", lambda v, sl=sl, r=r, par=par: v.scalar_tensor_tensor(
                            sl, s_t129[:, par:par + 129], s_pm128[:, r:r + 1], sl, ALU.mult, ALU.add),
                            dneed=kv_tk[i])
                        kv_war[i % 2] = S.c
                    S.op("vector", lambda v, h=h: v.tensor_copy(
                        s_sbb[:, h * 132:h * 132 + 129], s_sbf[:, h * 132:h * 132 + 129]))
                S.dma("sync", s_bm[:], bmask)
                S.dma("sync", s_q8[:], ag_qs_out)
                for c4 in range(4):
                    S.op("tensor", lambda t, c4=c4: t.matmul(
                        p_row[:], s_bm[:], s_q8[:, c4 * 512:(c4 + 1) * 512], start=True, stop=True))
                    S.op("vector", lambda v, c4=c4: v.tensor_scalar_mul(
                        s_qbar_row[0:1, c4 * 512:(c4 + 1) * 512], p_row[:], 1.0 / SEQ))
                S.dma("sync", scr_qbar, s_qbar_row[:])
                S.dma("sync", s_qbar16[:], scr_qbar.rearrange("o (h d) -> (o h) d", d=128))
                for m in range(4):
                    S.dma("sync", s_mnm16[:], mnm[m:m + 1, :].rearrange("o (h d) -> (o h) d", d=128))
                    S.op("vector", lambda v: v.tensor_tensor(s_tmp16[:], s_qbar16[:], s_mnm16[:], ALU.mult))
                    S.op("vector", lambda v, m=m: v.reduce_sum(s_rel[:, m:m + 1], s_tmp16[:], axis=AX.X))
                S.op("vector", lambda v: v.tensor_scalar_mul(s_rel[:], s_rel[:], HD ** -0.5))
                S.dma("sync", s_mnm4[:], mnm)
                S.op("vector", lambda v: v.reduce_sum(s_mns[:], s_mnm4[:], axis=AX.X))
                S.dma("sync", scr4m, s_mns[:])
                S.dma("sync", s_mrow[:], scr4m.rearrange("a b -> b a"))
                S.op("vector", lambda v: v.tensor_single_scalar(s_mrow[:], s_mrow[:], 1e-6, ALU.is_lt))
                S.op("tensor", lambda t: t.matmul(p_b16[0:NH, 0:4], s_ones1f[0:1, 0:NH], s_mrow[:], start=True, stop=True))
                S.op("vector", lambda v: v.tensor_copy(s_e16[:], p_b16[0:NH, 0:4]))
                S.op("vector", lambda v: v.tensor_scalar(s_ke16[:], s_e16[:], -1.0, 1.0, ALU.mult, ALU.add))
                S.op("vector", lambda v: v.tensor_tensor(s_rel[:], s_rel[:], s_ke16[:], ALU.mult))
                S.op("vector", lambda v: v.scalar_tensor_tensor(s_rel[:], s_e16[:], -1e30, s_rel[:], ALU.mult, ALU.add))
                S.op("vector", lambda v: v.reduce_max(s_mx[:], s_rel[:], axis=AX.X))
                S.op("vector", lambda v: v.tensor_scalar(s_rel[:], s_rel[:], s_mx[:, 0:1], None, ALU.subtract))
                S.op("scalar", lambda sc: sc.activation(s_rel[:], s_rel[:], AF.Exp))
                S.op("vector", lambda v: v.reduce_sum(s_sum[:], s_rel[:], axis=AX.X))
                S.op("vector", lambda v: v.reciprocal(s_sum[:], s_sum[:]))
                S.op("vector", lambda v: v.tensor_scalar(s_rel[:], s_rel[:], s_sum[:, 0:1], None, ALU.mult))
                S.op("vector", lambda v: v.tensor_tensor(s_rel[:], s_rel[:], s_ke16[:], ALU.mult))
                S.dma("sync", s_gc[:], gate_c)
                S.op("scalar", lambda sc: sc.activation(s_gc[:], s_gc[:], AF.Sigmoid))
                S.op("vector", lambda v: v.tensor_scalar(s_rel[:], s_rel[:], s_gc[:, 0:1], None, ALU.mult))
                S.dma("sync", s_gr[:], gate_r)
                S.op("scalar", lambda sc: sc.activation(s_gr[:], s_gr[:], AF.Sigmoid))
                S.op("vector", lambda v: v.tensor_scalar(s_gr[:], s_gr[:], -1.0, 1.0, ALU.mult, ALU.add))
                S.op("tensor", lambda t: t.matmul(p_b128[:], s_ones1f[:], s_gr[:], start=True, stop=True))
                S.op("vector", lambda v: v.tensor_copy(s_omg[:], p_b128[:]))
                S.dma("sync", scrwg, s_rel[:])
                for h in range(NH):
                    S.dma("sync", s_wgrow[:], scrwg[h:h + 1, :])
                    S.op("tensor", lambda t: t.matmul(p_b16[0:128, 0:4], s_ones1f[:], s_wgrow[:], start=True, stop=True))
                    S.op("vector", lambda v, h=h: v.tensor_copy(s_wb[:, h * 4:(h + 1) * 4], p_b16[0:128, 0:4]))
                pha.close()
                phb = ph.enter_context(ExitStack())
                pscb = lambda n, s, t: phb.enter_context(nc.psum_tensor(n, s, t))
                pT2 = [pscb(f"pT1{i}", [128, 128], BF16) for i in range(2)]
                pE2 = [pscb(f"pE1{i}", [128, 129], F32) for i in range(2)]
                pn2 = [pscb(f"pn1{i}", [128, 4], F32) for i in range(2)]
                pam2 = [pscb(f"pam1{i}", [128, 512], F32) for i in range(2)]
                nm_war = [0, 0]
                nm_tk = {}

                def load_num(i):
                    n, h = i // NH, i % NH
                    par = (i % 2) * 128
                    return [S.dma("sync", s_numt[:, par:par + 128],
                                  num_d[n * 128:(n + 1) * 128, h * 128:(h + 1) * 128],
                                  war=nm_war[i % 2], detach=True)]

                nm_tk[0] = load_num(0)
                sq_ld = S.d["sync"]
                dnw = [[("gpsimd", S.d["gpsimd"])]]
                qpt_rd = [0, 0]
                e_rd = [0, 0]
                nrm_rd = [0, 0]
                am_rd = [0, 0]
                first_pe = [True]
                for n in range(NCHUNK):
                    S.dma("sync", s_dent[:], den_d[n * 128:(n + 1) * 128, :])
                    for h in range(NH):
                        i = n * NH + h
                        par = i % 2
                        parn = (i % 2) * 128
                        if i + 1 < NCHUNK * NH:
                            nm_tk[i + 1] = load_num(i + 1)
                        sqt = s_sq[:, n * D + h * 128: n * D + (h + 1) * 128]
                        dn = [("sync", sq_ld)] if first_pe[0] else None
                        first_pe[0] = False
                        tq = S.top_group([
                            (lambda t, sqt=sqt, par=par: t.transpose(pT2[par][:], sqt, s_id[:]))],
                            cneed=qpt_rd[par], dneed=dn)
                        cpq = S.op("vector", lambda v, par=par: v.tensor_copy(
                            s_qpT[:, par * 128:(par + 1) * 128], pT2[par][:]), tneed=tq)
                        qpt_rd[par] = cpq
                        tb = S.top_group([
                            (lambda t, h=h, par=par: t.matmul(
                                pE2[par][:], s_qpT[:, par * 128:(par + 1) * 128],
                                s_sbb[:, h * 132:h * 132 + 129], start=True, stop=True))],
                            cneed=max(cpq, e_rd[par]))
                        tn = S.top_group([
                            (lambda t, h=h, par=par: t.matmul(
                                pn2[par][:], s_qpT[:, par * 128:(par + 1) * 128],
                                s_mnT[:, h * 4:(h + 1) * 4], start=True, stop=True))],
                            cneed=max(cpq, nrm_rd[par]), dneed=dnw[0])
                        ta = S.top_group([
                            (lambda t, h=h, par=par: t.matmul(
                                pam2[par][:], s_qpT[:, par * 128:(par + 1) * 128],
                                s_mem[:, h * 512:(h + 1) * 512], start=True, stop=True))],
                            cneed=max(cpq, am_rd[par]))
                        dnw[0] = None
                        S.op("vector", lambda v, parn=parn, par=par: v.tensor_tensor(
                            s_numtot[:], s_numt[:, parn:parn + 128], pE2[par][:, 0:128], ALU.add),
                            dneed=nm_tk[i], tneed=tb)
                        nm_war[i % 2] = S.c
                        e_rd[par] = S.op("vector", lambda v, h=h, par=par: v.tensor_tensor(
                            s_dtot[:], s_dent[:, h:h + 1], pE2[par][:, 128:129], ALU.add))
                        S.op("vector", lambda v: v.tensor_scalar_max(s_dtot[:], s_dtot[:], 1e-6))
                        S.op("vector", lambda v: v.reciprocal(s_dtot[:], s_dtot[:]))
                        S.op("vector", lambda v, h=h: v.tensor_tensor(s_dtot[:], s_dtot[:], s_omg[:, h:h + 1], ALU.mult))
                        S.op("vector", lambda v: v.tensor_scalar(s_acc[:], s_numtot[:], s_dtot[:, 0:1], None, ALU.mult))
                        nrm_rd[par] = S.op("vector", lambda v, par=par: v.tensor_scalar_max(
                            s_cn[:], pn2[par][:], 1e-6), tneed=tn)
                        S.op("vector", lambda v: v.reciprocal(s_cn[:], s_cn[:]))
                        S.op("vector", lambda v, h=h: v.tensor_tensor(s_cn[:], s_cn[:], s_wb[:, h * 4:(h + 1) * 4], ALU.mult))
                        for m in range(4):
                            tick = S.op("vector", lambda v, m=m, par=par: v.scalar_tensor_tensor(
                                s_acc[:], pam2[par][:, m * 128:(m + 1) * 128], s_cn[:, m:m + 1],
                                s_acc[:], ALU.mult, ALU.add), tneed=(ta if m == 0 else 0))
                        am_rd[par] = tick
                        S.op("vector", lambda v, n=n, h=h: v.tensor_copy(
                            s_attn[:, n * D + h * 128: n * D + (h + 1) * 128], s_acc[:]))
                phb.close()
                phd1.close()
                S.fence()
                sbl = lambda n, s, t: ph.enter_context(nc.sbuf_tensor(n, s, t))
                pO = ph.enter_context(nc.psum_tensor("pO", [128, 512], F32))
                pT = ph.enter_context(nc.psum_tensor("pT1", [128, 128], BF16))
                s_wo = sbl("s_wo", [128, 16 * D], BF16)
                s_aT = sbl("s_aT", [128, 16 * 128], BF16)
                s_gt = sbl("s_gt", [128, D], F32)
                s_tmp = sbl("s_tmp", [128, 512], F32)
                s_out0 = sbl("s_out0", [128, D], F32)
                S.dma("gpsimd", s_wo[:].rearrange("p (ec d) -> p ec d", d=D), wo_h)
                for tt in range(NCHUNK):
                    for ec in range(16):
                        S.op("tensor", lambda t, tt=tt, ec=ec: t.transpose(
                            pT[:], s_attn[:, tt * D + ec * 128: tt * D + (ec + 1) * 128], s_id[:]))
                        S.op("vector", lambda v, ec=ec: v.tensor_copy(s_aT[:, ec * 128:(ec + 1) * 128], pT[:]))
                    S.dma("sync", s_xt[:], x_s[tt * 128:(tt + 1) * 128, :])
                    S.dma("sync", s_gt[:], g_d[tt * 128:(tt + 1) * 128, :])
                    for eb in range(4):
                        S.mm_group([
                            (lambda t, ec=ec, eb=eb: t.matmul(
                                pO[:], s_aT[:, ec * 128:(ec + 1) * 128],
                                s_wo[:, ec * D + eb * 512: ec * D + eb * 512 + 512],
                                start=(ec == 0), stop=(ec == 15)))
                            for ec in range(16)])
                        S.op("vector", lambda v, eb=eb: v.tensor_tensor(
                            s_tmp[:], pO[:], s_gt[:, eb * 512:(eb + 1) * 512], ALU.mult))
                        S.op("vector", lambda v, eb=eb: v.tensor_tensor(
                            s_out0[:, eb * 512:(eb + 1) * 512], s_tmp[:], s_xt[:, eb * 512:(eb + 1) * 512], ALU.add))
                    S.dma("sync", x1_s[tt * 128:(tt + 1) * 128, :], s_out0[:])

        if has1:
            with ExitStack() as ph:
                S.fence()
                sbl = lambda n, s, t: ph.enter_context(nc.sbuf_tensor(n, s, t))
                psc = lambda n, s, t: ph.enter_context(nc.psum_tensor(n, s, t))
                pRS = psc("pRS", [1, 512], F32)
                s_idf = sbl("s_idf", [128, 128], F32)
                s_ba = sbl("s_ba", [1, 3 * D], BF16)
                s_m128 = sbl("s_m128", [1, 128], BF16)
                s_rs = sbl("s_rs", [1, TOK], F32)
                s_rsh = sbl("s_rsh", [1, TOK], BF16)
                s_rsl = sbl("s_rsl", [1, TOK], BF16)
                s_gsc = sbl("s_gsc", [128, 64], F32)
                s_usc = sbl("s_usc", [128, 64], F32)
                s_dsc = sbl("s_dsc", [128, 16], F32)

                S.dma("sync", s_idf[:], identf)
                S.op("gpsimd", lambda g: g.memset(s_m128[:], -128.0))
                S.dma("gpsimd", s_ba[:], b_attn)
                S.dma("gpsimd", s_gsc[:], gsc2.rearrange("a p -> p a"), allow_slow_non_contiguous=True)
                S.dma("gpsimd", s_usc[:], usc2.rearrange("a p -> p a"), allow_slow_non_contiguous=True)
                S.dma("gpsimd", s_dsc[:], dsc2.rearrange("a p -> p a"), allow_slow_non_contiguous=True)
                with nc.psum_tensor("pT2", [128, 128], BF16) as pT:
                    emit_ln(x1_s, pT)
                for tg in range(2):
                    S.mm_group([
                        (lambda t, dc=dc, tg=tg: t.matmul(
                            pRS[:], s_onescb[:], s_xnT[:, dc * TOK + tg * 512: dc * TOK + (tg + 1) * 512],
                            start=(dc == 0), stop=(dc == 15)))
                        for dc in range(16)])
                    S.op("vector", lambda v, tg=tg: v.tensor_copy(s_rs[0:1, tg * 512:(tg + 1) * 512], pRS[:]))
                S.op("vector", lambda v: v.tensor_copy(s_rsh[:], s_rs[:]))
                S.op("vector", lambda v: v.tensor_tensor(s_rsl[:], s_rs[:], s_rsh[:], ALU.subtract))

                with ExitStack() as ph2:
                    S.fence()
                    sbl2 = lambda n, s, t: ph2.enter_context(nc.sbuf_tensor(n, s, t))
                    pQ = ph2.enter_context(nc.psum_tensor("pQ", [128, 512], F32))
                    s_wbk = sbl2("s_wbk", [128, 2 * 16 * 512], BF16)
                    s_qkv = sbl2("s_qkv", [128, 512], BF16)
                    BW = 16 * 512
                    wa_war = [0, 0]
                    wa_tk = {}

                    def load_wa(ob, war):
                        par = (ob % 2) * BW
                        return [S.dma(
                            "gpsimd",
                            s_wbk[:, par:par + BW].rearrange("p (dc e) -> p dc e", e=512),
                            wa_h[ob],
                            war=war, detach=True)]

                    wa_tk[0] = load_wa(0, 0)
                    for ob in range(12):
                        t0 = ob // 4
                        par = (ob % 2) * BW
                        if ob + 1 < 12:
                            wa_tk[ob + 1] = load_wa(ob + 1, wa_war[(ob + 1) % 2])
                        for tt in range(NCHUNK):
                            S.mm_group([
                                (lambda t, dc=dc, tt=tt, par=par: t.matmul(
                                    pQ[:], s_xnT[:, dc * TOK + tt * 128: dc * TOK + (tt + 1) * 128],
                                    s_wbk[:, par + dc * 512: par + (dc + 1) * 512], start=(dc == 0), stop=False))
                                for dc in range(16)] + [
                                lambda t, ob=ob: t.matmul(
                                    pQ[:], s_ones1b[:], s_ba[:, ob * 512:(ob + 1) * 512], start=False, stop=True)],
                                dneed=wa_tk[ob])
                            S.op("vector", lambda v: v.tensor_copy(s_qkv[:], pQ[:]))
                            for hh2 in range(2):
                                j = 2 * (ob % 4) + hh2
                                base = j * 6144 + t0 * 2048
                                S.dma("sync",
                                      qkvc[base: base + 2048, :]
                                      .rearrange("(s q) e -> s q e", q=TOK)[:, tt * 128:(tt + 1) * 128, :]
                                      .rearrange("s p e -> p s e"),
                                      s_qkv[:, hh2 * 256:(hh2 + 1) * 256]
                                      .rearrange("p (s e) -> p s e", e=128))
                        wa_war[ob % 2] = S.c
                    S.cc("AllToAll", ALU.bypass, [qkvc], [qkvg])

                with ExitStack() as ph2:
                    S.fence()
                    sbl2 = lambda n, s, t: ph2.enter_context(nc.sbuf_tensor(n, s, t))
                    psc2 = lambda n, s, t: ph2.enter_context(nc.psum_tensor(n, s, t))
                    pG2 = [psc2(f"pG{i}", [128, 512], F32) for i in range(2)]
                    pU2 = [psc2(f"pU{i}", [128, 512], F32) for i in range(2)]
                    pDn2 = [psc2(f"pDn{i}", [128, 512], F32) for i in range(2)]
                    pTf = psc2("pTf2", [128, 128], F32)
                    s_gw = sbl2("s_gw", [128, 2 * 16 * 256], BF16)
                    s_uw = sbl2("s_uw", [128, 2 * 16 * 256], BF16)
                    s_hT = sbl2("s_hT1", [128, 64 * 512], BF16)
                    s_dw = sbl2("s_dw", [128, 2 * 4096], BF16)
                    s_sg2 = sbl2("s_sg2", [128, 512], F32)
                    s_g1 = sbl2("s_g1", [128, 512], F32)
                    s_rsh2 = sbl2("s_rsh2", [1, TOK], BF16)
                    s_rsl2 = sbl2("s_rsl2", [1, TOK], BF16)
                    BWG = 16 * 256
                    gu_war = [0, 0]
                    gu_tk = {}

                    def load_gu(it, war, twar=0):
                        hb32 = it % 32
                        par = (it % 2) * BWG
                        tks = []
                        for (wsrc, wdst) in ((gw_h, s_gw), (up_h, s_uw)):
                            tks.append(S.dma(
                                "gpsimd",
                                wdst[:, par:par + BWG].rearrange("p (dc e) -> p dc e", e=256),
                                wsrc[hb32],
                                war=war, twar=twar, detach=True))
                        return tks

                    dw_tk = {}

                    def load_dw(half, ec, q4, war, twar=0):
                        par = (q4 % 4) * 2048
                        return [S.dma(
                            "gpsimd", s_dw[:, par:par + 2048],
                            dwe32[ec][:, q4 * 2048:(q4 + 1) * 2048],
                            war=war, twar=twar, detach=True)]

                    rs_ready = S.c
                    sil_t = [0, 0]
                    stt_t = [0, 0]
                    gu_tw = [0, 0]
                    sil_dn = [0, 0]
                    tf_cp = [0]
                    gu_tk[0] = load_gu(0, S.c)
                    for half in range(2):
                        hoff = half * 512
                        for hb32 in range(32):
                            it = half * 32 + hb32
                            par = (it % 2) * BWG
                            if it + 1 < 64:
                                gu_tk[it + 1] = load_gu(it + 1, gu_war[(it + 1) % 2], gu_tw[(it + 1) % 2])
                            tks = []
                            for hc in range(2):
                                ic = hb32 * 2 + hc
                                q = ic % 2
                                tg = S.top_group([
                                    (lambda t, dc=dc, hc=hc, par=par, q=q: t.matmul(
                                        pG2[q][:], s_gw[:, par + dc * 256 + hc * 128: par + dc * 256 + (hc + 1) * 128],
                                        s_xnT[:, dc * TOK + hoff: dc * TOK + hoff + 512],
                                        start=(dc == 0), stop=False))
                                    for dc in range(16)] + [
                                    (lambda t, sr=sr, q=q: t.matmul(
                                        pG2[q][:], s_m128[:], sr[0:1, hoff:hoff + 512], start=False,
                                        stop=(sr is s_rsl)))
                                    for sr in (s_rsh, s_rsl)],
                                    cneed=max(sil_t[q], rs_ready), dneed=gu_tk[it])
                                tu = S.top_group([
                                    (lambda t, dc=dc, hc=hc, par=par, q=q: t.matmul(
                                        pU2[q][:], s_uw[:, par + dc * 256 + hc * 128: par + dc * 256 + (hc + 1) * 128],
                                        s_xnT[:, dc * TOK + hoff: dc * TOK + hoff + 512],
                                        start=(dc == 0), stop=False))
                                    for dc in range(16)] + [
                                    (lambda t, sr=sr, q=q: t.matmul(
                                        pU2[q][:], s_m128[:], sr[0:1, hoff:hoff + 512], start=False,
                                        stop=(sr is s_rsl)))
                                    for sr in (s_rsh, s_rsl)],
                                    cneed=max(stt_t[q], rs_ready), dneed=gu_tk[it])
                                tks.append(tu)
                                sil_t[q] = S.op("scalar", lambda sc, ic=ic, q=q: sc.activation(
                                    s_sg2[:], pG2[q][:], AF.Silu, scale=s_gsc[:, ic:ic + 1]), tneed=tg)
                                stt_t[q] = S.op("vector", lambda v, ic=ic, q=q: v.scalar_tensor_tensor(
                                    s_hT[:, ic * 512:(ic + 1) * 512], pU2[q][:], s_usc[:, ic:ic + 1],
                                    s_sg2[:], ALU.mult, ALU.mult), tneed=tu)
                            gu_war[it % 2] = S.c
                            gu_tw[it % 2] = tks[-1]
                        ht_done = S.c
                        hr = S.top_group([
                            (lambda t, icc=icc: t.matmul(
                                pRS[:], s_onescb[:], s_hT[:, icc * 512:(icc + 1) * 512],
                                start=(icc == 0), stop=(icc == 63)))
                            for icc in range(64)], cneed=ht_done)
                        S.op("vector", lambda v: v.tensor_copy(s_rs[0:1, hoff:hoff + 512], pRS[:]), tneed=hr)
                        S.op("vector", lambda v: v.tensor_copy(
                            s_rsh2[0:1, hoff:hoff + 512], s_rs[0:1, hoff:hoff + 512]))
                        S.op("vector", lambda v: v.tensor_tensor(
                            s_rsl2[0:1, hoff:hoff + 512], s_rs[0:1, hoff:hoff + 512],
                            s_rsh2[0:1, hoff:hoff + 512], ALU.subtract))
                        rs2_ready = S.c
                        for ec in range(16):
                            pe2 = ec % 2
                            for q4 in range(4):
                                if (half, ec, q4) not in dw_tk:
                                    dw_tk[(half, ec, q4)] = load_dw(half, ec, q4, S.c)
                            td = 0
                            for q4 in range(4):
                                par4 = q4 * 2048
                                grp = [
                                    (lambda t, ic=ic, par4=par4, q4=q4, pe2=pe2: t.matmul(
                                        pDn2[pe2][:], s_dw[:, par4 + (ic - q4 * 16) * 128: par4 + (ic - q4 * 16 + 1) * 128],
                                        s_hT[:, ic * 512:(ic + 1) * 512],
                                        start=(ic == 0), stop=False))
                                    for ic in range(q4 * 16, (q4 + 1) * 16)]
                                if q4 == 3:
                                    grp += [
                                        (lambda t, sr=sr, pe2=pe2: t.matmul(
                                            pDn2[pe2][:], s_m128[:], sr[0:1, hoff:hoff + 512], start=False,
                                            stop=(sr is s_rsl2)))
                                        for sr in (s_rsh2, s_rsl2)]
                                cn = ht_done
                                if q4 == 0:
                                    cn = max(cn, sil_dn[pe2])
                                if q4 == 3:
                                    cn = max(cn, rs2_ready)
                                td = S.top_group(grp, cneed=cn, dneed=dw_tk[(half, ec, q4)])
                                nh, ne = (half, ec + 1) if ec + 1 < 16 else (half + 1, 0)
                                if nh < 2:
                                    dw_tk[(nh, ne, q4)] = load_dw(nh, ne, q4, 0, twar=td)
                            sil_dn[pe2] = S.op("scalar", lambda sc, ec=ec, pe2=pe2: sc.activation(
                                s_g1[:], pDn2[pe2][:], AF.Silu, scale=s_dsc[:, ec:ec + 1]), tneed=td)
                            for sub in range(4):
                                tf = S.top_group([
                                    (lambda t, sub=sub: t.transpose(
                                        pTf[:], s_g1[:, sub * 128:(sub + 1) * 128], s_idf[:]))],
                                    cneed=max(sil_dn[pe2], tf_cp[0]))
                                tf_cp[0] = S.op("vector", lambda v: v.tensor_copy(s_sg2[:, 0:128], pTf[:]),
                                                tneed=tf)
                                S.dma("sync",
                                      g1_d[hoff + sub * 128: hoff + (sub + 1) * 128, ec * 128:(ec + 1) * 128],
                                      s_sg2[:, 0:128])

            with ExitStack() as ph:
                S.fence()
                sbl = lambda n, s, t: ph.enter_context(nc.sbuf_tensor(n, s, t))
                psc = lambda n, s, t: ph.enter_context(nc.psum_tensor(n, s, t))
                pS1 = [psc("pS1a", [128, 512], F32), psc("pS1b", [128, 512], F32)]
                pST = [psc("pSTa", [128, 512], F32), psc("pSTb", [128, 512], F32)]
                pY = [psc(f"pY{i}", [128, 512], F32) for i in range(4)]
                s_q = sbl("s_qh", [128, 2 * SEQ], BF16)
                s_k = sbl("s_kh", [128, 2 * SEQ], BF16)
                s_v = sbl("s_vh", [128, 2 * 32 * 132], BF16)
                s_dm = sbl("s_dm", [128, 4 * 512], BF16)
                s_idf2 = sbl("s_idf2", [128, 128], F32)
                s_nma = sbl("s_nma", [128, 32], F32)
                s_mt = sbl("s_mt", [128, 1], F32)
                s_negm_f = sbl("s_negm_f", [1, 8 * 512], F32)
                s_negm_bf = sbl("s_negm_bf", [1, 8 * 512], BF16)
                s_pt = sbl("s_pt", [128, 2 * 512], BF16)
                s_l = sbl("s_l", [128, 1], F32)
                s_yst = sbl("s_yst", [128, 512], BF16)
                S.dma("sync", s_idf2[:], identf)
                S.dma("sync", s_dm[:].rearrange("p (r w) -> p r w", w=512), dm.rearrange("r p w -> p r w"))
                SCc = HD ** -0.5
                VW = 32 * 132

                def load_slot(slot):
                    bs, hsub = slot // 2, slot % 2
                    pq = (slot % 2) * SEQ
                    pv = (slot % 2) * VW
                    vsl = s_v[:, pv:pv + VW]
                    tw = S.t
                    S.op("gpsimd", lambda g, vsl=vsl: g.memset(vsl, 0.0), tneed=tw)
                    S.op("gpsimd", lambda g, vsl=vsl: g.memset(
                        vsl.rearrange("p (c w) -> p c w", w=132)[:, :, 128:129], 1.0))
                    war = S.c
                    tks = []
                    for rho in range(4):
                        r = 4 * bs + rho
                        tks.append(S.dma(
                            "sync", s_q[:, pq + rho * TOK: pq + (rho + 1) * TOK],
                            qkvg[r * 6144 + hsub * TOK: r * 6144 + (hsub + 1) * TOK, :],
                            war=war, twar=tw, detach=True, transpose=True))
                        tks.append(S.dma(
                            "sync", s_k[:, pq + rho * TOK: pq + (rho + 1) * TOK],
                            qkvg[r * 6144 + 2048 + hsub * TOK: r * 6144 + 2048 + (hsub + 1) * TOK, :],
                            war=war, twar=tw, detach=True, transpose=True))
                        tks.append(S.dma(
                            "sync",
                            vsl.rearrange("p (c w) -> p c w", w=132)[:, rho * 8:(rho + 1) * 8, 0:128],
                            qkvg[r * 6144 + 2 * 2048 + hsub * TOK: r * 6144 + 2 * 2048 + (hsub + 1) * TOK, :]
                            .rearrange("(c p) e -> p c e", p=128),
                            war=war, twar=tw, detach=True))
                    return tks

                slot_tk = {0: load_slot(0)}
                red_t = [0, 0]
                exp_t = [0, 0]
                pv_last = [0, 0]
                fin_t = [0, 0, 0, 0]
                cp_t = [0]
                for slot in range(4):
                    bs, hsub = slot // 2, slot % 2
                    pq = (slot % 2) * SEQ
                    pv = (slot % 2) * VW
                    if slot + 1 < 4:
                        slot_tk[slot + 1] = load_slot(slot + 1)
                    sqh = s_q[:, pq:pq + SEQ]
                    skh = s_k[:, pq:pq + SEQ]
                    svh = s_v[:, pv:pv + VW]
                    S.op("gpsimd", lambda g: g.memset(s_nma[:], 0.0), tneed=S.t)
                    k = 0
                    for ig in range(8):
                        for j in range(4):
                            qt = ig * 4 + j
                            for bb in range(qt // 4):
                                par = k % 2
                                tk = S.top_group([
                                    (lambda t, qt=qt, bb=bb, par=par: t.matmul(
                                        pS1[par][:], sqh[:, qt * 128:(qt + 1) * 128],
                                        skh[:, bb * 512:(bb + 1) * 512], start=True, stop=True))],
                                    cneed=red_t[par], dneed=(slot_tk[slot] if k == 0 else None))
                                S.op("vector", lambda v, par=par: v.reduce_max(
                                    s_mt[:], pS1[par][:], axis=AX.X, negate=True), tneed=tk)
                                red_t[par] = S.op("vector", lambda v, qt=qt: v.tensor_tensor(
                                    s_nma[:, qt:qt + 1], s_nma[:, qt:qt + 1], s_mt[:], ALU.min))
                                k += 1
                    nm_ready = S.c
                    for qt in range(32):
                        tf = S.top_group([
                            (lambda t, qt=qt: t.transpose(
                                pST[0][0:1, 0:128], s_nma[:, qt:qt + 1], s_idf2[:]))],
                            cneed=max(nm_ready, cp_t[0], exp_t[0]))
                        cp_t[0] = S.op("vector", lambda v, qt=qt: v.tensor_copy(
                            s_negm_f[0:1, qt * 128:(qt + 1) * 128], pST[0][0:1, 0:128]), tneed=tf)
                    S.op("vector", lambda v: v.tensor_copy(s_negm_bf[:], s_negm_f[:]))
                    nb_ready = S.c
                    exp_t = [max(exp_t[0], nb_ready), exp_t[1]]
                    for ig in range(8):
                        nC = 4 * ig + 4
                        st_tk = {}

                        def emit_st(c):
                            par = c % 2
                            st_tk[c] = S.top_group([
                                (lambda t, c=c, ig=ig, par=par: t.matmul(
                                    pST[par][:], skh[:, c * 128:(c + 1) * 128],
                                    sqh[:, ig * 512:(ig + 1) * 512], start=True, stop=False)),
                                (lambda t, ig=ig, par=par: t.matmul(
                                    pST[par][:], s_ones1b[:],
                                    s_negm_bf[0:1, ig * 512:(ig + 1) * 512], start=False, stop=True))],
                                cneed=max(nb_ready, exp_t[par]))

                        emit_st(0)
                        for c in range(nC):
                            par = c % 2
                            exp_t[par] = S.op("scalar", lambda sc, par=par: sc.activation(
                                s_pt[:, par * 512:(par + 1) * 512], pST[par][:], AF.Exp, scale=SCc),
                                tneed=max(st_tk[c], pv_last[par]))
                            if c + 1 < nC:
                                emit_st(c + 1)
                            if c >= 4 * ig:
                                rel = c - 4 * ig
                                S.op("vector", lambda v, rel=rel, par=par: v.tensor_tensor(
                                    s_pt[:, par * 512:(par + 1) * 512], s_pt[:, par * 512:(par + 1) * 512],
                                    s_dm[:, rel * 512:(rel + 1) * 512], ALU.mult))
                            pt_ready = S.c
                            for sub in range(4):
                                if 4 * ig + sub >= c:
                                    cn = pt_ready if c > 0 else max(pt_ready, fin_t[sub])
                                    pv_last[par] = S.top_group([
                                        (lambda t, sub=sub, c=c, ig=ig, par=par: t.matmul(
                                            pY[sub][0:128, 0:129],
                                            s_pt[:, par * 512 + sub * 128: par * 512 + (sub + 1) * 128],
                                            svh[:, c * 132:c * 132 + 129],
                                            start=(c == 0), stop=(c == 4 * ig + sub)))],
                                        cneed=cn)
                                    if c == 4 * ig + sub:
                                        fin_t[sub] = pv_last[par]
                        for sub in range(4):
                            S.op("vector", lambda v, sub=sub: v.reciprocal(s_l[:], pY[sub][:, 128:129]),
                                 tneed=fin_t[sub])
                            fin_t[sub] = S.op("vector", lambda v, sub=sub: v.tensor_scalar(
                                s_yst[:, sub * 128:(sub + 1) * 128], pY[sub][:, 0:128],
                                s_l[:, 0:1], None, ALU.mult))
                        j2 = 4 * bs + ig // 2
                        base = j2 * 2048 + hsub * TOK + (ig % 2) * 512
                        S.dma("sync",
                              yc[base:base + 512, :].rearrange("(s q) e -> q s e", q=128),
                              s_yst[:].rearrange("p (s e) -> p s e", e=128))
                S.cc("AllToAll", ALU.bypass, [yc], [yg])

            with ExitStack() as ph:
                S.fence()
                sbl = lambda n, s, t: ph.enter_context(nc.sbuf_tensor(n, s, t))
                pGf = ph.enter_context(nc.psum_tensor("pGf", [128, 512], F32))
                s_yT = sbl("s_yT", [128, 16 * TOK], BF16)
                s_wp = sbl("s_wp", [128, 16 * D], BF16)
                s_bp = sbl("s_bp", [1, D], BF16)
                s_gt = sbl("s_gt2", [128, D], F32)
                s_tmp = sbl("s_tmp2", [128, 512], F32)
                s_outg = sbl("s_outg", [128, D], F32)
                S.dma("gpsimd", s_bp[:], b_proj)
                for h in range(NH):
                    S.dma("sync", s_yT[:, h * TOK:(h + 1) * TOK],
                          yg[(h // 2) * 2048 + (h % 2) * TOK: (h // 2) * 2048 + (h % 2 + 1) * TOK, :],
                          transpose=True)
                S.dma("gpsimd", s_wp[:].rearrange("p (cc d) -> p cc d", d=D), wp_h)
                for tt in range(NCHUNK):
                    S.dma("sync", s_xt[:], x1_s[tt * 128:(tt + 1) * 128, :])
                    S.dma("sync", s_gt[:], g1_d[tt * 128:(tt + 1) * 128, :])
                    for eb in range(4):
                        S.mm_group([
                            (lambda t, cc=cc, eb=eb, tt=tt: t.matmul(
                                pGf[:], s_yT[:, cc * TOK + tt * 128: cc * TOK + (tt + 1) * 128],
                                s_wp[:, cc * D + eb * 512: cc * D + eb * 512 + 512],
                                start=(cc == 0), stop=False))
                            for cc in range(16)] + [
                            lambda t, eb=eb: t.matmul(
                                pGf[:], s_ones1b[:], s_bp[:, eb * 512:(eb + 1) * 512], start=False, stop=True)])
                        S.op("vector", lambda v, eb=eb: v.tensor_tensor(
                            s_tmp[:], pGf[:], s_gt[:, eb * 512:(eb + 1) * 512], ALU.mult))
                        S.op("vector", lambda v, eb=eb: v.tensor_tensor(
                            s_outg[:, eb * 512:(eb + 1) * 512], s_tmp[:], s_xt[:, eb * 512:(eb + 1) * 512], ALU.add))
                    S.dma("sync", out_s[tt * 128:(tt + 1) * 128, :], s_outg[:])

        S.finish()
    return nc


def _mk_inputs(inputs, parts="L0L1", x1=None):
    has0 = "L0" in parts
    has1 = "L1" in parts
    ident = np.eye(128, dtype=BF)
    maps = []
    if has0:
        x = np.ascontiguousarray(inputs["L0_attn_hidden_states"].reshape(SEQ * BATCH, D))
        def _colmajor(wT, nb, w):
            return np.ascontiguousarray(wT.reshape(16, 128, nb, w).transpose(2, 1, 0, 3))
        wq_hv = _colmajor(inputs["L0_attn_w_q"].T, 4, 512)
        wk_hv = _colmajor(inputs["L0_attn_w_k"].T, 4, 512)
        wv_hv = _colmajor(inputs["L0_attn_w_v"].T, 4, 512)
        wo_hv = np.ascontiguousarray(inputs["L0_attn_w_o"].T.reshape(16, 128, D).transpose(1, 0, 2))
        memories = inputs["L0_attn_memories"]
        memh2v = np.ascontiguousarray(memories.transpose(2, 1, 0, 3).reshape(HD, NH, 512))
        mnT2v = np.ascontiguousarray(inputs["L0_attn_memory_norms"].transpose(2, 1, 0))
        mnmv = np.ascontiguousarray(inputs["L0_attn_memory_norms"].reshape(4, D))
        gate_cv = np.ascontiguousarray(inputs["L0_attn_gate"].reshape(NH, 1))
        gate_rv = np.ascontiguousarray(inputs["L0_attn_gate"].reshape(1, NH))
        triuv = np.triu(np.ones((128, 128), BF))
    if has1:
        b_attnv = np.ascontiguousarray(inputs["L1_attn_b_attn"].reshape(1, 3 * D))
        def _colmajor(wT, nb, w):
            return np.ascontiguousarray(wT.reshape(16, 128, nb, w).transpose(2, 1, 0, 3))
        wa_hv = _colmajor(inputs["L1_attn_W_attn"], 12, 512)
        gw_hv = _colmajor(inputs["L1_ffn_gate_w"].T, 32, 256)
        up_hv = _colmajor(inputs["L1_ffn_up_w"].T, 32, 256)
        wp_hv = np.ascontiguousarray(inputs["L1_attn_W_proj"].reshape(16, 128, D).transpose(1, 0, 2))
        dwe32v = np.ascontiguousarray(
            inputs["L1_ffn_down_w"].T.reshape(64, 128, 16, 128).transpose(2, 1, 0, 3).reshape(16, 128, FF1))
        gsc2v = np.ascontiguousarray(inputs["L1_ffn_gate_scale"].reshape(64, 128))
        usc2v = np.ascontiguousarray(inputs["L1_ffn_up_scale"].reshape(64, 128))
        dsc2v = np.ascontiguousarray(inputs["L1_ffn_down_scale"].reshape(16, 128))
        b_projv = np.ascontiguousarray(inputs["L1_attn_b_proj"].reshape(1, D))
        dmm = np.zeros((4, 128, 512), BF)
        for r in range(4):
            for tk in range(128):
                dmm[r, tk, r * 128 + tk:] = 1.0
        identfv = np.eye(128, dtype=np.float32)
    for r in range(NCORES):
        m = {"ident": ident}
        if has0:
            b = r // 4
            pm = np.zeros((1, 8), np.float32)
            pm[0, 4 * b:r] = 1.0
            bm = np.zeros((8, 1), np.float32)
            bm[4 * b:4 * b + 4, 0] = 1.0
            m.update({
                "x_s": np.ascontiguousarray(x[r * TOK:(r + 1) * TOK]),
                "w1s": _shard_rows(inputs["L0_ffn_w1"], r),
                "w2s": _shard_rows(inputs["L0_ffn_w2"], r),
                "w3s": _shard_rows(inputs["L0_ffn_w3"], r),
                "wq_h": wq_hv, "wk_h": wk_hv, "wv_h": wv_hv, "wo_h": wo_hv,
                "memh2": memh2v, "mnT2": mnT2v, "mnm": mnmv,
                "gate_c": gate_cv, "gate_r": gate_rv,
                "triu": triuv, "pmask": pm, "bmask": bm,
            })
        else:
            m["x1_s"] = np.ascontiguousarray(x1[r])
        if has1:
            m.update({
                "wa_h": wa_hv, "b_attn": b_attnv,
                "gw_h": gw_hv, "up_h": up_hv, "dwe32": dwe32v,
                "gsc2": gsc2v, "usc2": usc2v, "dsc2": dsc2v,
                "wp_h": wp_hv, "b_proj": b_projv,
                "dm": dmm, "identf": identfv,
            })
        maps.append(m)
    return maps


def kernel_mk1(inputs):
    nc = build_mk("L0L1")
    maps = _mk_inputs(inputs, "L0L1")
    res = run_bass_kernel_spmd(nc, maps, core_ids=list(range(NCORES)))
    rs = res.results
    out = np.concatenate([rs[r]["out_s"] for r in range(NCORES)], axis=0)
    return out.reshape(BATCH, SEQ, D)


def kernel_mk2(inputs):
    nc0 = build_mk("L0")
    res0 = run_bass_kernel_spmd(nc0, _mk_inputs(inputs, "L0"), core_ids=list(range(NCORES)))
    x1 = np.stack([res0.results[r]["x1_s"] for r in range(NCORES)], axis=0)
    nc1 = build_mk("L1")
    res1 = run_bass_kernel_spmd(nc1, _mk_inputs(inputs, "L1", x1=x1), core_ids=list(range(NCORES)))
    out = np.concatenate([res1.results[r]["out_s"] for r in range(NCORES)], axis=0)
    return out.reshape(BATCH, SEQ, D)



def _shard_rows(a, r, n=NCORES):
    rows = a.shape[0] // n
    return np.ascontiguousarray(a[r * rows:(r + 1) * rows])


def run_A(inputs):
    nc = build_A()
    in_maps = []
    for r in range(NCORES):
        in_maps.append({
            "w1s": _shard_rows(inputs["L0_ffn_w1"], r),
            "w2s": _shard_rows(inputs["L0_ffn_w2"], r),
            "w3s": _shard_rows(inputs["L0_ffn_w3"], r),
            "gws": _shard_rows(inputs["L1_ffn_gate_w"], r),
            "uws": _shard_rows(inputs["L1_ffn_up_w"], r),
            "dws": _shard_rows(inputs["L1_ffn_down_w"], r),
            "gsc": _shard_rows(inputs["L1_ffn_gate_scale"], r),
            "usc": _shard_rows(inputs["L1_ffn_up_scale"], r),
            "dsc": _shard_rows(inputs["L1_ffn_down_scale"], r),
        })
    res = run_bass_kernel_spmd(nc, in_maps, core_ids=list(range(NCORES)))
    rs = res.results
    gsums = np.concatenate([rs[r]["gsum"] for r in range(NCORES)], axis=0)
    gwq = np.concatenate([rs[r]["gwq"] for r in range(NCORES)], axis=0)
    uwq = np.concatenate([rs[r]["uwq"] for r in range(NCORES)], axis=0)
    dwq = np.concatenate([rs[r]["dwq"] for r in range(NCORES)], axis=0)
    return gsums, gwq, uwq, dwq


def run_B(inputs, gsums):
    nc = build_B()
    in_maps = []
    for r in range(NCORES):
        in_maps.append({
            "w1s": _shard_rows(inputs["L0_ffn_w1"], r),
            "w2s": _shard_rows(inputs["L0_ffn_w2"], r),
            "w3s": _shard_rows(inputs["L0_ffn_w3"], r),
            "gsums": gsums,
        })
    res = run_bass_kernel_spmd(nc, in_maps, core_ids=list(range(NCORES)))
    rs = res.results
    w1q = np.concatenate([rs[r]["w1q"] for r in range(NCORES)], axis=0)
    w2q = np.concatenate([rs[r]["w2q"] for r in range(NCORES)], axis=0)
    w3q = np.concatenate([rs[r]["w3q"] for r in range(NCORES)], axis=0)
    return w1q, w2q, w3q


def kernel(**inputs):
    return kernel_mk1(inputs)


# version 13
# speedup vs baseline: 1.2096x; 1.2096x over previous
import numpy as np
import ml_dtypes
from contextlib import ExitStack

import concourse.bass as bass
import concourse.mybir as mybir
from concourse.bass_utils import run_bass_kernel_spmd

F32 = mybir.dt.float32
BF16 = mybir.dt.bfloat16
I32 = mybir.dt.int32
AF = mybir.ActivationFunctionType
ALU = mybir.AluOpType
AX = mybir.AxisListType
BF = ml_dtypes.bfloat16

NCORES = 8
D = 2048
FF0 = 5632
FF1 = 8192
NH = 16
HD = 128
SEQ = 4096
BATCH = 2
TOK = 1024
CH = 128
NCHUNK = TOK // CH


class Sched:

    def __init__(self, nc, block, st):
        self.nc = nc
        self.block = block
        self.dsem = {"sync": st.enter_context(nc.semaphore("dsemS")),
                     "gpsimd": st.enter_context(nc.semaphore("dsemG"))}
        self.pfsem = {"sync": st.enter_context(nc.semaphore("pfsemS")),
                      "gpsimd": st.enter_context(nc.semaphore("pfsemG"))}
        self.csem = st.enter_context(nc.semaphore("csem"))
        self.tsem = st.enter_context(nc.semaphore("tsem"))
        self.d = {"sync": 0, "gpsimd": 0}
        self.p = {"sync": 0, "gpsimd": 0}
        self.c = 0
        self.t = 0
        self.floor = (0, 0, 0, 0, 0)
        self.floor_t = 0
        self.seen = {}
        self.pend_eng = None
        self.pend = []

    def _emit(self, engname, body):
        if self.pend_eng != engname:
            self.flush()
            self.pend_eng = engname
        self.pend.append(body)
        if len(self.pend) >= 64:
            self.flush()

    def flush(self):
        if self.pend:
            bodies = self.pend
            self.pend = []
            getattr(self.block, self.pend_eng)(
                lambda eng: [b(eng) for b in bodies])
        self.pend_eng = None

    def fence(self):
        self.flush()
        self.floor = (self.c, self.d["sync"], self.d["gpsimd"],
                      self.p["sync"], self.p["gpsimd"])
        self.floor_t = self.t

    def _w(self, eng, name, c_t, ds_t, dg_t, ps_t=0, pg_t=0, t_t=0):
        s = self.seen.setdefault(name, {"c": 0, "s": 0, "g": 0, "ps": 0, "pg": 0, "t": 0})
        t_t = max(t_t, self.floor_t)
        if ds_t > s["s"]:
            eng.wait_ge(self.dsem["sync"], ds_t)
            s["s"] = ds_t
        if dg_t > s["g"]:
            eng.wait_ge(self.dsem["gpsimd"], dg_t)
            s["g"] = dg_t
        if ps_t > s["ps"]:
            eng.wait_ge(self.pfsem["sync"], ps_t)
            s["ps"] = ps_t
        if pg_t > s["pg"]:
            eng.wait_ge(self.pfsem["gpsimd"], pg_t)
            s["pg"] = pg_t
        if t_t > s["t"]:
            eng.wait_ge(self.tsem, t_t)
            s["t"] = t_t
        if c_t > s["c"]:
            eng.wait_ge(self.csem, c_t)
            s["c"] = c_t

    def _dneed(self, dneed):
        if dneed == "all":
            return self.d["sync"], self.d["gpsimd"], 0, 0
        ds = dg = ps = pg = 0
        for t in (dneed or ()):
            q, v = t
            if q == "sync":
                ds = max(ds, v)
            elif q == "gpsimd":
                dg = max(dg, v)
            elif q == "ps":
                ps = max(ps, v)
            else:
                pg = max(pg, v)
        return ds, dg, ps, pg

    def dma(self, engname, out, in_, war=None, detach=False, twar=0, **kw):
        fc, fds, fdg, fps, fpg = self.floor
        c_t = max(self.c if war is None else war, fc)
        sem = self.pfsem[engname] if detach else self.dsem[engname]
        def body(eng):
            self._w(eng, engname, c_t, fds, fdg, fps, fpg, twar)
            eng.dma_start(out=out, in_=in_, **kw).then_inc(sem, 16)
        self._emit(engname, body)
        if detach:
            self.p[engname] += 16
            return (("ps" if engname == "sync" else "pg"), self.p[engname])
        self.d[engname] += 16
        return (engname, self.d[engname])

    def op(self, engname, fn, dneed="all", tneed=0):
        ds_t, dg_t, ps_t, pg_t = self._dneed(dneed)
        c_t = self.c
        def body(eng):
            self._w(eng, engname, c_t, ds_t, dg_t, ps_t, pg_t, tneed)
            fn(eng).then_inc(self.csem, 1)
        self._emit(engname, body)
        self.c += 1
        return self.c

    def top_group(self, fns, cneed=0, dneed=None):
        ds_t, dg_t, ps_t, pg_t = self._dneed(dneed)
        c_t = max(cneed, self.floor[0])
        n = len(fns)

        def body(eng):
            self._w(eng, "tensor", c_t, ds_t, dg_t, ps_t, pg_t, 0)
            for i, fn in enumerate(fns):
                ins = fn(eng)
                if i == n - 1:
                    ins.then_inc(self.tsem, 1)
        self._emit("tensor", body)
        self.t += 1
        return self.t

    def ops(self, engname, fns):
        for fn in fns:
            self.op(engname, fn)

    def mm_group(self, fns, dneed="all"):
        n = len(fns)
        ds_t, dg_t, ps_t, pg_t = self._dneed(dneed)
        c_t = self.c

        def body(eng):
            self._w(eng, "tensor", c_t, ds_t, dg_t, ps_t, pg_t)
            for i, fn in enumerate(fns):
                ins = fn(eng)
                if i == n - 1:
                    ins.then_inc(self.csem, 1)
        self._emit("tensor", body)
        self.c += 1

    def cc(self, kind, op, ins, outs):
        from concourse.aot_env import allow_ici_all_to_all
        ds_t, dg_t, ps_t, pg_t = self._dneed("all")
        c_t = self.c

        t_t = self.t

        def body(eng):
            self._w(eng, "gpsimd", c_t, ds_t, dg_t, self.p["sync"], self.p["gpsimd"], t_t)
            with allow_ici_all_to_all():
                eng.collective_compute(
                    kind, op, replica_groups=[list(range(NCORES))],
                    ins=[a.opt() for a in ins], outs=[a.opt() for a in outs],
                ).then_inc(self.csem, 1)
        self.flush()
        self.block.gpsimd(body)
        self.c += 1

    def finish(self, engname="sync"):
        self.flush()
        def body(eng):
            eng.wait_ge(self.dsem["sync"], self.d["sync"])
            eng.wait_ge(self.dsem["gpsimd"], self.d["gpsimd"])
            if self.p["sync"]:
                eng.wait_ge(self.pfsem["sync"], self.p["sync"])
            if self.p["gpsimd"]:
                eng.wait_ge(self.pfsem["gpsimd"], self.p["gpsimd"])
            if self.t:
                eng.wait_ge(self.tsem, self.t)
            eng.wait_ge(self.csem, self.c)
        getattr(self.block, engname)(body)


def _ceil(a, b):
    return (a + b - 1) // b



def build_A():
    nc = bass.Bass("TRN2", target_bir_lowering=False)
    R1 = FF0 // 8
    R3 = D // 8
    RG = FF1 // 8
    RD = D // 8

    w1s = nc.dram_tensor("w1s", [R1, D], F32, kind="ExternalInput").ap()
    w2s = nc.dram_tensor("w2s", [R1, D], F32, kind="ExternalInput").ap()
    w3s = nc.dram_tensor("w3s", [R3, FF0], F32, kind="ExternalInput").ap()
    gws = nc.dram_tensor("gws", [RG, D], I32, kind="ExternalInput").ap()
    uws = nc.dram_tensor("uws", [RG, D], I32, kind="ExternalInput").ap()
    dws = nc.dram_tensor("dws", [RD, FF1], I32, kind="ExternalInput").ap()
    gsc = nc.dram_tensor("gsc", [RG], F32, kind="ExternalInput").ap()
    usc = nc.dram_tensor("usc", [RG], F32, kind="ExternalInput").ap()
    dsc = nc.dram_tensor("dsc", [RD], F32, kind="ExternalInput").ap()

    gsum = nc.dram_tensor("gsum", [1, 4], F32, kind="ExternalOutput").ap()
    gwq = nc.dram_tensor("gwq", [RG, D], BF16, kind="ExternalOutput").ap()
    uwq = nc.dram_tensor("uwq", [RG, D], BF16, kind="ExternalOutput").ap()
    dwq = nc.dram_tensor("dwq", [RD, FF1], BF16, kind="ExternalOutput").ap()

    with ExitStack() as st:
        sb = lambda n, s, t: st.enter_context(nc.sbuf_tensor(n, s, t))
        ps = lambda n, s, t: st.enter_context(nc.psum_tensor(n, s, t))
        s_w = sb("s_w", [128, FF1], F32)
        s_col = sb("s_col", [128, 1], F32)
        s_ones = sb("s_ones", [128, 1], F32)
        s_red = sb("s_red", [128, 1], F32)
        s_q = sb("s_q", [128, FF1], BF16)
        s_gs = sb("s_gs", [1, 4], F32)
        p_gs = ps("p_gs", [1, 4], F32)
        block = st.enter_context(nc.Block())
        S = Sched(nc, block, st)

        S.op("gpsimd", lambda g: g.memset(s_ones[:], 1.0))
        S.op("gpsimd", lambda g: g.memset(s_gs[:], 0.0))

        def abssum(src, rows, cols, k):
            nt = _ceil(rows, 128)
            for t in range(nt):
                pc = min(128, rows - t * 128)
                S.dma("sync", s_w[0:pc, 0:cols], src[t * 128:t * 128 + pc, :])
                S.op("vector", lambda v, pc=pc, cols=cols: v.tensor_reduce(
                    s_red[0:pc, :], s_w[0:pc, 0:cols], axis=AX.X, op=ALU.add,
                    apply_absolute_value=True))
                S.op("tensor", lambda t_, pc=pc: t_.matmul(
                    p_gs[0:1, 0:1], s_red[0:pc, :], s_ones[0:pc, :],
                    start=True, stop=True))
                S.op("vector", lambda v, k=k: v.tensor_tensor(
                    s_gs[0:1, k:k + 1], s_gs[0:1, k:k + 1], p_gs[0:1, 0:1], ALU.add))

        abssum(w1s, R1, D, 0)
        abssum(w2s, R1, D, 1)
        abssum(w3s, R3, FF0, 2)
        S.dma("sync", gsum, s_gs[:])

        def dequant(srcw, srcs, dst, rows, cols):
            nt = _ceil(rows, 128)
            for t in range(nt):
                pc = min(128, rows - t * 128)
                S.dma("gpsimd", s_w[0:pc, 0:cols], srcw[t * 128:t * 128 + pc, :])
                S.dma("sync", s_col[0:pc, :], srcs[t * 128:t * 128 + pc].unsqueeze(1))
                S.op("vector", lambda v, pc=pc, cols=cols: v.tensor_scalar(
                    s_q[0:pc, 0:cols], s_w[0:pc, 0:cols], -128.0, s_col[0:pc, 0:1],
                    ALU.add, ALU.mult))
                S.dma("sync", dst[t * 128:t * 128 + pc, :], s_q[0:pc, 0:cols])

        dequant(gws, gsc, gwq, RG, D)
        dequant(uws, usc, uwq, RG, D)
        dequant(dws, dsc, dwq, RD, FF1)

        S.finish()
    return nc



def build_B():
    nc = bass.Bass("TRN2", target_bir_lowering=False)
    R1 = FF0 // 8
    R3 = D // 8
    w1s = nc.dram_tensor("w1s", [R1, D], F32, kind="ExternalInput").ap()
    w2s = nc.dram_tensor("w2s", [R1, D], F32, kind="ExternalInput").ap()
    w3s = nc.dram_tensor("w3s", [R3, FF0], F32, kind="ExternalInput").ap()
    gsums = nc.dram_tensor("gsums", [8, 4], F32, kind="ExternalInput").ap()
    w1q = nc.dram_tensor("w1q", [R1, D], BF16, kind="ExternalOutput").ap()
    w2q = nc.dram_tensor("w2q", [R1, D], BF16, kind="ExternalOutput").ap()
    w3q = nc.dram_tensor("w3q", [R3, FF0], BF16, kind="ExternalOutput").ap()

    nelem = [FF0 * D, FF0 * D, D * FF0]

    with ExitStack() as st:
        sb = lambda n, s, t: st.enter_context(nc.sbuf_tensor(n, s, t))
        ps = lambda n, s, t: st.enter_context(nc.psum_tensor(n, s, t))
        s_gp = sb("s_gp", [8, 4], F32)
        s_ones8 = sb("s_ones8", [8, 1], F32)
        s_ones1 = sb("s_ones1", [1, 128], F32)
        s_thr_row = sb("s_thr_row", [1, 4], F32)
        s_thr = sb("s_thr", [128, 4], F32)
        s_nthr = sb("s_nthr", [128, 4], F32)
        s_w = sb("s_w", [128, FF0], F32)
        s_a = sb("s_a", [128, FF0], F32)
        s_b = sb("s_b", [128, FF0], F32)
        s_q = sb("s_q", [128, FF0], BF16)
        p_sum = ps("p_sum", [1, 4], F32)
        p_bc = ps("p_bc", [128, 4], F32)
        block = st.enter_context(nc.Block())
        S = Sched(nc, block, st)

        S.op("gpsimd", lambda g: g.memset(s_ones8[:], 1.0))
        S.op("gpsimd", lambda g: g.memset(s_ones1[:], 1.0))
        S.dma("sync", s_gp[:], gsums)
        S.op("tensor", lambda t: t.matmul(p_sum[:], s_ones8[:], s_gp[:], start=True, stop=True))
        S.op("vector", lambda v: v.tensor_copy(s_thr_row[:], p_sum[:]))
        for k in range(3):
            S.op("vector", lambda v, k=k: v.tensor_scalar(
                s_thr_row[0:1, k:k + 1], s_thr_row[0:1, k:k + 1],
                0.5 / nelem[k], 0.5e-8, ALU.mult, ALU.add))
        S.op("tensor", lambda t: t.matmul(p_bc[:], s_ones1[:], s_thr_row[:], start=True, stop=True))
        S.op("vector", lambda v: v.tensor_copy(s_thr[:], p_bc[:]))
        S.op("vector", lambda v: v.tensor_scalar_mul(s_nthr[:], s_thr[:], -1.0))

        def quant(src, dst, rows, cols, k):
            nt = _ceil(rows, 128)
            for t in range(nt):
                pc = min(128, rows - t * 128)
                S.dma("sync", s_w[0:pc, 0:cols], src[t * 128:t * 128 + pc, :])
                S.op("vector", lambda v, pc=pc, cols=cols, k=k: v.tensor_scalar(
                    s_a[0:pc, 0:cols], s_w[0:pc, 0:cols], s_thr[0:pc, k:k + 1], None, ALU.is_ge))
                S.op("vector", lambda v, pc=pc, cols=cols, k=k: v.tensor_scalar(
                    s_b[0:pc, 0:cols], s_w[0:pc, 0:cols], s_nthr[0:pc, k:k + 1], None, ALU.is_le))
                S.op("vector", lambda v, pc=pc, cols=cols: v.tensor_tensor(
                    s_q[0:pc, 0:cols], s_a[0:pc, 0:cols], s_b[0:pc, 0:cols], ALU.subtract))
                S.dma("sync", dst[t * 128:t * 128 + pc, :], s_q[0:pc, 0:cols])

        quant(w1s, w1q, R1, D, 0)
        quant(w2s, w2q, R1, D, 1)
        quant(w3s, w3q, R3, FF0, 2)
        S.finish()
    return nc



def build_C(debug=False):
    nc = bass.Bass("TRN2", target_bir_lowering=False)
    x_s = nc.dram_tensor("x_s", [TOK, D], F32, kind="ExternalInput").ap()
    wqT = nc.dram_tensor("wqT", [D, D], F32, kind="ExternalInput").ap()
    wkT = nc.dram_tensor("wkT", [D, D], F32, kind="ExternalInput").ap()
    wvT = nc.dram_tensor("wvT", [D, D], F32, kind="ExternalInput").ap()
    w1qT = nc.dram_tensor("w1qT", [D, FF0], BF16, kind="ExternalInput").ap()
    w2qT = nc.dram_tensor("w2qT", [D, FF0], BF16, kind="ExternalInput").ap()
    w3qT = nc.dram_tensor("w3qT", [FF0, D], BF16, kind="ExternalInput").ap()
    triu = nc.dram_tensor("triu", [128, 128], BF16, kind="ExternalInput").ap()
    ident = nc.dram_tensor("ident", [128, 128], BF16, kind="ExternalInput").ap()

    sq_s = nc.dram_tensor("sq_s", [TOK, D], BF16, kind="ExternalOutput").ap()
    num_s = nc.dram_tensor("num_s", [TOK, D], F32, kind="ExternalOutput").ap()
    den_s = nc.dram_tensor("den_s", [TOK, NH], F32, kind="ExternalOutput").ap()
    kvz_s = nc.dram_tensor("kvz_s", [NH, HD, 129], F32, kind="ExternalOutput").ap()
    qsum_s = nc.dram_tensor("qsum_s", [1, D], F32, kind="ExternalOutput").ap()
    g_s = nc.dram_tensor("g_s", [TOK, D], F32, kind="ExternalOutput").ap()
    dbg_q = nc.dram_tensor("dbg_q", [TOK, 512], F32, kind="ExternalOutput").ap() if debug else None
    dbg_xnT = nc.dram_tensor("dbg_xnT", [128, 16 * TOK], BF16, kind="ExternalOutput").ap() if debug else None

    with ExitStack() as st:
        sb = lambda n, s, t: st.enter_context(nc.sbuf_tensor(n, s, t))
        s_xt = sb("s_xt", [128, D], F32)
        s_sqr = sb("s_sqr", [128, D], F32)
        s_mean = sb("s_mean", [128, 1], F32)
        s_sumsq = sb("s_sumsq", [128, 1], F32)
        s_var = sb("s_var", [128, 1], F32)
        s_m2 = sb("s_m2", [128, 1], F32)
        s_rstd = sb("s_rstd", [128, 1], F32)
        s_xnt = sb("s_xnt", [128, D], BF16)
        s_xnT = sb("s_xnT", [128, 16 * TOK], BF16)
        s_id = sb("s_id", [128, 128], BF16)
        s_triu = sb("s_triu", [128, 128], BF16)
        s_ones = sb("s_ones", [128, 1], F32)

        block = st.enter_context(nc.Block())
        S = Sched(nc, block, st)

        S.op("gpsimd", lambda g: g.memset(s_ones[:], 1.0))
        S.dma("sync", s_id[:], ident)
        S.dma("sync", s_triu[:], triu)

        with ExitStack() as ps1:
            sbl = lambda n, s, t: ps1.enter_context(nc.sbuf_tensor(n, s, t))
            s_wq = sbl("s_wq", [128, 16 * 512], BF16)
            s_wk = sbl("s_wk", [128, 16 * 512], BF16)
            s_wv = sbl("s_wv", [128, 16 * 512], BF16)
            s_sqb = sbl("s_sqb", [128, NCHUNK * 512], BF16)
            s_skb = sbl("s_skb", [128, NCHUNK * 512], BF16)
            s_vb = sbl("s_vb", [128, 4 * NCHUNK * 132], BF16)
            s_t1 = sbl("s_t1", [128, 512], F32)
            s_t2 = sbl("s_t2", [128, 512], F32)
            s_qf = sbl("s_qf", [128, 512], F32)
            s_state = sbl("s_state", [128, 4 * 132], F32)
            s_spb = sbl("s_spb", [128, 132], BF16)
            s_qpT = sbl("s_qpT", [128, 128], BF16)
            s_kpT = sbl("s_kpT", [128, 128], BF16)
            s_ms = sbl("s_ms", [128, 128], BF16)
            s_numst = sbl("s_numst", [128, 512], F32)
            s_denst = sbl("s_denst", [128, NCHUNK * NH], F32)
            s_qsumst = sbl("s_qsumst", [1, D], F32)
            psc = lambda n, s, t: ps1.enter_context(nc.psum_tensor(n, s, t))
            pS = psc("pS", [1, 512], F32)
            pA = psc("pA", [128, 512], F32)
            pB = psc("pB", [128, 512], F32)
            pC = psc("pC", [128, 512], F32)
            pD = psc("pD", [128, 129], F32)
            pE = psc("pE", [128, 129], F32)
            pF = psc("pF", [128, 128], F32)
            pT = psc("pT", [128, 128], BF16)

            for tt in range(NCHUNK):
                S.dma("sync", s_xt[:], x_s[tt * 128:(tt + 1) * 128, :])
                S.op("vector", lambda v: v.reduce_sum(s_mean[:], s_xt[:], axis=AX.X))
                S.op("vector", lambda v: v.tensor_scalar_mul(s_mean[:], s_mean[:], 1.0 / D))
                S.op("scalar", lambda sc: sc.activation(s_sqr[:], s_xt[:], AF.Square, accum_out=s_sumsq[:]))
                S.op("vector", lambda v: v.tensor_scalar_mul(s_var[:], s_sumsq[:], 1.0 / D))
                S.op("vector", lambda v: v.scalar_tensor_tensor(s_m2[:], s_mean[:], -1.0, s_mean[:], ALU.mult, ALU.mult))
                S.op("vector", lambda v: v.tensor_tensor(s_var[:], s_var[:], s_m2[:], ALU.add))
                S.op("vector", lambda v: v.tensor_scalar_add(s_var[:], s_var[:], 1e-5))
                S.op("scalar", lambda sc: sc.activation(s_rstd[:], s_var[:], AF.Ln))
                S.op("scalar", lambda sc: sc.activation(s_rstd[:], s_rstd[:], AF.Exp, scale=-0.5))
                S.op("vector", lambda v: v.tensor_scalar(s_xnt[:], s_xt[:], s_mean[:, 0:1], s_rstd[:, 0:1], ALU.subtract, ALU.mult))
                for dc in range(16):
                    S.op("tensor", lambda t, dc=dc: t.transpose(pT[:], s_xnt[:, dc * 128:(dc + 1) * 128], s_id[:]))
                    S.op("vector", lambda v, dc=dc, tt=tt: v.tensor_copy(
                        s_xnT[:, dc * TOK + tt * 128: dc * TOK + (tt + 1) * 128], pT[:]))

            for blk in range(4):
                cols = slice(blk * 512, (blk + 1) * 512)
                for (wsrc, wdst) in ((wqT, s_wq), (wkT, s_wk), (wvT, s_wv)):
                    S.dma("gpsimd",
                          wdst[:].rearrange("p (dc e) -> p dc e", e=512),
                          wsrc[:, cols].rearrange("(dc p) e -> p dc e", p=128))
                S.op("gpsimd", lambda g: g.memset(s_vb[:], 0.0))
                S.op("gpsimd", lambda g: g.memset(
                    s_vb[:].rearrange("p (t c) -> p t c", c=132)[:, :, 128:129], 1.0))
                for tt in range(NCHUNK):
                    xn_t = lambda dc, tt=tt: s_xnT[:, dc * TOK + tt * 128: dc * TOK + (tt + 1) * 128]
                    for (wdst, pdst) in ((s_wq, pA), (s_wk, pB), (s_wv, pC)):
                        for dc in range(16):
                            S.op("tensor", lambda t, dc=dc, wdst=wdst, pdst=pdst, tt=tt: t.matmul(
                                pdst[:], xn_t(dc), wdst[:, dc * 512:(dc + 1) * 512],
                                start=(dc == 0), stop=(dc == 15)))
                    S.op("scalar", lambda sc: sc.copy(s_qf[:], pA[:]))
                    S.op("tensor", lambda t, tt=tt: t.matmul(
                        pS[:], s_ones[:], s_qf[:], start=(tt == 0), stop=(tt == NCHUNK - 1)))
                    if debug and blk == 0:
                        S.dma("sync", dbg_q[tt * 128:(tt + 1) * 128, :], s_qf[:])
                    for (psrc, dstbuf) in ((pA, s_sqb), (pB, s_skb)):
                        S.op("vector", lambda v, psrc=psrc: v.tensor_scalar_min(s_t1[:], psrc[:], 0.0))
                        S.op("scalar", lambda sc: sc.activation(s_t2[:], s_t1[:], AF.Exp))
                        S.op("vector", lambda v, psrc=psrc: v.tensor_scalar_max(s_t1[:], psrc[:], 0.0))
                        S.op("vector", lambda v, dstbuf=dstbuf, tt=tt: v.tensor_tensor(
                            dstbuf[:, tt * 512:(tt + 1) * 512], s_t2[:], s_t1[:], ALU.add))
                    for h in range(4):
                        S.op("vector", lambda v, h=h, tt=tt: v.tensor_copy(
                            s_vb[:, (h * NCHUNK + tt) * 132:(h * NCHUNK + tt) * 132 + 128],
                            pC[:, h * 128:(h + 1) * 128]))
                    S.dma("sync", sq_s[tt * 128:(tt + 1) * 128, cols], s_sqb[:, tt * 512:(tt + 1) * 512])
                S.op("vector", lambda v, blk=blk: v.tensor_copy(s_qsumst[0:1, blk * 512:(blk + 1) * 512], pS[:]))
                S.op("gpsimd", lambda g: g.memset(s_state[:], 0.0))
                for n in range(NCHUNK):
                    for h in range(4):
                        qp = s_sqb[:, n * 512 + h * 128: n * 512 + (h + 1) * 128]
                        kp = s_skb[:, n * 512 + h * 128: n * 512 + (h + 1) * 128]
                        vt = s_vb[:, (h * NCHUNK + n) * 132:(h * NCHUNK + n) * 132 + 129]
                        stt = s_state[:, h * 132:h * 132 + 129]
                        S.op("vector", lambda v, stt=stt: v.tensor_copy(s_spb[:, 0:129], stt))
                        S.op("tensor", lambda t, qp=qp: t.transpose(pT[:], qp, s_id[:]))
                        S.op("vector", lambda v: v.tensor_copy(s_qpT[:], pT[:]))
                        S.op("tensor", lambda t, kp=kp: t.transpose(pT[:], kp, s_id[:]))
                        S.op("vector", lambda v: v.tensor_copy(s_kpT[:], pT[:]))
                        S.op("tensor", lambda t, kp=kp, vt=vt: t.matmul(pD[:], kp, vt, start=True, stop=True))
                        S.op("vector", lambda v, stt=stt: v.tensor_tensor(stt, stt, pD[:], ALU.add))
                        S.op("tensor", lambda t: t.matmul(pF[:], s_kpT[:], s_qpT[:], start=True, stop=True))
                        S.op("vector", lambda v: v.tensor_tensor(s_ms[:], pF[:], s_triu[:], ALU.mult))
                        S.op("tensor", lambda t: t.matmul(pE[:], s_qpT[:], s_spb[:, 0:129], start=True, stop=False))
                        S.op("tensor", lambda t, vt=vt: t.matmul(pE[:], s_ms[:], vt, start=False, stop=True))
                        S.op("vector", lambda v, h=h: v.tensor_copy(
                            s_numst[:, h * 128:(h + 1) * 128], pE[:, 0:128]))
                        S.op("vector", lambda v, n=n, h=h, blk=blk: v.tensor_copy(
                            s_denst[:, n * NH + blk * 4 + h: n * NH + blk * 4 + h + 1], pE[:, 128:129]))
                    S.dma("sync", num_s[n * 128:(n + 1) * 128, cols], s_numst[:])
                for h in range(4):
                    S.dma("sync", kvz_s[blk * 4 + h], s_state[:, h * 132:h * 132 + 129])
            if debug:
                S.dma("sync", dbg_xnT, s_xnT[:])
            S.dma("sync", qsum_s, s_qsumst[:])
            S.dma("sync",
                  den_s.rearrange("(n t) h -> t n h", t=128),
                  s_denst[:].rearrange("p (n h) -> p n h", h=NH))

        with ExitStack() as ps2:
            sbl = lambda n, s, t: ps2.enter_context(nc.sbuf_tensor(n, s, t))
            s_w1b = sbl("s_w1b", [128, 16 * 512], BF16)
            s_w2b = sbl("s_w2b", [128, 16 * 512], BF16)
            s_hT = sbl("s_hT", [128, 44 * 512], BF16)
            s_w3r = sbl("s_w3r", [128, D], BF16)
            s_sg = sbl("s_sg", [128, 512], F32)
            s_gst = sbl("s_gst", [128, D], F32)
            psc = lambda n, s, t: ps2.enter_context(nc.psum_tensor(n, s, t))
            pf = [psc(f"pf{i}", [128, 512], F32) for i in range(8)]
            for half in range(2):
                hoff = half * 512
                for hb in range(11):
                    for (wsrc, wdst) in ((w1qT, s_w1b), (w2qT, s_w2b)):
                        S.dma("sync",
                              wdst[:].rearrange("p (dc e) -> p dc e", e=512),
                              wsrc[:, hb * 512:(hb + 1) * 512].rearrange("(dc p) e -> p dc e", p=128))
                    for hc in range(4):
                        for (wdst, pdst) in ((s_w1b, pf[0]), (s_w2b, pf[1])):
                            for dc in range(16):
                                S.op("tensor", lambda t, dc=dc, wdst=wdst, pdst=pdst, hc=hc: t.matmul(
                                    pdst[:], wdst[:, dc * 512 + hc * 128: dc * 512 + (hc + 1) * 128],
                                    s_xnT[:, dc * TOK + hoff: dc * TOK + hoff + 512],
                                    start=(dc == 0), stop=(dc == 15)))
                        S.op("scalar", lambda sc: sc.activation(s_sg[:], pf[0][:], AF.Sigmoid))
                        S.op("vector", lambda v, hb=hb, hc=hc: v.tensor_tensor(
                            s_hT[:, (hb * 4 + hc) * 512:(hb * 4 + hc + 1) * 512],
                            s_sg[:], pf[1][:], ALU.mult))
                for tp in range(2):
                    for hc44 in range(44):
                        S.dma("sync", s_w3r[:], w3qT[hc44 * 128:(hc44 + 1) * 128, :])
                        for t2 in range(2):
                            for eb in range(4):
                                S.op("tensor", lambda t, hc44=hc44, t2=t2, eb=eb, tp=tp: t.matmul(
                                    pf[t2 * 4 + eb][:],
                                    s_hT[:, hc44 * 512 + (tp * 2 + t2) * 128: hc44 * 512 + (tp * 2 + t2 + 1) * 128],
                                    s_w3r[:, eb * 512:(eb + 1) * 512],
                                    start=(hc44 == 0), stop=(hc44 == 43)))
                    for t2 in range(2):
                        for eb in range(4):
                            S.op("scalar", lambda sc, t2=t2, eb=eb: sc.activation(
                                s_gst[:, eb * 512:(eb + 1) * 512], pf[t2 * 4 + eb][:], AF.Silu))
                        row0 = (half * 4 + tp * 2 + t2) * 128
                        S.dma("sync", g_s[row0:row0 + 128, :], s_gst[:])

        S.finish()
    return nc


def run_C(inputs, w1q, w2q, w3q):
    nc = build_C()
    x = np.ascontiguousarray(inputs["L0_attn_hidden_states"].reshape(SEQ * BATCH, D))
    wqT = np.ascontiguousarray(inputs["L0_attn_w_q"].T)
    wkT = np.ascontiguousarray(inputs["L0_attn_w_k"].T)
    wvT = np.ascontiguousarray(inputs["L0_attn_w_v"].T)
    w1qT = np.ascontiguousarray(w1q.T)
    w2qT = np.ascontiguousarray(w2q.T)
    w3qT = np.ascontiguousarray(w3q.T)
    triu = np.triu(np.ones((128, 128), BF))
    ident = np.eye(128, dtype=BF)
    in_maps = []
    for r in range(NCORES):
        in_maps.append({
            "x_s": np.ascontiguousarray(x[r * TOK:(r + 1) * TOK]),
            "wqT": wqT, "wkT": wkT, "wvT": wvT,
            "w1qT": w1qT, "w2qT": w2qT, "w3qT": w3qT,
            "triu": triu, "ident": ident,
        })
    res = run_bass_kernel_spmd(nc, in_maps, core_ids=list(range(NCORES)))
    rs = res.results
    return {k: np.stack([rs[r][k] for r in range(NCORES)], axis=0)
            for k in ("sq_s", "num_s", "den_s", "kvz_s", "qsum_s", "g_s")}



def build_D():
    nc = bass.Bass("TRN2", target_bir_lowering=False)
    x_s = nc.dram_tensor("x_s", [TOK, D], F32, kind="ExternalInput").ap()
    sq_s = nc.dram_tensor("sq_s", [TOK, D], BF16, kind="ExternalInput").ap()
    num_s = nc.dram_tensor("num_s", [TOK, D], F32, kind="ExternalInput").ap()
    den_s = nc.dram_tensor("den_s", [TOK, NH], F32, kind="ExternalInput").ap()
    pkvz = nc.dram_tensor("pkvz", [3, NH, HD, 129], F32, kind="ExternalInput").ap()
    qsum4 = nc.dram_tensor("qsum4", [4, D], F32, kind="ExternalInput").ap()
    memh = nc.dram_tensor("memh", [NH, HD, 512], F32, kind="ExternalInput").ap()
    mnT = nc.dram_tensor("mnT", [NH, HD, 4], F32, kind="ExternalInput").ap()
    mnm = nc.dram_tensor("mnm", [4, D], F32, kind="ExternalInput").ap()
    gate_c = nc.dram_tensor("gate_c", [NH, 1], F32, kind="ExternalInput").ap()
    gate_r = nc.dram_tensor("gate_r", [1, NH], F32, kind="ExternalInput").ap()
    w_oT = nc.dram_tensor("w_oT", [D, D], F32, kind="ExternalInput").ap()
    g_s = nc.dram_tensor("g_s", [TOK, D], F32, kind="ExternalInput").ap()
    ident = nc.dram_tensor("ident", [128, 128], BF16, kind="ExternalInput").ap()
    x1_s = nc.dram_tensor("x1_s", [TOK, D], F32, kind="ExternalOutput").ap()

    scr_qbar = nc.dram_tensor("scr_qbar", [1, D], F32, kind="Internal").ap()
    scr4 = nc.dram_tensor("scr4", [4, 1], F32, kind="Internal").ap()
    scrwg = nc.dram_tensor("scrwg", [NH, 4], F32, kind="Internal").ap()

    with ExitStack() as st:
        sb = lambda n, s, t: st.enter_context(nc.sbuf_tensor(n, s, t))
        psc = lambda n, s, t: st.enter_context(nc.psum_tensor(n, s, t))
        p_am = psc("p_am", [128, 512], F32)
        p_row = psc("p_row", [1, 512], F32)
        pO = psc("pO", [128, 512], F32)
        pE = psc("pE", [128, 129], F32)
        p_nrm = psc("p_nrm", [128, 4], F32)
        pT = psc("pT", [128, 128], BF16)
        p_b16 = psc("p_b16", [128, 4], F32)
        p_b128 = psc("p_b128", [128, NH], F32)

        s_id = sb("s_id", [128, 128], BF16)
        s_ones1 = sb("s_ones1", [1, 128], F32)
        s_ones4 = sb("s_ones4", [4, 1], F32)
        s_omg = sb("s_omg", [128, NH], F32)
        s_wb = sb("s_wb", [128, NH * 4], F32)
        s_attn = sb("s_attn", [128, NCHUNK * D], BF16)

        S = None
        block = st.enter_context(nc.Block())
        S = Sched(nc, block, st)
        S.dma("sync", s_id[:], ident)
        S.op("gpsimd", lambda g: g.memset(s_ones1[:], 1.0))
        S.op("gpsimd", lambda g: g.memset(s_ones4[:], 1.0))

        with ExitStack() as ph1:
            sbl = lambda n, s, t: ph1.enter_context(nc.sbuf_tensor(n, s, t))
            s_sq = sbl("s_sq", [128, NCHUNK * D], BF16)
            s_mem = sbl("s_mem", [128, NH * 512], BF16)
            s_mnT = sbl("s_mnT", [128, NH * 4], BF16)
            s_sbf = sbl("s_sbf", [128, NH * 132], F32)
            s_sbb = sbl("s_sbb", [128, NH * 132], BF16)
            s_t129 = sbl("s_t129", [128, 129], F32)
            s_q4 = sbl("s_q4", [4, D], F32)
            s_qbar_row = sbl("s_qbar_row", [1, D], F32)
            s_qbar16 = sbl("s_qbar16", [NH, 128], F32)
            s_mnm16 = sbl("s_mnm16", [NH, 128], F32)
            s_tmp16 = sbl("s_tmp16", [NH, 128], F32)
            s_rel = sbl("s_rel", [NH, 4], F32)
            s_mnm4 = sbl("s_mnm4", [4, D], F32)
            s_mns = sbl("s_mns", [4, 1], F32)
            s_mrow = sbl("s_mrow", [1, 4], F32)
            s_e16 = sbl("s_e16", [NH, 4], F32)
            s_ke16 = sbl("s_ke16", [NH, 4], F32)
            s_mx = sbl("s_mx", [NH, 1], F32)
            s_sum = sbl("s_sum", [NH, 1], F32)
            s_gc = sbl("s_gc", [NH, 1], F32)
            s_gr = sbl("s_gr", [1, NH], F32)
            s_wgrow = sbl("s_wgrow", [1, 4], F32)
            s_numt = sbl("s_numt", [128, 128], F32)
            s_dent = sbl("s_dent", [128, NH], F32)
            s_numtot = sbl("s_numtot", [128, 128], F32)
            s_dtot = sbl("s_dtot", [128, 1], F32)
            s_acc = sbl("s_acc", [128, 128], F32)
            s_cn = sbl("s_cn", [128, 4], F32)
            s_qpT = sbl("s_qpT", [128, 128], BF16)

            S.dma("gpsimd", s_mem[:].rearrange("p (h e) -> p h e", e=512),
                  memh.rearrange("h d e -> d h e"))
            S.dma("gpsimd", s_mnT[:].rearrange("p (h m) -> p h m", m=4),
                  mnT.rearrange("h d m -> d h m"))
            S.dma("sync", s_sq[:].rearrange("p (n e) -> p n e", e=D),
                  sq_s.rearrange("(n p) e -> p n e", p=128))
            for h in range(NH):
                sl = s_sbf[:, h * 132:h * 132 + 129]
                S.dma("sync", s_t129[:], pkvz[0, h])
                S.op("vector", lambda v, sl=sl: v.tensor_copy(sl, s_t129[:]))
                for j in (1, 2):
                    S.dma("sync", s_t129[:], pkvz[j, h])
                    S.op("vector", lambda v, sl=sl: v.tensor_tensor(sl, sl, s_t129[:], ALU.add))
                S.op("vector", lambda v, h=h: v.tensor_copy(
                    s_sbb[:, h * 132:h * 132 + 129], s_sbf[:, h * 132:h * 132 + 129]))
            S.dma("sync", s_q4[:], qsum4)
            for c in range(4):
                S.op("tensor", lambda t, c=c: t.matmul(
                    p_row[:], s_ones4[:], s_q4[:, c * 512:(c + 1) * 512], start=True, stop=True))
                S.op("vector", lambda v, c=c: v.tensor_scalar_mul(
                    s_qbar_row[0:1, c * 512:(c + 1) * 512], p_row[:], 1.0 / SEQ))
            S.dma("sync", scr_qbar, s_qbar_row[:])
            S.dma("sync", s_qbar16[:], scr_qbar.rearrange("o (h d) -> (o h) d", d=128))
            for m in range(4):
                S.dma("sync", s_mnm16[:], mnm[m:m + 1, :].rearrange("o (h d) -> (o h) d", d=128))
                S.op("vector", lambda v: v.tensor_tensor(s_tmp16[:], s_qbar16[:], s_mnm16[:], ALU.mult))
                S.op("vector", lambda v, m=m: v.reduce_sum(s_rel[:, m:m + 1], s_tmp16[:], axis=AX.X))
            S.op("vector", lambda v: v.tensor_scalar_mul(s_rel[:], s_rel[:], HD ** -0.5))
            S.dma("sync", s_mnm4[:], mnm)
            S.op("vector", lambda v: v.reduce_sum(s_mns[:], s_mnm4[:], axis=AX.X))
            S.dma("sync", scr4, s_mns[:])
            S.dma("sync", s_mrow[:], scr4.rearrange("a b -> b a"))
            S.op("vector", lambda v: v.tensor_single_scalar(s_mrow[:], s_mrow[:], 1e-6, ALU.is_lt))
            S.op("tensor", lambda t: t.matmul(p_b16[0:NH, 0:4], s_ones1[0:1, 0:NH], s_mrow[:], start=True, stop=True))
            S.op("vector", lambda v: v.tensor_copy(s_e16[:], p_b16[0:NH, 0:4]))
            S.op("vector", lambda v: v.tensor_scalar(s_ke16[:], s_e16[:], -1.0, 1.0, ALU.mult, ALU.add))
            S.op("vector", lambda v: v.tensor_tensor(s_rel[:], s_rel[:], s_ke16[:], ALU.mult))
            S.op("vector", lambda v: v.scalar_tensor_tensor(s_rel[:], s_e16[:], -1e30, s_rel[:], ALU.mult, ALU.add))
            S.op("vector", lambda v: v.reduce_max(s_mx[:], s_rel[:], axis=AX.X))
            S.op("vector", lambda v: v.tensor_scalar(s_rel[:], s_rel[:], s_mx[:, 0:1], None, ALU.subtract))
            S.op("scalar", lambda sc: sc.activation(s_rel[:], s_rel[:], AF.Exp))
            S.op("vector", lambda v: v.reduce_sum(s_sum[:], s_rel[:], axis=AX.X))
            S.op("vector", lambda v: v.reciprocal(s_sum[:], s_sum[:]))
            S.op("vector", lambda v: v.tensor_scalar(s_rel[:], s_rel[:], s_sum[:, 0:1], None, ALU.mult))
            S.op("vector", lambda v: v.tensor_tensor(s_rel[:], s_rel[:], s_ke16[:], ALU.mult))
            S.dma("sync", s_gc[:], gate_c)
            S.op("scalar", lambda sc: sc.activation(s_gc[:], s_gc[:], AF.Sigmoid))
            S.op("vector", lambda v: v.tensor_scalar(s_rel[:], s_rel[:], s_gc[:, 0:1], None, ALU.mult))
            S.dma("sync", s_gr[:], gate_r)
            S.op("scalar", lambda sc: sc.activation(s_gr[:], s_gr[:], AF.Sigmoid))
            S.op("vector", lambda v: v.tensor_scalar(s_gr[:], s_gr[:], -1.0, 1.0, ALU.mult, ALU.add))
            S.op("tensor", lambda t: t.matmul(p_b128[:], s_ones1[:], s_gr[:], start=True, stop=True))
            S.op("vector", lambda v: v.tensor_copy(s_omg[:], p_b128[:]))
            S.dma("sync", scrwg, s_rel[:])
            for h in range(NH):
                S.dma("sync", s_wgrow[:], scrwg[h:h + 1, :])
                S.op("tensor", lambda t: t.matmul(p_b16[0:128, 0:4], s_ones1[:], s_wgrow[:], start=True, stop=True))
                S.op("vector", lambda v, h=h: v.tensor_copy(s_wb[:, h * 4:(h + 1) * 4], p_b16[0:128, 0:4]))

            for n in range(NCHUNK):
                S.dma("sync", s_dent[:], den_s[n * 128:(n + 1) * 128, :])
                for h in range(NH):
                    sqt = s_sq[:, n * D + h * 128: n * D + (h + 1) * 128]
                    S.op("tensor", lambda t, sqt=sqt: t.transpose(pT[:], sqt, s_id[:]))
                    S.op("vector", lambda v: v.tensor_copy(s_qpT[:], pT[:]))
                    S.op("tensor", lambda t, h=h: t.matmul(
                        pE[:], s_qpT[:], s_sbb[:, h * 132:h * 132 + 129], start=True, stop=True))
                    S.dma("sync", s_numt[:], num_s[n * 128:(n + 1) * 128, h * 128:(h + 1) * 128])
                    S.op("vector", lambda v: v.tensor_tensor(s_numtot[:], s_numt[:], pE[:, 0:128], ALU.add))
                    S.op("vector", lambda v, h=h: v.tensor_tensor(
                        s_dtot[:], s_dent[:, h:h + 1], pE[:, 128:129], ALU.add))
                    S.op("vector", lambda v: v.tensor_scalar_max(s_dtot[:], s_dtot[:], 1e-6))
                    S.op("vector", lambda v: v.reciprocal(s_dtot[:], s_dtot[:]))
                    S.op("vector", lambda v, h=h: v.tensor_tensor(s_dtot[:], s_dtot[:], s_omg[:, h:h + 1], ALU.mult))
                    S.op("vector", lambda v: v.tensor_scalar(s_acc[:], s_numtot[:], s_dtot[:, 0:1], None, ALU.mult))
                    S.op("tensor", lambda t, h=h: t.matmul(
                        p_nrm[:], s_qpT[:], s_mnT[:, h * 4:(h + 1) * 4], start=True, stop=True))
                    S.op("tensor", lambda t, h=h: t.matmul(
                        p_am[:], s_qpT[:], s_mem[:, h * 512:(h + 1) * 512], start=True, stop=True))
                    S.op("vector", lambda v: v.tensor_scalar_max(s_cn[:], p_nrm[:], 1e-6))
                    S.op("vector", lambda v: v.reciprocal(s_cn[:], s_cn[:]))
                    S.op("vector", lambda v, h=h: v.tensor_tensor(s_cn[:], s_cn[:], s_wb[:, h * 4:(h + 1) * 4], ALU.mult))
                    for m in range(4):
                        S.op("vector", lambda v, m=m: v.scalar_tensor_tensor(
                            s_acc[:], p_am[:, m * 128:(m + 1) * 128], s_cn[:, m:m + 1],
                            s_acc[:], ALU.mult, ALU.add))
                    S.op("vector", lambda v, n=n, h=h: v.tensor_copy(
                        s_attn[:, n * D + h * 128: n * D + (h + 1) * 128], s_acc[:]))

        with ExitStack() as ph2:
            sbl = lambda n, s, t: ph2.enter_context(nc.sbuf_tensor(n, s, t))
            s_wo = sbl("s_wo", [128, 16 * D], BF16)
            s_aT = sbl("s_aT", [128, 16 * 128], BF16)
            s_xt = sbl("s_xt", [128, D], F32)
            s_gt = sbl("s_gt", [128, D], F32)
            s_tmp = sbl("s_tmp", [128, 512], F32)
            s_out = sbl("s_out", [128, D], F32)
            S.dma("gpsimd", s_wo[:].rearrange("p (ec d) -> p ec d", d=D),
                  w_oT.rearrange("(ec p) d -> p ec d", p=128))
            for tt in range(NCHUNK):
                for ec in range(16):
                    S.op("tensor", lambda t, tt=tt, ec=ec: t.transpose(
                        pT[:], s_attn[:, tt * D + ec * 128: tt * D + (ec + 1) * 128], s_id[:]))
                    S.op("vector", lambda v, ec=ec: v.tensor_copy(s_aT[:, ec * 128:(ec + 1) * 128], pT[:]))
                S.dma("sync", s_xt[:], x_s[tt * 128:(tt + 1) * 128, :])
                S.dma("sync", s_gt[:], g_s[tt * 128:(tt + 1) * 128, :])
                for eb in range(4):
                    for ec in range(16):
                        S.op("tensor", lambda t, ec=ec, eb=eb: t.matmul(
                            pO[:], s_aT[:, ec * 128:(ec + 1) * 128],
                            s_wo[:, ec * D + eb * 512: ec * D + eb * 512 + 512],
                            start=(ec == 0), stop=(ec == 15)))
                    S.op("vector", lambda v, eb=eb: v.tensor_tensor(
                        s_tmp[:], pO[:], s_gt[:, eb * 512:(eb + 1) * 512], ALU.mult))
                    S.op("vector", lambda v, eb=eb: v.tensor_tensor(
                        s_out[:, eb * 512:(eb + 1) * 512], s_tmp[:], s_xt[:, eb * 512:(eb + 1) * 512], ALU.add))
                S.dma("sync", x1_s[tt * 128:(tt + 1) * 128, :], s_out[:])

        S.finish()
    return nc


def run_D(inputs, Cout):
    nc = build_D()
    x = np.ascontiguousarray(inputs["L0_attn_hidden_states"].reshape(SEQ * BATCH, D))
    memories = inputs["L0_attn_memories"]
    memh = np.ascontiguousarray(memories.transpose(1, 2, 0, 3).reshape(NH, HD, 512))
    mnT = np.ascontiguousarray(inputs["L0_attn_memory_norms"].transpose(1, 2, 0))
    mnm = np.ascontiguousarray(inputs["L0_attn_memory_norms"].reshape(4, D))
    gate_c = np.ascontiguousarray(inputs["L0_attn_gate"].reshape(NH, 1))
    gate_r = np.ascontiguousarray(inputs["L0_attn_gate"].reshape(1, NH))
    w_oT = np.ascontiguousarray(inputs["L0_attn_w_o"].T)
    ident = np.eye(128, dtype=BF)
    in_maps = []
    for r in range(NCORES):
        b = r // 4
        pk = np.zeros((3, NH, HD, 129), np.float32)
        for j, rp in enumerate(range(4 * b, r)):
            pk[j] = Cout["kvz_s"][rp]
        q4 = np.ascontiguousarray(Cout["qsum_s"][4 * b:4 * b + 4, 0])
        in_maps.append({
            "x_s": np.ascontiguousarray(x[r * TOK:(r + 1) * TOK]),
            "sq_s": Cout["sq_s"][r], "num_s": Cout["num_s"][r],
            "den_s": Cout["den_s"][r], "pkvz": pk, "qsum4": q4,
            "memh": memh, "mnT": mnT, "mnm": mnm,
            "gate_c": gate_c, "gate_r": gate_r, "w_oT": w_oT,
            "g_s": Cout["g_s"][r], "ident": ident,
        })
    res = run_bass_kernel_spmd(nc, in_maps, core_ids=list(range(NCORES)))
    rs = res.results
    return np.stack([rs[r]["x1_s"] for r in range(NCORES)], axis=0)



def build_E():
    nc = bass.Bass("TRN2", target_bir_lowering=False)
    x1_s = nc.dram_tensor("x1_s", [TOK, D], F32, kind="ExternalInput").ap()
    W_attn = nc.dram_tensor("W_attn", [D, 3 * D], F32, kind="ExternalInput").ap()
    b_attn = nc.dram_tensor("b_attn", [1, 3 * D], F32, kind="ExternalInput").ap()
    gwqT = nc.dram_tensor("gwqT", [D, FF1], BF16, kind="ExternalInput").ap()
    upqT = nc.dram_tensor("upqT", [D, FF1], BF16, kind="ExternalInput").ap()
    dwqe = nc.dram_tensor("dwqe", [16, 128, FF1], BF16, kind="ExternalInput").ap()
    ident = nc.dram_tensor("ident", [128, 128], BF16, kind="ExternalInput").ap()

    q_s = nc.dram_tensor("q_s", [TOK, D], BF16, kind="ExternalOutput").ap()
    k_s = nc.dram_tensor("k_s", [TOK, D], BF16, kind="ExternalOutput").ap()
    v_s = nc.dram_tensor("v_s", [TOK, D], BF16, kind="ExternalOutput").ap()
    g1T_s = nc.dram_tensor("g1T_s", [D, TOK], F32, kind="ExternalOutput").ap()

    with ExitStack() as st:
        sb = lambda n, s, t: st.enter_context(nc.sbuf_tensor(n, s, t))
        psc = lambda n, s, t: st.enter_context(nc.psum_tensor(n, s, t))
        pQ = psc("pQ", [128, 512], F32)
        pG = psc("pG", [128, 512], F32)
        pU = psc("pU", [128, 512], F32)
        pDn = psc("pDn", [128, 512], F32)
        pT = psc("pT", [128, 128], BF16)

        s_xt = sb("s_xt", [128, D], F32)
        s_sqr = sb("s_sqr", [128, D], F32)
        s_mean = sb("s_mean", [128, 1], F32)
        s_sumsq = sb("s_sumsq", [128, 1], F32)
        s_var = sb("s_var", [128, 1], F32)
        s_m2 = sb("s_m2", [128, 1], F32)
        s_rstd = sb("s_rstd", [128, 1], F32)
        s_xnt = sb("s_xnt", [128, D], BF16)
        s_xnT = sb("s_xnT", [128, 16 * TOK], BF16)
        s_id = sb("s_id", [128, 128], BF16)
        s_ones1 = sb("s_ones1", [1, 128], BF16)
        s_ba = sb("s_ba", [1, 3 * D], BF16)

        block = st.enter_context(nc.Block())
        S = Sched(nc, block, st)
        S.dma("sync", s_id[:], ident)
        S.op("gpsimd", lambda g: g.memset(s_ones1[:], 1.0))
        S.dma("gpsimd", s_ba[:], b_attn)

        for tt in range(NCHUNK):
            S.dma("sync", s_xt[:], x1_s[tt * 128:(tt + 1) * 128, :])
            S.op("vector", lambda v: v.reduce_sum(s_mean[:], s_xt[:], axis=AX.X))
            S.op("vector", lambda v: v.tensor_scalar_mul(s_mean[:], s_mean[:], 1.0 / D))
            S.op("scalar", lambda sc: sc.activation(s_sqr[:], s_xt[:], AF.Square, accum_out=s_sumsq[:]))
            S.op("vector", lambda v: v.tensor_scalar_mul(s_var[:], s_sumsq[:], 1.0 / D))
            S.op("vector", lambda v: v.scalar_tensor_tensor(s_m2[:], s_mean[:], -1.0, s_mean[:], ALU.mult, ALU.mult))
            S.op("vector", lambda v: v.tensor_tensor(s_var[:], s_var[:], s_m2[:], ALU.add))
            S.op("vector", lambda v: v.tensor_scalar_add(s_var[:], s_var[:], 1e-5))
            S.op("scalar", lambda sc: sc.activation(s_rstd[:], s_var[:], AF.Ln))
            S.op("scalar", lambda sc: sc.activation(s_rstd[:], s_rstd[:], AF.Exp, scale=-0.5))
            S.op("vector", lambda v: v.tensor_scalar(s_xnt[:], s_xt[:], s_mean[:, 0:1], s_rstd[:, 0:1], ALU.subtract, ALU.mult))
            for dc in range(16):
                S.op("tensor", lambda t, dc=dc: t.transpose(pT[:], s_xnt[:, dc * 128:(dc + 1) * 128], s_id[:]))
                S.op("vector", lambda v, dc=dc, tt=tt: v.tensor_copy(
                    s_xnT[:, dc * TOK + tt * 128: dc * TOK + (tt + 1) * 128], pT[:]))

        with ExitStack() as ph1:
            sbl = lambda n, s, t: ph1.enter_context(nc.sbuf_tensor(n, s, t))
            s_wb = sbl("s_wb", [128, 16 * 512], BF16)
            s_qkv = sbl("s_qkv", [128, 512], BF16)
            outs = [q_s, q_s, q_s, q_s, k_s, k_s, k_s, k_s, v_s, v_s, v_s, v_s]
            for ob in range(12):
                S.dma("gpsimd",
                      s_wb[:].rearrange("p (dc e) -> p dc e", e=512),
                      W_attn[:, ob * 512:(ob + 1) * 512].rearrange("(dc p) e -> p dc e", p=128))
                for tt in range(NCHUNK):
                    for dc in range(16):
                        S.op("tensor", lambda t, dc=dc, tt=tt: t.matmul(
                            pQ[:], s_xnT[:, dc * TOK + tt * 128: dc * TOK + (tt + 1) * 128],
                            s_wb[:, dc * 512:(dc + 1) * 512], start=(dc == 0), stop=False))
                    S.op("tensor", lambda t, ob=ob: t.matmul(
                        pQ[:], s_ones1[:], s_ba[:, ob * 512:(ob + 1) * 512], start=False, stop=True))
                    S.op("vector", lambda v: v.tensor_copy(s_qkv[:], pQ[:]))
                    S.dma("sync", outs[ob][tt * 128:(tt + 1) * 128, (ob % 4) * 512:(ob % 4) * 512 + 512], s_qkv[:])

        with ExitStack() as ph2:
            sbl = lambda n, s, t: ph2.enter_context(nc.sbuf_tensor(n, s, t))
            s_gw = sbl("s_gw", [128, 16 * 512], BF16)
            s_uw = sbl("s_uw", [128, 16 * 512], BF16)
            s_hT = sbl("s_hT", [128, 64 * 512], BF16)
            s_dw = sbl("s_dw", [128, FF1], BF16)
            s_sg = sbl("s_sg", [128, 512], F32)
            s_g1 = sbl("s_g1", [128, 512], F32)
            for half in range(2):
                hoff = half * 512
                for hb in range(16):
                    for (wsrc, wdst) in ((gwqT, s_gw), (upqT, s_uw)):
                        S.dma("sync",
                              wdst[:].rearrange("p (dc e) -> p dc e", e=512),
                              wsrc[:, hb * 512:(hb + 1) * 512].rearrange("(dc p) e -> p dc e", p=128))
                    for hc in range(4):
                        for (wdst, pdst) in ((s_gw, pG), (s_uw, pU)):
                            for dc in range(16):
                                S.op("tensor", lambda t, dc=dc, wdst=wdst, pdst=pdst, hc=hc: t.matmul(
                                    pdst[:], wdst[:, dc * 512 + hc * 128: dc * 512 + (hc + 1) * 128],
                                    s_xnT[:, dc * TOK + hoff: dc * TOK + hoff + 512],
                                    start=(dc == 0), stop=(dc == 15)))
                        S.op("scalar", lambda sc: sc.activation(s_sg[:], pG[:], AF.Silu))
                        S.op("vector", lambda v, hb=hb, hc=hc: v.tensor_tensor(
                            s_hT[:, (hb * 4 + hc) * 512:(hb * 4 + hc + 1) * 512],
                            s_sg[:], pU[:], ALU.mult))
                for ec in range(16):
                    S.dma("sync", s_dw[:], dwqe[ec])
                    for ic in range(64):
                        S.op("tensor", lambda t, ic=ic: t.matmul(
                            pDn[:], s_dw[:, ic * 128:(ic + 1) * 128],
                            s_hT[:, ic * 512:(ic + 1) * 512],
                            start=(ic == 0), stop=(ic == 63)))
                    S.op("scalar", lambda sc: sc.activation(s_g1[:], pDn[:], AF.Silu))
                    S.dma("sync", g1T_s[ec * 128:(ec + 1) * 128, hoff:hoff + 512], s_g1[:])

        S.finish()
    return nc


def run_E(inputs, x1, gwq, uwq, dwq):
    nc = build_E()
    b_attn = np.ascontiguousarray(inputs["L1_attn_b_attn"].reshape(1, 3 * D))
    gwqT = np.ascontiguousarray(gwq.T)
    upqT = np.ascontiguousarray(uwq.T)
    dwqT = dwq.T
    dwqe = np.ascontiguousarray(
        dwqT.reshape(64, 128, 16, 128).transpose(2, 1, 0, 3).reshape(16, 128, FF1))
    ident = np.eye(128, dtype=BF)
    in_maps = []
    for r in range(NCORES):
        in_maps.append({
            "x1_s": np.ascontiguousarray(x1[r]),
            "W_attn": inputs["L1_attn_W_attn"], "b_attn": b_attn,
            "gwqT": gwqT, "upqT": upqT, "dwqe": dwqe, "ident": ident,
        })
    res = run_bass_kernel_spmd(nc, in_maps, core_ids=list(range(NCORES)))
    rs = res.results
    return {k: np.stack([rs[r][k] for r in range(NCORES)], axis=0)
            for k in ("q_s", "k_s", "v_s", "g1T_s")}



def build_F():
    nc = bass.Bass("TRN2", target_bir_lowering=False)
    Qf = nc.dram_tensor("Qf", [4, HD, SEQ], BF16, kind="ExternalInput").ap()
    Kf = nc.dram_tensor("Kf", [4, HD, SEQ], BF16, kind="ExternalInput").ap()
    Vt = nc.dram_tensor("Vt", [4, SEQ, HD], BF16, kind="ExternalInput").ap()
    dm = nc.dram_tensor("dm", [4, 128, 512], BF16, kind="ExternalInput").ap()
    identf = nc.dram_tensor("identf", [128, 128], F32, kind="ExternalInput").ap()
    y_s = nc.dram_tensor("y_s", [4, SEQ, HD], BF16, kind="ExternalOutput").ap()

    SC = HD ** -0.5

    with ExitStack() as st:
        sb = lambda n, s, t: st.enter_context(nc.sbuf_tensor(n, s, t))
        psc = lambda n, s, t: st.enter_context(nc.psum_tensor(n, s, t))
        pS1 = psc("pS1", [128, 512], F32)
        pST = psc("pST", [128, 512], F32)
        pY = [psc(f"pY{i}", [128, 512], F32) for i in range(4)]
        pTm = psc("pTm", [1, 128], F32)

        s_q = sb("s_q", [128, SEQ], BF16)
        s_k = sb("s_k", [128, SEQ], BF16)
        s_v = sb("s_v", [128, 32 * 132], BF16)
        s_dm = sb("s_dm", [128, 4 * 512], BF16)
        s_idf = sb("s_idf", [128, 128], F32)
        s_ones1 = sb("s_ones1", [1, 128], BF16)
        s_nm = sb("s_nm", [128, 4], F32)
        s_mt = sb("s_mt", [128, 1], F32)
        s_negm_row = sb("s_negm_row", [1, 512], F32)
        s_negm_bf = sb("s_negm_bf", [1, 512], BF16)
        s_pt = sb("s_pt", [128, 512], BF16)
        s_l = sb("s_l", [128, 1], F32)
        s_yst = sb("s_yst", [128, 512], BF16)

        block = st.enter_context(nc.Block())
        S = Sched(nc, block, st)
        S.dma("sync", s_idf[:], identf)
        S.dma("sync", s_dm[:].rearrange("p (r w) -> p r w", w=512), dm.rearrange("r p w -> p r w"))
        S.op("gpsimd", lambda g: g.memset(s_ones1[:], 1.0))

        for pr in range(4):
            S.dma("sync", s_q[:], Qf[pr])
            S.dma("sync", s_k[:], Kf[pr])
            S.op("gpsimd", lambda g: g.memset(s_v[:], 0.0))
            S.op("gpsimd", lambda g: g.memset(
                s_v[:].rearrange("p (c w) -> p c w", w=132)[:, :, 128:129], 1.0))
            S.dma("sync",
                  s_v[:].rearrange("p (c w) -> p c w", w=132)[:, :, 0:128],
                  Vt[pr].rearrange("(c p) e -> p c e", p=128))
            for ig in range(8):
                S.op("gpsimd", lambda g: g.memset(s_nm[:], 0.0))
                for j in range(4):
                    qt = ig * 4 + j
                    for bb in range(qt // 4):
                        S.op("tensor", lambda t, qt=qt, bb=bb: t.matmul(
                            pS1[:], s_q[:, qt * 128:(qt + 1) * 128],
                            s_k[:, bb * 512:(bb + 1) * 512], start=True, stop=True))
                        S.op("vector", lambda v: v.reduce_max(s_mt[:], pS1[:], axis=AX.X, negate=True))
                        S.op("vector", lambda v, j=j: v.tensor_tensor(
                            s_nm[:, j:j + 1], s_nm[:, j:j + 1], s_mt[:], ALU.min))
                for j in range(4):
                    S.op("tensor", lambda t, j=j: t.transpose(pTm[:], s_nm[:, j:j + 1], s_idf[:]))
                    S.op("vector", lambda v, j=j: v.tensor_copy(
                        s_negm_row[0:1, j * 128:(j + 1) * 128], pTm[:]))
                S.op("vector", lambda v: v.tensor_copy(s_negm_bf[:], s_negm_row[:]))
                for c in range(4 * ig + 4):
                    S.op("tensor", lambda t, c=c, ig=ig: t.matmul(
                        pST[:], s_k[:, c * 128:(c + 1) * 128],
                        s_q[:, ig * 512:(ig + 1) * 512], start=True, stop=False))
                    S.op("tensor", lambda t: t.matmul(
                        pST[:], s_ones1[:], s_negm_bf[:], start=False, stop=True))
                    S.op("scalar", lambda sc: sc.activation(s_pt[:], pST[:], AF.Exp, scale=SC))
                    if c >= 4 * ig:
                        rel = c - 4 * ig
                        S.op("vector", lambda v, rel=rel: v.tensor_tensor(
                            s_pt[:], s_pt[:], s_dm[:, rel * 512:(rel + 1) * 512], ALU.mult))
                    for sub in range(4):
                        if 4 * ig + sub >= c:
                            S.op("tensor", lambda t, sub=sub, c=c, ig=ig: t.matmul(
                                pY[sub][0:128, 0:129], s_pt[:, sub * 128:(sub + 1) * 128],
                                s_v[:, c * 132:c * 132 + 129],
                                start=(c == 0), stop=(c == 4 * ig + sub)))
                for sub in range(4):
                    S.op("vector", lambda v, sub=sub: v.reciprocal(s_l[:], pY[sub][:, 128:129]))
                    S.op("vector", lambda v, sub=sub: v.tensor_scalar(
                        s_yst[:, sub * 128:(sub + 1) * 128], pY[sub][:, 0:128],
                        s_l[:, 0:1], None, ALU.mult))
                S.dma("sync",
                      y_s[pr, ig * 512:(ig + 1) * 512, :].rearrange("(s q) e -> q s e", q=128),
                      s_yst[:].rearrange("p (s e) -> p s e", e=128))
        S.finish()
    return nc


def run_F(inputs, Eout):
    nc = build_F()
    q = np.ascontiguousarray(Eout["q_s"].reshape(BATCH, SEQ, NH, HD))
    k = np.ascontiguousarray(Eout["k_s"].reshape(BATCH, SEQ, NH, HD))
    v = np.ascontiguousarray(Eout["v_s"].reshape(BATCH, SEQ, NH, HD))
    dmm = np.zeros((4, 128, 512), BF)
    for r in range(4):
        for tk in range(128):
            dmm[r, tk, r * 128 + tk:] = 1.0
    identf = np.eye(128, dtype=np.float32)
    in_maps = []
    for c in range(NCORES):
        qf = np.zeros((4, HD, SEQ), BF)
        kf = np.zeros((4, HD, SEQ), BF)
        vt = np.zeros((4, SEQ, HD), BF)
        for i in range(4):
            pair = 4 * c + i
            b, h = pair // NH, pair % NH
            qf[i] = q[b, :, h, :].T
            kf[i] = k[b, :, h, :].T
            vt[i] = v[b, :, h, :]
        in_maps.append({"Qf": qf, "Kf": kf, "Vt": vt, "dm": dmm, "identf": identf})
    res = run_bass_kernel_spmd(nc, in_maps, core_ids=list(range(NCORES)))
    rs = res.results
    return np.stack([rs[r]["y_s"] for r in range(NCORES)], axis=0)



def build_G():
    nc = bass.Bass("TRN2", target_bir_lowering=False)
    x1_s = nc.dram_tensor("x1_s", [TOK, D], F32, kind="ExternalInput").ap()
    yT_s = nc.dram_tensor("yT_s", [D, TOK], BF16, kind="ExternalInput").ap()
    W_proj = nc.dram_tensor("W_proj", [D, D], F32, kind="ExternalInput").ap()
    b_proj = nc.dram_tensor("b_proj", [1, D], F32, kind="ExternalInput").ap()
    g1_s = nc.dram_tensor("g1_s", [TOK, D], F32, kind="ExternalInput").ap()
    out_s = nc.dram_tensor("out_s", [TOK, D], F32, kind="ExternalOutput").ap()

    with ExitStack() as st:
        sb = lambda n, s, t: st.enter_context(nc.sbuf_tensor(n, s, t))
        pG = st.enter_context(nc.psum_tensor("pG", [128, 512], F32))
        s_yT = sb("s_yT", [128, 16 * TOK], BF16)
        s_wp = sb("s_wp", [128, 16 * D], BF16)
        s_bp = sb("s_bp", [1, D], BF16)
        s_ones1 = sb("s_ones1", [1, 128], BF16)
        s_xt = sb("s_xt", [128, D], F32)
        s_gt = sb("s_gt", [128, D], F32)
        s_tmp = sb("s_tmp", [128, 512], F32)
        s_out = sb("s_out", [128, D], F32)
        block = st.enter_context(nc.Block())
        S = Sched(nc, block, st)
        S.op("gpsimd", lambda g: g.memset(s_ones1[:], 1.0))
        S.dma("gpsimd", s_bp[:], b_proj)
        S.dma("sync", s_yT[:].rearrange("p (cc t) -> p cc t", t=TOK),
              yT_s.rearrange("(cc p) t -> p cc t", p=128))
        S.dma("gpsimd", s_wp[:].rearrange("p (cc d) -> p cc d", d=D),
              W_proj.rearrange("(cc p) d -> p cc d", p=128))
        for tt in range(NCHUNK):
            S.dma("sync", s_xt[:], x1_s[tt * 128:(tt + 1) * 128, :])
            S.dma("sync", s_gt[:], g1_s[tt * 128:(tt + 1) * 128, :])
            for eb in range(4):
                for cc in range(16):
                    S.op("tensor", lambda t, cc=cc, eb=eb, tt=tt: t.matmul(
                        pG[:], s_yT[:, cc * TOK + tt * 128: cc * TOK + (tt + 1) * 128],
                        s_wp[:, cc * D + eb * 512: cc * D + eb * 512 + 512],
                        start=(cc == 0), stop=False))
                S.op("tensor", lambda t, eb=eb: t.matmul(
                    pG[:], s_ones1[:], s_bp[:, eb * 512:(eb + 1) * 512], start=False, stop=True))
                S.op("vector", lambda v, eb=eb: v.tensor_tensor(
                    s_tmp[:], pG[:], s_gt[:, eb * 512:(eb + 1) * 512], ALU.mult))
                S.op("vector", lambda v, eb=eb: v.tensor_tensor(
                    s_out[:, eb * 512:(eb + 1) * 512], s_tmp[:], s_xt[:, eb * 512:(eb + 1) * 512], ALU.add))
            S.dma("sync", out_s[tt * 128:(tt + 1) * 128, :], s_out[:])
        S.finish()
    return nc


def run_G(inputs, x1, y, g1T):
    nc = build_G()
    yy = y.reshape(BATCH, NH, SEQ, HD).transpose(0, 2, 1, 3).reshape(SEQ * BATCH, D)
    g1 = np.concatenate([g1T[r].T for r in range(NCORES)], axis=0)
    b_proj = np.ascontiguousarray(inputs["L1_attn_b_proj"].reshape(1, D))
    in_maps = []
    for r in range(NCORES):
        in_maps.append({
            "x1_s": np.ascontiguousarray(x1[r]),
            "yT_s": np.ascontiguousarray(yy[r * TOK:(r + 1) * TOK].T),
            "W_proj": inputs["L1_attn_W_proj"], "b_proj": b_proj,
            "g1_s": np.ascontiguousarray(g1[r * TOK:(r + 1) * TOK]),
        })
    res = run_bass_kernel_spmd(nc, in_maps, core_ids=list(range(NCORES)))
    rs = res.results
    return np.concatenate([rs[r]["out_s"] for r in range(NCORES)], axis=0)



def build_mk(parts="L0L1", dbg=False):
    nc = bass.Bass("TRN2", target_bir_lowering=False)
    R1 = FF0 // 8
    R3 = D // 8
    has0 = "L0" in parts
    has1 = "L1" in parts
    IK = "ExternalOutput" if dbg else "Internal"

    if has0:
        x_s = nc.dram_tensor("x_s", [TOK, D], F32, kind="ExternalInput").ap()
        w1s = nc.dram_tensor("w1s", [R1, D], F32, kind="ExternalInput").ap()
        w2s = nc.dram_tensor("w2s", [R1, D], F32, kind="ExternalInput").ap()
        w3s = nc.dram_tensor("w3s", [R3, FF0], F32, kind="ExternalInput").ap()
        wq_h = nc.dram_tensor("wq_h", [4, 128, 16, 512], F32, kind="ExternalInput").ap()
        wk_h = nc.dram_tensor("wk_h", [4, 128, 16, 512], F32, kind="ExternalInput").ap()
        wv_h = nc.dram_tensor("wv_h", [4, 128, 16, 512], F32, kind="ExternalInput").ap()
        wo_h = nc.dram_tensor("wo_h", [128, 16, D], F32, kind="ExternalInput").ap()
        memh2 = nc.dram_tensor("memh2", [HD, NH, 512], F32, kind="ExternalInput").ap()
        mnT2 = nc.dram_tensor("mnT2", [HD, NH, 4], F32, kind="ExternalInput").ap()
        mnm = nc.dram_tensor("mnm", [4, D], F32, kind="ExternalInput").ap()
        gate_c = nc.dram_tensor("gate_c", [NH, 1], F32, kind="ExternalInput").ap()
        gate_r = nc.dram_tensor("gate_r", [1, NH], F32, kind="ExternalInput").ap()
        triu = nc.dram_tensor("triu", [128, 128], BF16, kind="ExternalInput").ap()
        pmask = nc.dram_tensor("pmask", [1, 8], F32, kind="ExternalInput").ap()
        bmask = nc.dram_tensor("bmask", [8, 1], F32, kind="ExternalInput").ap()
        x1_s = nc.dram_tensor("x1_s", [TOK, D], F32, kind="ExternalOutput").ap()
    else:
        x1_s = nc.dram_tensor("x1_s", [TOK, D], F32, kind="ExternalInput").ap()
    if has1:
        wa_h = nc.dram_tensor("wa_h", [12, 128, 16, 512], F32, kind="ExternalInput").ap()
        b_attn = nc.dram_tensor("b_attn", [1, 3 * D], F32, kind="ExternalInput").ap()
        gw_h = nc.dram_tensor("gw_h", [32, 128, 16, 256], I32, kind="ExternalInput").ap()
        up_h = nc.dram_tensor("up_h", [32, 128, 16, 256], I32, kind="ExternalInput").ap()
        dwe32 = nc.dram_tensor("dwe32", [16, 128, FF1], I32, kind="ExternalInput").ap()
        gsc2 = nc.dram_tensor("gsc2", [64, 128], F32, kind="ExternalInput").ap()
        usc2 = nc.dram_tensor("usc2", [64, 128], F32, kind="ExternalInput").ap()
        dsc2 = nc.dram_tensor("dsc2", [16, 128], F32, kind="ExternalInput").ap()
        wp_h = nc.dram_tensor("wp_h", [128, 16, D], F32, kind="ExternalInput").ap()
        b_proj = nc.dram_tensor("b_proj", [1, D], F32, kind="ExternalInput").ap()
        dm = nc.dram_tensor("dm", [4, 128, 512], BF16, kind="ExternalInput").ap()
        identf = nc.dram_tensor("identf", [128, 128], F32, kind="ExternalInput").ap()
        out_s = nc.dram_tensor("out_s", [TOK, D], F32, kind="ExternalOutput").ap()
    ident = nc.dram_tensor("ident", [128, 128], BF16, kind="ExternalInput").ap()

    if has0:
        ag_g_in = nc.dram_tensor("ag_g_in", [1, 4], F32, kind="Internal").ap()
        ag_g_out = nc.dram_tensor("ag_g_out", [8, 4], F32, kind="Internal").ap()
        ag_w12_in = nc.dram_tensor("ag_w12_in", [2 * D, R1], BF16, kind="Internal").ap()
        ag_w12_out = nc.dram_tensor("ag_w12_out", [8 * 2 * D, R1], BF16, kind="Internal").ap()
        ag_w3_in = nc.dram_tensor("ag_w3_in", [FF0, R3], BF16, kind="Internal").ap()
        ag_w3_out = nc.dram_tensor("ag_w3_out", [8 * FF0, R3], BF16, kind="Internal").ap()
        ag_kvz_in = nc.dram_tensor("ag_kvz_in", [NH * HD, 129], F32, kind="Internal").ap()
        ag_kvz_out = nc.dram_tensor("ag_kvz_out", [8 * NH * HD, 129], F32, kind="Internal").ap()
        ag_qs_in = nc.dram_tensor("ag_qs_in", [1, D], F32, kind="Internal").ap()
        ag_qs_out = nc.dram_tensor("ag_qs_out", [8, D], F32, kind="Internal").ap()
        sq_d = nc.dram_tensor("sq_d", [TOK, D], BF16, kind=IK).ap()
        num_d = nc.dram_tensor("num_d", [TOK, D], F32, kind=IK).ap()
        den_d = nc.dram_tensor("den_d", [TOK, NH], F32, kind=IK).ap()
        g_d = nc.dram_tensor("g_d", [TOK, D], F32, kind=IK).ap()
        scr_qbar = nc.dram_tensor("scr_qbar", [1, D], F32, kind="Internal").ap()
        scrwg = nc.dram_tensor("scrwg", [NH, 4], F32, kind="Internal").ap()
    if has1:
        qkvc = nc.dram_tensor("qkvc", [8 * 3 * 2 * TOK, HD], BF16, kind="Internal").ap()
        qkvg = nc.dram_tensor("qkvg", [8 * 3 * 2 * TOK, HD], BF16, kind="Internal").ap()
        yc = nc.dram_tensor("yc", [8 * 2 * TOK, HD], BF16, kind="Internal").ap()
        yg = nc.dram_tensor("yg", [8 * 2 * TOK, HD], BF16, kind="Internal").ap()
        g1_d = nc.dram_tensor("g1_d", [TOK, D], F32, kind=IK).ap()
        scr4m = nc.dram_tensor("scr4m", [4, 1], F32, kind="Internal").ap()
    else:
        scr4m = nc.dram_tensor("scr4m", [4, 1], F32, kind="Internal").ap()

    with ExitStack() as st:
        sb = lambda n, s, t: st.enter_context(nc.sbuf_tensor(n, s, t))
        s_id = sb("s_id", [128, 128], BF16)
        s_ones1b = sb("s_ones1b", [1, 128], BF16)
        s_ones1f = sb("s_ones1f", [1, 128], F32)
        s_onescf = sb("s_onescf", [128, 1], F32)
        s_onescb = sb("s_onescb", [128, 1], BF16)
        s_xnT = sb("s_xnT", [128, 16 * TOK], BF16)
        s_xt = sb("s_xt", [128, D], F32)
        s_sqr = sb("s_sqr", [128, D], F32)
        s_mean = sb("s_mean", [128, 1], F32)
        s_sumsq = sb("s_sumsq", [128, 1], F32)
        s_var = sb("s_var", [128, 1], F32)
        s_m2 = sb("s_m2", [128, 1], F32)
        s_rstd = sb("s_rstd", [128, 1], F32)
        s_xnt = sb("s_xnt", [128, D], BF16)

        block = st.enter_context(nc.Block())
        S = Sched(nc, block, st)
        S.dma("sync", s_id[:], ident)
        S.op("gpsimd", lambda g: g.memset(s_ones1b[:], 1.0))
        S.op("gpsimd", lambda g: g.memset(s_ones1f[:], 1.0))
        S.op("gpsimd", lambda g: g.memset(s_onescf[:], 1.0))
        S.op("gpsimd", lambda g: g.memset(s_onescb[:], 1.0))

        def emit_ln(src, pT):
            for tt in range(NCHUNK):
                S.dma("sync", s_xt[:], src[tt * 128:(tt + 1) * 128, :])
                S.op("vector", lambda v: v.reduce_sum(s_mean[:], s_xt[:], axis=AX.X))
                S.op("vector", lambda v: v.tensor_scalar_mul(s_mean[:], s_mean[:], 1.0 / D))
                S.op("scalar", lambda sc: sc.activation(s_sqr[:], s_xt[:], AF.Square, accum_out=s_sumsq[:]))
                S.op("vector", lambda v: v.tensor_scalar_mul(s_var[:], s_sumsq[:], 1.0 / D))
                S.op("vector", lambda v: v.scalar_tensor_tensor(s_m2[:], s_mean[:], -1.0, s_mean[:], ALU.mult, ALU.mult))
                S.op("vector", lambda v: v.tensor_tensor(s_var[:], s_var[:], s_m2[:], ALU.add))
                S.op("vector", lambda v: v.tensor_scalar_add(s_var[:], s_var[:], 1e-5))
                S.op("scalar", lambda sc: sc.activation(s_rstd[:], s_var[:], AF.Ln))
                S.op("scalar", lambda sc: sc.activation(s_rstd[:], s_rstd[:], AF.Exp, scale=-0.5))
                S.op("vector", lambda v: v.tensor_scalar(s_xnt[:], s_xt[:], s_mean[:, 0:1], s_rstd[:, 0:1], ALU.subtract, ALU.mult))
                for dc in range(16):
                    S.op("tensor", lambda t, dc=dc: t.transpose(pT[:], s_xnt[:, dc * 128:(dc + 1) * 128], s_id[:]))
                    S.op("vector", lambda v, dc=dc, tt=tt: v.tensor_copy(
                        s_xnT[:, dc * TOK + tt * 128: dc * TOK + (tt + 1) * 128], pT[:]))

        if has0:
            with ExitStack() as ph:
                S.fence()
                sbl = lambda n, s, t: ph.enter_context(nc.sbuf_tensor(n, s, t))
                psc = lambda n, s, t: ph.enter_context(nc.psum_tensor(n, s, t))
                s_w = sbl("s_w", [128, 2 * FF0], F32)
                s_red = sbl("s_red", [128, 1], F32)
                s_gs = sbl("s_gs", [1, 4], F32)
                s_gp = sbl("s_gp", [8, 4], F32)
                s_ones8 = sbl("s_ones8", [8, 1], F32)
                s_thr_row = sbl("s_thr_row", [1, 4], F32)
                s_thr = sbl("s_thr", [128, 4], F32)
                s_nthr = sbl("s_nthr", [128, 4], F32)
                s_a = sbl("s_a", [128, FF0], F32)
                s_b = sbl("s_b", [128, FF0], F32)
                s_q = sbl("s_q", [128, FF0], BF16)
                s_wc = sbl("s_wc", [128, 16 * R1], BF16)
                p_gs = psc("p_gs", [1, 4], F32)
                p_bc = psc("p_bc", [128, 4], F32)
                pT = psc("pTw", [128, 128], BF16)

                S.op("gpsimd", lambda g: g.memset(s_ones8[:], 1.0))
                S.op("gpsimd", lambda g: g.memset(s_gs[:], 0.0))
                nelem = [FF0 * D, FF0 * D, D * FF0]
                tiles = []
                for k, (src, rows, cols) in enumerate(((w1s, R1, D), (w2s, R1, D), (w3s, R3, FF0))):
                    for t in range(_ceil(rows, 128)):
                        pc = min(128, rows - t * 128)
                        tiles.append((k, src, t, pc, cols))
                wq_war = [0, 0]
                wq_tk = {}

                def load_tile(i):
                    k, src, t, pc, cols = tiles[i]
                    par = (i % 2) * FF0
                    return [S.dma("sync", s_w[0:pc, par:par + cols], src[t * 128:t * 128 + pc, :],
                                  war=wq_war[i % 2], detach=True)]

                wq_tk[0] = load_tile(0)
                for i, (k, src, t, pc, cols) in enumerate(tiles):
                    par = (i % 2) * FF0
                    if i + 1 < len(tiles):
                        wq_tk[i + 1] = load_tile(i + 1)
                    S.op("vector", lambda v, pc=pc, cols=cols, par=par: v.tensor_reduce(
                        s_red[0:pc, :], s_w[0:pc, par:par + cols], axis=AX.X, op=ALU.add,
                        apply_absolute_value=True), dneed=wq_tk[i])
                    wq_war[i % 2] = S.c
                    S.op("tensor", lambda t_, pc=pc: t_.matmul(
                        p_gs[0:1, 0:1], s_red[0:pc, :], s_onescf[0:pc, :], start=True, stop=True))
                    S.op("vector", lambda v, k=k: v.tensor_tensor(
                        s_gs[0:1, k:k + 1], s_gs[0:1, k:k + 1], p_gs[0:1, 0:1], ALU.add))
                S.dma("sync", ag_g_in, s_gs[:])
                S.cc("AllGather", ALU.bypass, [ag_g_in], [ag_g_out])
                S.dma("sync", s_gp[:], ag_g_out)
                S.op("tensor", lambda t: t.matmul(p_gs[:], s_ones8[:], s_gp[:], start=True, stop=True))
                S.op("vector", lambda v: v.tensor_copy(s_thr_row[:], p_gs[:]))
                for k in range(3):
                    S.op("vector", lambda v, k=k: v.tensor_scalar(
                        s_thr_row[0:1, k:k + 1], s_thr_row[0:1, k:k + 1],
                        0.5 / nelem[k], 0.5e-8, ALU.mult, ALU.add))
                S.op("tensor", lambda t: t.matmul(p_bc[:], s_ones1f[:], s_thr_row[:], start=True, stop=True))
                S.op("vector", lambda v: v.tensor_copy(s_thr[:], p_bc[:]))
                S.op("vector", lambda v: v.tensor_scalar_mul(s_nthr[:], s_thr[:], -1.0))

                qt_state = {"i": 0}

                def quant_t(src, rows, cols, k, dst, dstw):
                    for t in range(_ceil(rows, 128)):
                        pc = min(128, rows - t * 128)
                        i = qt_state["i"]
                        qt_state["i"] += 1
                        par = (i % 2) * FF0
                        tk = S.dma("sync", s_w[0:pc, par:par + cols], src[t * 128:t * 128 + pc, :],
                                   war=wq_war[i % 2], detach=True)
                        S.op("vector", lambda v, pc=pc, cols=cols, k=k, par=par: v.tensor_scalar(
                            s_a[0:pc, 0:cols], s_w[0:pc, par:par + cols], s_thr[0:pc, k:k + 1], None, ALU.is_ge),
                            dneed=[tk])
                        S.op("vector", lambda v, pc=pc, cols=cols, k=k, par=par: v.tensor_scalar(
                            s_b[0:pc, 0:cols], s_w[0:pc, par:par + cols], s_nthr[0:pc, k:k + 1], None, ALU.is_le))
                        wq_war[i % 2] = S.c
                        S.op("vector", lambda v, pc=pc, cols=cols: v.tensor_tensor(
                            s_q[0:pc, 0:cols], s_a[0:pc, 0:cols], s_b[0:pc, 0:cols], ALU.subtract))
                        for cc in range(cols // 128):
                            S.op("tensor", lambda t_, cc=cc, pc=pc: t_.transpose(
                                pT[0:128, 0:pc], s_q[0:pc, cc * 128:(cc + 1) * 128], s_id[0:pc, 0:pc]))
                            S.op("vector", lambda v, cc=cc, t=t, pc=pc: v.tensor_copy(
                                s_wc[:, cc * rows + t * 128: cc * rows + t * 128 + pc], pT[0:128, 0:pc]))
                    S.dma("sync",
                          dst.rearrange("(cc p) r -> p cc r", p=128),
                          s_wc[:, 0:(cols // 128) * rows].rearrange("p (cc r) -> p cc r", r=rows))

                quant_t(w1s, R1, D, 0, ag_w12_in[0:D, :], R1)
                quant_t(w2s, R1, D, 1, ag_w12_in[D:2 * D, :], R1)
                quant_t(w3s, R3, FF0, 2, ag_w3_in, R3)
                S.cc("AllGather", ALU.bypass, [ag_w12_in], [ag_w12_out])
                S.cc("AllGather", ALU.bypass, [ag_w3_in], [ag_w3_out])

            with ExitStack() as ph:
                S.fence()
                sbl = lambda n, s, t: ph.enter_context(nc.sbuf_tensor(n, s, t))
                psc = lambda n, s, t: ph.enter_context(nc.psum_tensor(n, s, t))
                pT = psc("pT0", [128, 128], BF16)
                pS = psc("pS0", [1, 512], F32)
                pA = psc("pA0", [128, 512], F32)
                pB = psc("pB0", [128, 512], F32)
                pC = psc("pC0", [128, 512], F32)
                pD = psc("pD0", [128, 129], F32)
                pE = psc("pE0", [128, 129], F32)
                pF = psc("pF0", [128, 128], F32)
                s_triu = sbl("s_triu", [128, 128], BF16)
                s_wq = sbl("s_wq", [128, 2 * 16 * 512], BF16)
                s_wk = sbl("s_wk", [128, 2 * 16 * 512], BF16)
                s_wv = sbl("s_wv", [128, 2 * 16 * 512], BF16)
                s_sqb = sbl("s_sqb", [128, NCHUNK * 512], BF16)
                s_skb = sbl("s_skb", [128, NCHUNK * 512], BF16)
                s_vb = sbl("s_vb", [128, 4 * NCHUNK * 132], BF16)
                s_t1 = sbl("s_t1", [128, 512], F32)
                s_t2 = sbl("s_t2", [128, 512], F32)
                s_qf = sbl("s_qf", [128, 512], F32)
                s_state = sbl("s_state", [128, 4 * 132], F32)
                s_spb = sbl("s_spb", [128, 132], BF16)
                s_qpT = sbl("s_qpT", [128, 128], BF16)
                s_kpT = sbl("s_kpT", [128, 128], BF16)
                s_ms = sbl("s_ms", [128, 128], BF16)
                s_numst = sbl("s_numst", [128, 512], F32)
                s_denst = sbl("s_denst", [128, NCHUNK * NH], F32)
                s_qsumst = sbl("s_qsumst", [1, D], F32)

                S.dma("sync", s_triu[:], triu)
                emit_ln(x_s, pT)
                BW = 16 * 512

                def ld_wblk(blk, war, twar=0):
                    par = (blk % 2) * BW
                    tks = []
                    for (wsrc, wdst) in ((wq_h, s_wq), (wk_h, s_wk), (wv_h, s_wv)):
                        tks.append(S.dma(
                            "gpsimd",
                            wdst[:, par:par + BW].rearrange("p (dc e) -> p dc e", e=512),
                            wsrc[blk],
                            war=war, twar=twar, detach=True))
                    return tks

                wt_war = [0, 0]
                wt_tw = [0, 0]
                wtk = [None] * 4
                wtk[0] = ld_wblk(0, 0)
                xn_ready = S.c
                q_rd, k_rd, v_rd, qs_rd = [xn_ready], [xn_ready], [xn_ready], [0]
                for blk in range(4):
                    par = (blk % 2) * BW
                    if blk + 1 < 4:
                        wtk[blk + 1] = ld_wblk(blk + 1, wt_war[(blk + 1) % 2], wt_tw[(blk + 1) % 2])
                    S.op("gpsimd", lambda g: g.memset(s_vb[:], 0.0), tneed=S.t)
                    S.op("gpsimd", lambda g: g.memset(
                        s_vb[:].rearrange("p (t c) -> p t c", c=132)[:, :, 128:129], 1.0))
                    tqs = 0
                    for tt in range(NCHUNK):
                        tq = S.top_group([
                            (lambda t, dc=dc, tt=tt, par=par: t.matmul(
                                pA[:], s_xnT[:, dc * TOK + tt * 128: dc * TOK + (tt + 1) * 128],
                                s_wq[:, par + dc * 512: par + (dc + 1) * 512],
                                start=(dc == 0), stop=(dc == 15)))
                            for dc in range(16)], cneed=q_rd[0], dneed=wtk[blk])
                        tk2 = S.top_group([
                            (lambda t, dc=dc, tt=tt, par=par: t.matmul(
                                pB[:], s_xnT[:, dc * TOK + tt * 128: dc * TOK + (tt + 1) * 128],
                                s_wk[:, par + dc * 512: par + (dc + 1) * 512],
                                start=(dc == 0), stop=(dc == 15)))
                            for dc in range(16)], cneed=k_rd[0], dneed=wtk[blk])
                        tv = S.top_group([
                            (lambda t, dc=dc, tt=tt, par=par: t.matmul(
                                pC[:], s_xnT[:, dc * TOK + tt * 128: dc * TOK + (tt + 1) * 128],
                                s_wv[:, par + dc * 512: par + (dc + 1) * 512],
                                start=(dc == 0), stop=(dc == 15)))
                            for dc in range(16)], cneed=v_rd[0], dneed=wtk[blk])
                        cp = S.op("scalar", lambda sc: sc.copy(s_qf[:], pA[:]), tneed=tq)
                        tqs = S.top_group([
                            (lambda t, tt=tt: t.matmul(
                                pS[:], s_onescf[:], s_qf[:], start=(tt == 0), stop=(tt == NCHUNK - 1)))],
                            cneed=max(cp, qs_rd[0]))
                        for (psrc, dstbuf, tkk) in ((pA, s_sqb, tq), (pB, s_skb, tk2)):
                            S.op("vector", lambda v, psrc=psrc: v.tensor_scalar_min(s_t1[:], psrc[:], 0.0),
                                 tneed=tkk)
                            S.op("scalar", lambda sc: sc.activation(s_t2[:], s_t1[:], AF.Exp))
                            rdt = S.op("vector", lambda v, psrc=psrc: v.tensor_scalar_max(s_t1[:], psrc[:], 0.0))
                            S.op("vector", lambda v, dstbuf=dstbuf, tt=tt: v.tensor_tensor(
                                dstbuf[:, tt * 512:(tt + 1) * 512], s_t2[:], s_t1[:], ALU.add))
                            if psrc is pA:
                                q_rd[0] = rdt
                            else:
                                k_rd[0] = rdt
                        for h in range(4):
                            v_rd[0] = S.op("vector", lambda v, h=h, tt=tt: v.tensor_copy(
                                s_vb[:, (h * NCHUNK + tt) * 132:(h * NCHUNK + tt) * 132 + 128],
                                pC[:, h * 128:(h + 1) * 128]), tneed=tv)
                        S.dma("sync", sq_d[tt * 128:(tt + 1) * 128, blk * 512:(blk + 1) * 512],
                              s_sqb[:, tt * 512:(tt + 1) * 512])
                    wt_war[blk % 2] = S.c
                    wt_tw[blk % 2] = tqs
                    qs_rd[0] = S.op("vector", lambda v, blk=blk: v.tensor_copy(
                        s_qsumst[0:1, blk * 512:(blk + 1) * 512], pS[:]), tneed=tqs)
                    S.op("gpsimd", lambda g: g.memset(s_state[:], 0.0))
                    for n in range(NCHUNK):
                        for h in range(4):
                            qp = s_sqb[:, n * 512 + h * 128: n * 512 + (h + 1) * 128]
                            kp = s_skb[:, n * 512 + h * 128: n * 512 + (h + 1) * 128]
                            vt = s_vb[:, (h * NCHUNK + n) * 132:(h * NCHUNK + n) * 132 + 129]
                            stt = s_state[:, h * 132:h * 132 + 129]
                            S.op("vector", lambda v, stt=stt: v.tensor_copy(s_spb[:, 0:129], stt))
                            S.op("tensor", lambda t, qp=qp: t.transpose(pT[:], qp, s_id[:]))
                            S.op("vector", lambda v: v.tensor_copy(s_qpT[:], pT[:]))
                            S.op("tensor", lambda t, kp=kp: t.transpose(pT[:], kp, s_id[:]))
                            S.op("vector", lambda v: v.tensor_copy(s_kpT[:], pT[:]))
                            S.op("tensor", lambda t, kp=kp, vt=vt: t.matmul(pD[:], kp, vt, start=True, stop=True))
                            S.op("vector", lambda v, stt=stt: v.tensor_tensor(stt, stt, pD[:], ALU.add))
                            S.op("tensor", lambda t: t.matmul(pF[:], s_kpT[:], s_qpT[:], start=True, stop=True))
                            S.op("vector", lambda v: v.tensor_tensor(s_ms[:], pF[:], s_triu[:], ALU.mult))
                            S.mm_group([
                                lambda t: t.matmul(pE[:], s_qpT[:], s_spb[:, 0:129], start=True, stop=False),
                                lambda t, vt=vt: t.matmul(pE[:], s_ms[:], vt, start=False, stop=True)])
                            S.op("vector", lambda v, h=h: v.tensor_copy(
                                s_numst[:, h * 128:(h + 1) * 128], pE[:, 0:128]))
                            S.op("vector", lambda v, n=n, h=h, blk=blk: v.tensor_copy(
                                s_denst[:, n * NH + blk * 4 + h: n * NH + blk * 4 + h + 1], pE[:, 128:129]))
                        S.dma("sync", num_d[n * 128:(n + 1) * 128, blk * 512:(blk + 1) * 512], s_numst[:])
                    for h in range(4):
                        S.dma("sync", ag_kvz_in[(blk * 4 + h) * HD:(blk * 4 + h + 1) * HD, :],
                              s_state[:, h * 132:h * 132 + 129])
                S.dma("sync", ag_qs_in, s_qsumst[:])
                S.dma("sync",
                      den_d.rearrange("(n t) h -> t n h", t=128),
                      s_denst[:].rearrange("p (n h) -> p n h", h=NH))
                S.cc("AllGather", ALU.bypass, [ag_kvz_in], [ag_kvz_out])
                S.cc("AllGather", ALU.bypass, [ag_qs_in], [ag_qs_out])

            with ExitStack() as ph:
                S.fence()
                sbl = lambda n, s, t: ph.enter_context(nc.sbuf_tensor(n, s, t))
                psc = lambda n, s, t: ph.enter_context(nc.psum_tensor(n, s, t))
                s_w1b = sbl("s_w1b", [128, 2 * 16 * 512], BF16)
                s_w2b = sbl("s_w2b", [128, 2 * 16 * 512], BF16)
                s_hT = sbl("s_hT0", [128, 44 * 512], BF16)
                s_w3r = sbl("s_w3r", [128, 2 * D], BF16)
                s_sg = sbl("s_sg", [128, 512], F32)
                s_gst = sbl("s_gst", [128, D], F32)
                pf = [psc(f"pff{i}", [128, 512], F32) for i in range(8)]
                BW = 16 * 512

                def load_w12(it, war):
                    hb = it % 11
                    par = (it % 2) * BW
                    tks = []
                    for (woff, dstbuf) in ((0, s_w1b), (D, s_w2b)):
                        h0 = hb * 512
                        done = 0
                        while done < 512:
                            h = h0 + done
                            r = h // R1
                            lo = h - r * R1
                            w = min(512 - done, R1 - lo)
                            tks.append(S.dma(
                                "sync",
                                dstbuf[:, par:par + BW].rearrange("p (dc e) -> p dc e", e=512)[:, :, done:done + w],
                                ag_w12_out[r * 2 * D + woff: r * 2 * D + woff + D, lo:lo + w]
                                .rearrange("(dc p) e -> p dc e", p=128),
                                war=war, detach=True))
                            done += w
                    return tks

                def load_w3(jt, war):
                    hc44 = jt % 44
                    par = (jt % 2) * D
                    tks = []
                    for r8 in range(8):
                        tks.append(S.dma(
                            "sync", s_w3r[:, par + r8 * 256: par + (r8 + 1) * 256],
                            ag_w3_out[r8 * FF0 + hc44 * 128: r8 * FF0 + (hc44 + 1) * 128, :],
                            war=war, detach=True))
                    return tks

                w12_war = [0, 0]
                w12_tk = {}
                w12_tk[0] = load_w12(0, 0)
                w3_war = [0, 0]
                w3_tk = {}
                for half in range(2):
                    hoff = half * 512
                    for hb in range(11):
                        it = half * 11 + hb
                        par = (it % 2) * BW
                        if it + 1 < 22:
                            w12_tk[it + 1] = load_w12(it + 1, w12_war[(it + 1) % 2])
                        for hc in range(4):
                            for (wdst, pdst) in ((s_w1b, pf[0]), (s_w2b, pf[1])):
                                S.mm_group([
                                    (lambda t, dc=dc, wdst=wdst, pdst=pdst, hc=hc, par=par: t.matmul(
                                        pdst[:], wdst[:, par + dc * 512 + hc * 128: par + dc * 512 + (hc + 1) * 128],
                                        s_xnT[:, dc * TOK + hoff: dc * TOK + hoff + 512],
                                        start=(dc == 0), stop=(dc == 15)))
                                    for dc in range(16)], dneed=w12_tk[it])
                            S.op("scalar", lambda sc: sc.activation(s_sg[:], pf[0][:], AF.Sigmoid))
                            S.op("vector", lambda v, hb=hb, hc=hc: v.tensor_tensor(
                                s_hT[:, (hb * 4 + hc) * 512:(hb * 4 + hc + 1) * 512],
                                s_sg[:], pf[1][:], ALU.mult))
                        w12_war[it % 2] = S.c
                    for tp in range(2):
                        for hc44 in range(44):
                            jt = (half * 2 + tp) * 44 + hc44
                            parw = (jt % 2) * D
                            if jt not in w3_tk:
                                w3_tk[jt] = load_w3(jt, w3_war[jt % 2])
                            if hc44 + 1 < 44:
                                w3_tk[jt + 1] = load_w3(jt + 1, w3_war[(jt + 1) % 2])
                            for t2 in range(2):
                                S.mm_group([
                                    (lambda t, hc44=hc44, t2=t2, eb=eb, tp=tp, parw=parw: t.matmul(
                                        pf[t2 * 4 + eb][:],
                                        s_hT[:, hc44 * 512 + (tp * 2 + t2) * 128: hc44 * 512 + (tp * 2 + t2 + 1) * 128],
                                        s_w3r[:, parw + eb * 512: parw + (eb + 1) * 512],
                                        start=(hc44 == 0), stop=(hc44 == 43)))
                                    for eb in range(4)], dneed=w3_tk[jt])
                            w3_war[jt % 2] = S.c
                        for t2 in range(2):
                            for eb in range(4):
                                S.op("scalar", lambda sc, t2=t2, eb=eb: sc.activation(
                                    s_gst[:, eb * 512:(eb + 1) * 512], pf[t2 * 4 + eb][:], AF.Silu))
                            row0 = (half * 4 + tp * 2 + t2) * 128
                            S.dma("sync", g_d[row0:row0 + 128, :], s_gst[:])

            with ExitStack() as ph:
                S.fence()
                sbl = lambda n, s, t: ph.enter_context(nc.sbuf_tensor(n, s, t))
                pha = ph.enter_context(ExitStack())
                psc = lambda n, s, t: pha.enter_context(nc.psum_tensor(n, s, t))
                p_row = psc("p_row", [1, 512], F32)
                p_b16 = psc("p_b16", [128, 4], F32)
                p_b128 = psc("p_b128", [128, NH], F32)
                s_omg = sbl("s_omg", [128, NH], F32)
                s_wb = sbl("s_wb", [128, NH * 4], F32)
                s_attn = sbl("s_attn", [128, NCHUNK * D], BF16)
                phd1 = ph.enter_context(ExitStack())
                sbl = lambda n, s, t: phd1.enter_context(nc.sbuf_tensor(n, s, t))
                s_sq = sbl("s_sq2", [128, NCHUNK * D], BF16)
                s_mem = sbl("s_mem", [128, NH * 512], BF16)
                s_mnT = sbl("s_mnT", [128, NH * 4], BF16)
                s_sbf = sbl("s_sbf", [128, NH * 132], F32)
                s_sbb = sbl("s_sbb", [128, NH * 132], BF16)
                s_t129 = sbl("s_t129", [128, 2 * 132], F32)
                s_pm = sbl("s_pm", [1, 8], F32)
                s_pm128 = sbl("s_pm128", [128, 8], F32)
                s_bm = sbl("s_bm", [8, 1], F32)
                s_q8 = sbl("s_q8", [8, D], F32)
                s_qbar_row = sbl("s_qbar_row", [1, D], F32)
                s_qbar16 = sbl("s_qbar16", [NH, 128], F32)
                s_mnm16 = sbl("s_mnm16", [NH, 128], F32)
                s_tmp16 = sbl("s_tmp16", [NH, 128], F32)
                s_rel = sbl("s_rel", [NH, 4], F32)
                s_mnm4 = sbl("s_mnm4", [4, D], F32)
                s_mns = sbl("s_mns", [4, 1], F32)
                s_mrow = sbl("s_mrow", [1, 4], F32)
                s_e16 = sbl("s_e16", [NH, 4], F32)
                s_ke16 = sbl("s_ke16", [NH, 4], F32)
                s_mx = sbl("s_mx", [NH, 1], F32)
                s_sum = sbl("s_sum", [NH, 1], F32)
                s_gc = sbl("s_gc", [NH, 1], F32)
                s_gr = sbl("s_gr", [1, NH], F32)
                s_wgrow = sbl("s_wgrow", [1, 4], F32)
                s_numt = sbl("s_numt", [128, 2 * 128], F32)
                s_dent = sbl("s_dent", [128, NH], F32)
                s_numtot = sbl("s_numtot", [128, 128], F32)
                s_dtot = sbl("s_dtot", [128, 1], F32)
                s_acc = sbl("s_acc", [128, 128], F32)
                s_cn = sbl("s_cn", [128, 4], F32)
                s_qpT = sbl("s_qpT1", [128, 2 * 128], BF16)

                S.dma("gpsimd", s_mem[:].rearrange("p (h e) -> p h e", e=512), memh2)
                S.dma("gpsimd", s_mnT[:].rearrange("p (h m) -> p h m", m=4), mnT2)
                S.dma("sync", s_sq[:].rearrange("p (n e) -> p n e", e=D),
                      sq_d.rearrange("(n p) e -> p n e", p=128))
                S.dma("sync", s_pm[:], pmask)
                S.op("tensor", lambda t: t.matmul(p_b128[0:128, 0:8], s_ones1f[:], s_pm[:], start=True, stop=True))
                S.op("vector", lambda v: v.tensor_copy(s_pm128[:], p_b128[0:128, 0:8]))
                kv_war = [0, 0]
                kv_tk = {}

                def load_kvz(i):
                    r, h = i % 8, i // 8
                    par = (i % 2) * 132
                    tk = S.dma("sync", s_t129[:, par:par + 129],
                               ag_kvz_out[(r * NH + h) * HD:(r * NH + h + 1) * HD, :],
                               war=kv_war[i % 2], detach=True)
                    return [tk]

                kv_tk[0] = load_kvz(0)
                for h in range(NH):
                    sl = s_sbf[:, h * 132:h * 132 + 129]
                    S.op("gpsimd", lambda g, sl=sl: g.memset(sl, 0.0))
                    for r in range(8):
                        i = h * 8 + r
                        par = (i % 2) * 132
                        if i + 1 < NH * 8:
                            kv_tk[i + 1] = load_kvz(i + 1)
                        S.op("vector", lambda v, sl=sl, r=r, par=par: v.scalar_tensor_tensor(
                            sl, s_t129[:, par:par + 129], s_pm128[:, r:r + 1], sl, ALU.mult, ALU.add),
                            dneed=kv_tk[i])
                        kv_war[i % 2] = S.c
                    S.op("vector", lambda v, h=h: v.tensor_copy(
                        s_sbb[:, h * 132:h * 132 + 129], s_sbf[:, h * 132:h * 132 + 129]))
                S.dma("sync", s_bm[:], bmask)
                S.dma("sync", s_q8[:], ag_qs_out)
                for c4 in range(4):
                    S.op("tensor", lambda t, c4=c4: t.matmul(
                        p_row[:], s_bm[:], s_q8[:, c4 * 512:(c4 + 1) * 512], start=True, stop=True))
                    S.op("vector", lambda v, c4=c4: v.tensor_scalar_mul(
                        s_qbar_row[0:1, c4 * 512:(c4 + 1) * 512], p_row[:], 1.0 / SEQ))
                S.dma("sync", scr_qbar, s_qbar_row[:])
                S.dma("sync", s_qbar16[:], scr_qbar.rearrange("o (h d) -> (o h) d", d=128))
                for m in range(4):
                    S.dma("sync", s_mnm16[:], mnm[m:m + 1, :].rearrange("o (h d) -> (o h) d", d=128))
                    S.op("vector", lambda v: v.tensor_tensor(s_tmp16[:], s_qbar16[:], s_mnm16[:], ALU.mult))
                    S.op("vector", lambda v, m=m: v.reduce_sum(s_rel[:, m:m + 1], s_tmp16[:], axis=AX.X))
                S.op("vector", lambda v: v.tensor_scalar_mul(s_rel[:], s_rel[:], HD ** -0.5))
                S.dma("sync", s_mnm4[:], mnm)
                S.op("vector", lambda v: v.reduce_sum(s_mns[:], s_mnm4[:], axis=AX.X))
                S.dma("sync", scr4m, s_mns[:])
                S.dma("sync", s_mrow[:], scr4m.rearrange("a b -> b a"))
                S.op("vector", lambda v: v.tensor_single_scalar(s_mrow[:], s_mrow[:], 1e-6, ALU.is_lt))
                S.op("tensor", lambda t: t.matmul(p_b16[0:NH, 0:4], s_ones1f[0:1, 0:NH], s_mrow[:], start=True, stop=True))
                S.op("vector", lambda v: v.tensor_copy(s_e16[:], p_b16[0:NH, 0:4]))
                S.op("vector", lambda v: v.tensor_scalar(s_ke16[:], s_e16[:], -1.0, 1.0, ALU.mult, ALU.add))
                S.op("vector", lambda v: v.tensor_tensor(s_rel[:], s_rel[:], s_ke16[:], ALU.mult))
                S.op("vector", lambda v: v.scalar_tensor_tensor(s_rel[:], s_e16[:], -1e30, s_rel[:], ALU.mult, ALU.add))
                S.op("vector", lambda v: v.reduce_max(s_mx[:], s_rel[:], axis=AX.X))
                S.op("vector", lambda v: v.tensor_scalar(s_rel[:], s_rel[:], s_mx[:, 0:1], None, ALU.subtract))
                S.op("scalar", lambda sc: sc.activation(s_rel[:], s_rel[:], AF.Exp))
                S.op("vector", lambda v: v.reduce_sum(s_sum[:], s_rel[:], axis=AX.X))
                S.op("vector", lambda v: v.reciprocal(s_sum[:], s_sum[:]))
                S.op("vector", lambda v: v.tensor_scalar(s_rel[:], s_rel[:], s_sum[:, 0:1], None, ALU.mult))
                S.op("vector", lambda v: v.tensor_tensor(s_rel[:], s_rel[:], s_ke16[:], ALU.mult))
                S.dma("sync", s_gc[:], gate_c)
                S.op("scalar", lambda sc: sc.activation(s_gc[:], s_gc[:], AF.Sigmoid))
                S.op("vector", lambda v: v.tensor_scalar(s_rel[:], s_rel[:], s_gc[:, 0:1], None, ALU.mult))
                S.dma("sync", s_gr[:], gate_r)
                S.op("scalar", lambda sc: sc.activation(s_gr[:], s_gr[:], AF.Sigmoid))
                S.op("vector", lambda v: v.tensor_scalar(s_gr[:], s_gr[:], -1.0, 1.0, ALU.mult, ALU.add))
                S.op("tensor", lambda t: t.matmul(p_b128[:], s_ones1f[:], s_gr[:], start=True, stop=True))
                S.op("vector", lambda v: v.tensor_copy(s_omg[:], p_b128[:]))
                S.dma("sync", scrwg, s_rel[:])
                for h in range(NH):
                    S.dma("sync", s_wgrow[:], scrwg[h:h + 1, :])
                    S.op("tensor", lambda t: t.matmul(p_b16[0:128, 0:4], s_ones1f[:], s_wgrow[:], start=True, stop=True))
                    S.op("vector", lambda v, h=h: v.tensor_copy(s_wb[:, h * 4:(h + 1) * 4], p_b16[0:128, 0:4]))
                pha.close()
                phb = ph.enter_context(ExitStack())
                pscb = lambda n, s, t: phb.enter_context(nc.psum_tensor(n, s, t))
                pT2 = [pscb(f"pT1{i}", [128, 128], BF16) for i in range(2)]
                pE2 = [pscb(f"pE1{i}", [128, 129], F32) for i in range(2)]
                pn2 = [pscb(f"pn1{i}", [128, 4], F32) for i in range(2)]
                pam2 = [pscb(f"pam1{i}", [128, 512], F32) for i in range(2)]
                nm_war = [0, 0]
                nm_tk = {}

                def load_num(i):
                    n, h = i // NH, i % NH
                    par = (i % 2) * 128
                    return [S.dma("sync", s_numt[:, par:par + 128],
                                  num_d[n * 128:(n + 1) * 128, h * 128:(h + 1) * 128],
                                  war=nm_war[i % 2], detach=True)]

                nm_tk[0] = load_num(0)
                sq_ld = S.d["sync"]
                dnw = [[("gpsimd", S.d["gpsimd"])]]
                qpt_rd = [0, 0]
                e_rd = [0, 0]
                nrm_rd = [0, 0]
                am_rd = [0, 0]
                first_pe = [True]
                for n in range(NCHUNK):
                    S.dma("sync", s_dent[:], den_d[n * 128:(n + 1) * 128, :])
                    for h in range(NH):
                        i = n * NH + h
                        par = i % 2
                        parn = (i % 2) * 128
                        if i + 1 < NCHUNK * NH:
                            nm_tk[i + 1] = load_num(i + 1)
                        sqt = s_sq[:, n * D + h * 128: n * D + (h + 1) * 128]
                        dn = [("sync", sq_ld)] if first_pe[0] else None
                        first_pe[0] = False
                        tq = S.top_group([
                            (lambda t, sqt=sqt, par=par: t.transpose(pT2[par][:], sqt, s_id[:]))],
                            cneed=qpt_rd[par], dneed=dn)
                        cpq = S.op("vector", lambda v, par=par: v.tensor_copy(
                            s_qpT[:, par * 128:(par + 1) * 128], pT2[par][:]), tneed=tq)
                        qpt_rd[par] = cpq
                        tb = S.top_group([
                            (lambda t, h=h, par=par: t.matmul(
                                pE2[par][:], s_qpT[:, par * 128:(par + 1) * 128],
                                s_sbb[:, h * 132:h * 132 + 129], start=True, stop=True))],
                            cneed=max(cpq, e_rd[par]))
                        tn = S.top_group([
                            (lambda t, h=h, par=par: t.matmul(
                                pn2[par][:], s_qpT[:, par * 128:(par + 1) * 128],
                                s_mnT[:, h * 4:(h + 1) * 4], start=True, stop=True))],
                            cneed=max(cpq, nrm_rd[par]), dneed=dnw[0])
                        ta = S.top_group([
                            (lambda t, h=h, par=par: t.matmul(
                                pam2[par][:], s_qpT[:, par * 128:(par + 1) * 128],
                                s_mem[:, h * 512:(h + 1) * 512], start=True, stop=True))],
                            cneed=max(cpq, am_rd[par]))
                        dnw[0] = None
                        S.op("vector", lambda v, parn=parn, par=par: v.tensor_tensor(
                            s_numtot[:], s_numt[:, parn:parn + 128], pE2[par][:, 0:128], ALU.add),
                            dneed=nm_tk[i], tneed=tb)
                        nm_war[i % 2] = S.c
                        e_rd[par] = S.op("vector", lambda v, h=h, par=par: v.tensor_tensor(
                            s_dtot[:], s_dent[:, h:h + 1], pE2[par][:, 128:129], ALU.add))
                        S.op("vector", lambda v: v.tensor_scalar_max(s_dtot[:], s_dtot[:], 1e-6))
                        S.op("vector", lambda v: v.reciprocal(s_dtot[:], s_dtot[:]))
                        S.op("vector", lambda v, h=h: v.tensor_tensor(s_dtot[:], s_dtot[:], s_omg[:, h:h + 1], ALU.mult))
                        S.op("vector", lambda v: v.tensor_scalar(s_acc[:], s_numtot[:], s_dtot[:, 0:1], None, ALU.mult))
                        nrm_rd[par] = S.op("vector", lambda v, par=par: v.tensor_scalar_max(
                            s_cn[:], pn2[par][:], 1e-6), tneed=tn)
                        S.op("vector", lambda v: v.reciprocal(s_cn[:], s_cn[:]))
                        S.op("vector", lambda v, h=h: v.tensor_tensor(s_cn[:], s_cn[:], s_wb[:, h * 4:(h + 1) * 4], ALU.mult))
                        for m in range(4):
                            tick = S.op("vector", lambda v, m=m, par=par: v.scalar_tensor_tensor(
                                s_acc[:], pam2[par][:, m * 128:(m + 1) * 128], s_cn[:, m:m + 1],
                                s_acc[:], ALU.mult, ALU.add), tneed=(ta if m == 0 else 0))
                        am_rd[par] = tick
                        S.op("vector", lambda v, n=n, h=h: v.tensor_copy(
                            s_attn[:, n * D + h * 128: n * D + (h + 1) * 128], s_acc[:]))
                phb.close()
                phd1.close()
                S.fence()
                sbl = lambda n, s, t: ph.enter_context(nc.sbuf_tensor(n, s, t))
                pO = ph.enter_context(nc.psum_tensor("pO", [128, 512], F32))
                pT = ph.enter_context(nc.psum_tensor("pT1", [128, 128], BF16))
                s_wo = sbl("s_wo", [128, 16 * D], BF16)
                s_aT = sbl("s_aT", [128, 16 * 128], BF16)
                s_gt = sbl("s_gt", [128, D], F32)
                s_tmp = sbl("s_tmp", [128, 512], F32)
                s_out0 = sbl("s_out0", [128, D], F32)
                S.dma("gpsimd", s_wo[:].rearrange("p (ec d) -> p ec d", d=D), wo_h)
                for tt in range(NCHUNK):
                    for ec in range(16):
                        S.op("tensor", lambda t, tt=tt, ec=ec: t.transpose(
                            pT[:], s_attn[:, tt * D + ec * 128: tt * D + (ec + 1) * 128], s_id[:]))
                        S.op("vector", lambda v, ec=ec: v.tensor_copy(s_aT[:, ec * 128:(ec + 1) * 128], pT[:]))
                    S.dma("sync", s_xt[:], x_s[tt * 128:(tt + 1) * 128, :])
                    S.dma("sync", s_gt[:], g_d[tt * 128:(tt + 1) * 128, :])
                    for eb in range(4):
                        S.mm_group([
                            (lambda t, ec=ec, eb=eb: t.matmul(
                                pO[:], s_aT[:, ec * 128:(ec + 1) * 128],
                                s_wo[:, ec * D + eb * 512: ec * D + eb * 512 + 512],
                                start=(ec == 0), stop=(ec == 15)))
                            for ec in range(16)])
                        S.op("vector", lambda v, eb=eb: v.tensor_tensor(
                            s_tmp[:], pO[:], s_gt[:, eb * 512:(eb + 1) * 512], ALU.mult))
                        S.op("vector", lambda v, eb=eb: v.tensor_tensor(
                            s_out0[:, eb * 512:(eb + 1) * 512], s_tmp[:], s_xt[:, eb * 512:(eb + 1) * 512], ALU.add))
                    S.dma("sync", x1_s[tt * 128:(tt + 1) * 128, :], s_out0[:])

        if has1:
            with ExitStack() as ph:
                S.fence()
                sbl = lambda n, s, t: ph.enter_context(nc.sbuf_tensor(n, s, t))
                psc = lambda n, s, t: ph.enter_context(nc.psum_tensor(n, s, t))
                pRS = psc("pRS", [1, 512], F32)
                s_idf = sbl("s_idf", [128, 128], F32)
                s_ba = sbl("s_ba", [1, 3 * D], BF16)
                s_m128 = sbl("s_m128", [1, 128], BF16)
                s_rs = sbl("s_rs", [1, TOK], F32)
                s_rsh = sbl("s_rsh", [1, TOK], BF16)
                s_rsl = sbl("s_rsl", [1, TOK], BF16)
                s_gsc = sbl("s_gsc", [128, 64], F32)
                s_usc = sbl("s_usc", [128, 64], F32)
                s_dsc = sbl("s_dsc", [128, 16], F32)

                S.dma("sync", s_idf[:], identf)
                S.op("gpsimd", lambda g: g.memset(s_m128[:], -128.0))
                S.dma("gpsimd", s_ba[:], b_attn)
                S.dma("gpsimd", s_gsc[:], gsc2.rearrange("a p -> p a"), allow_slow_non_contiguous=True)
                S.dma("gpsimd", s_usc[:], usc2.rearrange("a p -> p a"), allow_slow_non_contiguous=True)
                S.dma("gpsimd", s_dsc[:], dsc2.rearrange("a p -> p a"), allow_slow_non_contiguous=True)
                with nc.psum_tensor("pT2", [128, 128], BF16) as pT:
                    emit_ln(x1_s, pT)
                for tg in range(2):
                    S.mm_group([
                        (lambda t, dc=dc, tg=tg: t.matmul(
                            pRS[:], s_onescb[:], s_xnT[:, dc * TOK + tg * 512: dc * TOK + (tg + 1) * 512],
                            start=(dc == 0), stop=(dc == 15)))
                        for dc in range(16)])
                    S.op("vector", lambda v, tg=tg: v.tensor_copy(s_rs[0:1, tg * 512:(tg + 1) * 512], pRS[:]))
                S.op("vector", lambda v: v.tensor_copy(s_rsh[:], s_rs[:]))
                S.op("vector", lambda v: v.tensor_tensor(s_rsl[:], s_rs[:], s_rsh[:], ALU.subtract))

                with ExitStack() as ph2:
                    S.fence()
                    sbl2 = lambda n, s, t: ph2.enter_context(nc.sbuf_tensor(n, s, t))
                    pQ2 = [ph2.enter_context(nc.psum_tensor(f"pQ{i}", [128, 512], F32)) for i in range(2)]
                    s_wbk = sbl2("s_wbk", [128, 2 * 16 * 512], BF16)
                    s_qkv = sbl2("s_qkv", [128, 2 * 512], BF16)
                    BW = 16 * 512
                    wa_war = [0, 0]
                    wa_tk = {}

                    def load_wa(ob, war, twar=0):
                        par = (ob % 2) * BW
                        return [S.dma(
                            "gpsimd",
                            s_wbk[:, par:par + BW].rearrange("p (dc e) -> p dc e", e=512),
                            wa_h[ob],
                            war=war, twar=twar, detach=True)]

                    wa_tk[0] = load_wa(0, 0)
                    wa_tw = [0, 0]
                    qkv_ready = S.c
                    cp_rd = [qkv_ready, qkv_ready]
                    for ob in range(12):
                        t0 = ob // 4
                        par = (ob % 2) * BW
                        if ob + 1 < 12:
                            wa_tk[ob + 1] = load_wa(ob + 1, wa_war[(ob + 1) % 2], wa_tw[(ob + 1) % 2])
                        tg = 0
                        for tt in range(NCHUNK):
                            i = ob * NCHUNK + tt
                            pp = i % 2
                            tg = S.top_group([
                                (lambda t, dc=dc, tt=tt, par=par, pp=pp: t.matmul(
                                    pQ2[pp][:], s_xnT[:, dc * TOK + tt * 128: dc * TOK + (tt + 1) * 128],
                                    s_wbk[:, par + dc * 512: par + (dc + 1) * 512], start=(dc == 0), stop=False))
                                for dc in range(16)] + [
                                (lambda t, ob=ob, pp=pp: t.matmul(
                                    pQ2[pp][:], s_ones1b[:], s_ba[:, ob * 512:(ob + 1) * 512], start=False, stop=True))],
                                cneed=cp_rd[pp], dneed=wa_tk[ob])
                            cp_rd[pp] = S.op("vector", lambda v, pp=pp: v.tensor_copy(
                                s_qkv[:, pp * 512:(pp + 1) * 512], pQ2[pp][:]), tneed=tg)
                            for hh2 in range(2):
                                j = 2 * (ob % 4) + hh2
                                base = j * 6144 + t0 * 2048
                                S.dma("sync",
                                      qkvc[base: base + 2048, :]
                                      .rearrange("(s q) e -> s q e", q=TOK)[:, tt * 128:(tt + 1) * 128, :]
                                      .rearrange("s p e -> p s e"),
                                      s_qkv[:, pp * 512 + hh2 * 256: pp * 512 + (hh2 + 1) * 256]
                                      .rearrange("p (s e) -> p s e", e=128))
                        wa_war[ob % 2] = S.c
                        wa_tw[ob % 2] = tg
                    S.cc("AllToAll", ALU.bypass, [qkvc], [qkvg])

                with ExitStack() as ph2:
                    S.fence()
                    sbl2 = lambda n, s, t: ph2.enter_context(nc.sbuf_tensor(n, s, t))
                    psc2 = lambda n, s, t: ph2.enter_context(nc.psum_tensor(n, s, t))
                    pG2 = [psc2(f"pG{i}", [128, 512], F32) for i in range(2)]
                    pU2 = [psc2(f"pU{i}", [128, 512], F32) for i in range(2)]
                    pDn2 = [psc2(f"pDn{i}", [128, 512], F32) for i in range(2)]
                    pTf = psc2("pTf2", [128, 128], F32)
                    s_gw = sbl2("s_gw", [128, 2 * 16 * 256], BF16)
                    s_uw = sbl2("s_uw", [128, 2 * 16 * 256], BF16)
                    s_hT = sbl2("s_hT1", [128, 64 * 512], BF16)
                    s_dw = sbl2("s_dw", [128, 2 * 4096], BF16)
                    s_sg2 = sbl2("s_sg2", [128, 512], F32)
                    s_g1 = sbl2("s_g1", [128, 512], F32)
                    s_rsh2 = sbl2("s_rsh2", [1, TOK], BF16)
                    s_rsl2 = sbl2("s_rsl2", [1, TOK], BF16)
                    BWG = 16 * 256
                    gu_war = [0, 0]
                    gu_tk = {}

                    def load_gu(it, war, twar=0):
                        hb32 = it % 32
                        par = (it % 2) * BWG
                        tks = []
                        for (wsrc, wdst) in ((gw_h, s_gw), (up_h, s_uw)):
                            tks.append(S.dma(
                                "gpsimd",
                                wdst[:, par:par + BWG].rearrange("p (dc e) -> p dc e", e=256),
                                wsrc[hb32],
                                war=war, twar=twar, detach=True))
                        return tks

                    dw_tk = {}

                    def load_dw(half, ec, q4, war, twar=0):
                        par = (q4 % 4) * 2048
                        return [S.dma(
                            "gpsimd", s_dw[:, par:par + 2048],
                            dwe32[ec][:, q4 * 2048:(q4 + 1) * 2048],
                            war=war, twar=twar, detach=True)]

                    rs_ready = S.c
                    sil_t = [0, 0]
                    stt_t = [0, 0]
                    gu_tw = [0, 0]
                    sil_dn = [0, 0]
                    tf_cp = [0]
                    gu_tk[0] = load_gu(0, S.c)
                    for half in range(2):
                        hoff = half * 512
                        for hb32 in range(32):
                            it = half * 32 + hb32
                            par = (it % 2) * BWG
                            if it + 1 < 64:
                                gu_tk[it + 1] = load_gu(it + 1, gu_war[(it + 1) % 2], gu_tw[(it + 1) % 2])
                            tks = []
                            for hc in range(2):
                                ic = hb32 * 2 + hc
                                q = ic % 2
                                tg = S.top_group([
                                    (lambda t, dc=dc, hc=hc, par=par, q=q: t.matmul(
                                        pG2[q][:], s_gw[:, par + dc * 256 + hc * 128: par + dc * 256 + (hc + 1) * 128],
                                        s_xnT[:, dc * TOK + hoff: dc * TOK + hoff + 512],
                                        start=(dc == 0), stop=False))
                                    for dc in range(16)] + [
                                    (lambda t, sr=sr, q=q: t.matmul(
                                        pG2[q][:], s_m128[:], sr[0:1, hoff:hoff + 512], start=False,
                                        stop=(sr is s_rsl)))
                                    for sr in (s_rsh, s_rsl)],
                                    cneed=max(sil_t[q], rs_ready), dneed=gu_tk[it])
                                tu = S.top_group([
                                    (lambda t, dc=dc, hc=hc, par=par, q=q: t.matmul(
                                        pU2[q][:], s_uw[:, par + dc * 256 + hc * 128: par + dc * 256 + (hc + 1) * 128],
                                        s_xnT[:, dc * TOK + hoff: dc * TOK + hoff + 512],
                                        start=(dc == 0), stop=False))
                                    for dc in range(16)] + [
                                    (lambda t, sr=sr, q=q: t.matmul(
                                        pU2[q][:], s_m128[:], sr[0:1, hoff:hoff + 512], start=False,
                                        stop=(sr is s_rsl)))
                                    for sr in (s_rsh, s_rsl)],
                                    cneed=max(stt_t[q], rs_ready), dneed=gu_tk[it])
                                tks.append(tu)
                                sil_t[q] = S.op("scalar", lambda sc, ic=ic, q=q: sc.activation(
                                    s_sg2[:], pG2[q][:], AF.Silu, scale=s_gsc[:, ic:ic + 1]), tneed=tg)
                                stt_t[q] = S.op("vector", lambda v, ic=ic, q=q: v.scalar_tensor_tensor(
                                    s_hT[:, ic * 512:(ic + 1) * 512], pU2[q][:], s_usc[:, ic:ic + 1],
                                    s_sg2[:], ALU.mult, ALU.mult), tneed=tu)
                            gu_war[it % 2] = S.c
                            gu_tw[it % 2] = tks[-1]
                        ht_done = S.c
                        hr = S.top_group([
                            (lambda t, icc=icc: t.matmul(
                                pRS[:], s_onescb[:], s_hT[:, icc * 512:(icc + 1) * 512],
                                start=(icc == 0), stop=(icc == 63)))
                            for icc in range(64)], cneed=ht_done)
                        S.op("vector", lambda v: v.tensor_copy(s_rs[0:1, hoff:hoff + 512], pRS[:]), tneed=hr)
                        S.op("vector", lambda v: v.tensor_copy(
                            s_rsh2[0:1, hoff:hoff + 512], s_rs[0:1, hoff:hoff + 512]))
                        S.op("vector", lambda v: v.tensor_tensor(
                            s_rsl2[0:1, hoff:hoff + 512], s_rs[0:1, hoff:hoff + 512],
                            s_rsh2[0:1, hoff:hoff + 512], ALU.subtract))
                        rs2_ready = S.c
                        for ec in range(16):
                            pe2 = ec % 2
                            for q4 in range(4):
                                if (half, ec, q4) not in dw_tk:
                                    dw_tk[(half, ec, q4)] = load_dw(half, ec, q4, S.c)
                            td = 0
                            for q4 in range(4):
                                par4 = q4 * 2048
                                grp = [
                                    (lambda t, ic=ic, par4=par4, q4=q4, pe2=pe2: t.matmul(
                                        pDn2[pe2][:], s_dw[:, par4 + (ic - q4 * 16) * 128: par4 + (ic - q4 * 16 + 1) * 128],
                                        s_hT[:, ic * 512:(ic + 1) * 512],
                                        start=(ic == 0), stop=False))
                                    for ic in range(q4 * 16, (q4 + 1) * 16)]
                                if q4 == 3:
                                    grp += [
                                        (lambda t, sr=sr, pe2=pe2: t.matmul(
                                            pDn2[pe2][:], s_m128[:], sr[0:1, hoff:hoff + 512], start=False,
                                            stop=(sr is s_rsl2)))
                                        for sr in (s_rsh2, s_rsl2)]
                                cn = ht_done
                                if q4 == 0:
                                    cn = max(cn, sil_dn[pe2])
                                if q4 == 3:
                                    cn = max(cn, rs2_ready)
                                td = S.top_group(grp, cneed=cn, dneed=dw_tk[(half, ec, q4)])
                                nh, ne = (half, ec + 1) if ec + 1 < 16 else (half + 1, 0)
                                if nh < 2:
                                    dw_tk[(nh, ne, q4)] = load_dw(nh, ne, q4, 0, twar=td)
                            sil_dn[pe2] = S.op("scalar", lambda sc, ec=ec, pe2=pe2: sc.activation(
                                s_g1[:], pDn2[pe2][:], AF.Silu, scale=s_dsc[:, ec:ec + 1]), tneed=td)
                            for sub in range(4):
                                tf = S.top_group([
                                    (lambda t, sub=sub: t.transpose(
                                        pTf[:], s_g1[:, sub * 128:(sub + 1) * 128], s_idf[:]))],
                                    cneed=max(sil_dn[pe2], tf_cp[0]))
                                tf_cp[0] = S.op("vector", lambda v: v.tensor_copy(s_sg2[:, 0:128], pTf[:]),
                                                tneed=tf)
                                S.dma("sync",
                                      g1_d[hoff + sub * 128: hoff + (sub + 1) * 128, ec * 128:(ec + 1) * 128],
                                      s_sg2[:, 0:128])

            with ExitStack() as ph:
                S.fence()
                sbl = lambda n, s, t: ph.enter_context(nc.sbuf_tensor(n, s, t))
                psc = lambda n, s, t: ph.enter_context(nc.psum_tensor(n, s, t))
                pS1 = [psc("pS1a", [128, 512], F32), psc("pS1b", [128, 512], F32)]
                pST = [psc("pSTa", [128, 512], F32), psc("pSTb", [128, 512], F32)]
                pY = [psc(f"pY{i}", [128, 512], F32) for i in range(4)]
                s_q = sbl("s_qh", [128, 2 * SEQ], BF16)
                s_k = sbl("s_kh", [128, 2 * SEQ], BF16)
                s_v = sbl("s_vh", [128, 2 * 32 * 132], BF16)
                s_dm = sbl("s_dm", [128, 4 * 512], BF16)
                s_idf2 = sbl("s_idf2", [128, 128], F32)
                s_nma = sbl("s_nma", [128, 32], F32)
                s_mt = sbl("s_mt", [128, 1], F32)
                s_negm_f = sbl("s_negm_f", [1, 8 * 512], F32)
                s_negm_bf = sbl("s_negm_bf", [1, 8 * 512], BF16)
                s_pt = sbl("s_pt", [128, 2 * 512], BF16)
                s_l = sbl("s_l", [128, 1], F32)
                s_yst = sbl("s_yst", [128, 512], BF16)
                S.dma("sync", s_idf2[:], identf)
                S.dma("sync", s_dm[:].rearrange("p (r w) -> p r w", w=512), dm.rearrange("r p w -> p r w"))
                SCc = HD ** -0.5
                VW = 32 * 132

                def load_slot(slot):
                    bs, hsub = slot // 2, slot % 2
                    pq = (slot % 2) * SEQ
                    pv = (slot % 2) * VW
                    vsl = s_v[:, pv:pv + VW]
                    tw = S.t
                    S.op("gpsimd", lambda g, vsl=vsl: g.memset(vsl, 0.0), tneed=tw)
                    S.op("gpsimd", lambda g, vsl=vsl: g.memset(
                        vsl.rearrange("p (c w) -> p c w", w=132)[:, :, 128:129], 1.0))
                    war = S.c
                    tks = []
                    for rho in range(4):
                        r = 4 * bs + rho
                        tks.append(S.dma(
                            "sync", s_q[:, pq + rho * TOK: pq + (rho + 1) * TOK],
                            qkvg[r * 6144 + hsub * TOK: r * 6144 + (hsub + 1) * TOK, :],
                            war=war, twar=tw, detach=True, transpose=True))
                        tks.append(S.dma(
                            "sync", s_k[:, pq + rho * TOK: pq + (rho + 1) * TOK],
                            qkvg[r * 6144 + 2048 + hsub * TOK: r * 6144 + 2048 + (hsub + 1) * TOK, :],
                            war=war, twar=tw, detach=True, transpose=True))
                        tks.append(S.dma(
                            "sync",
                            vsl.rearrange("p (c w) -> p c w", w=132)[:, rho * 8:(rho + 1) * 8, 0:128],
                            qkvg[r * 6144 + 2 * 2048 + hsub * TOK: r * 6144 + 2 * 2048 + (hsub + 1) * TOK, :]
                            .rearrange("(c p) e -> p c e", p=128),
                            war=war, twar=tw, detach=True))
                    return tks

                slot_tk = {0: load_slot(0)}
                red_t = [0, 0]
                exp_t = [0, 0]
                pv_last = [0, 0]
                fin_t = [0, 0, 0, 0]
                cp_t = [0]
                for slot in range(4):
                    bs, hsub = slot // 2, slot % 2
                    pq = (slot % 2) * SEQ
                    pv = (slot % 2) * VW
                    if slot + 1 < 4:
                        slot_tk[slot + 1] = load_slot(slot + 1)
                    sqh = s_q[:, pq:pq + SEQ]
                    skh = s_k[:, pq:pq + SEQ]
                    svh = s_v[:, pv:pv + VW]
                    S.op("gpsimd", lambda g: g.memset(s_nma[:], 0.0), tneed=S.t)
                    k = 0
                    for ig in range(8):
                        for j in range(4):
                            qt = ig * 4 + j
                            for bb in range(qt // 4):
                                par = k % 2
                                tk = S.top_group([
                                    (lambda t, qt=qt, bb=bb, par=par: t.matmul(
                                        pS1[par][:], sqh[:, qt * 128:(qt + 1) * 128],
                                        skh[:, bb * 512:(bb + 1) * 512], start=True, stop=True))],
                                    cneed=red_t[par], dneed=(slot_tk[slot] if k == 0 else None))
                                S.op("vector", lambda v, par=par: v.reduce_max(
                                    s_mt[:], pS1[par][:], axis=AX.X, negate=True), tneed=tk)
                                red_t[par] = S.op("vector", lambda v, qt=qt: v.tensor_tensor(
                                    s_nma[:, qt:qt + 1], s_nma[:, qt:qt + 1], s_mt[:], ALU.min))
                                k += 1
                    nm_ready = S.c
                    for qt in range(32):
                        tf = S.top_group([
                            (lambda t, qt=qt: t.transpose(
                                pST[0][0:1, 0:128], s_nma[:, qt:qt + 1], s_idf2[:]))],
                            cneed=max(nm_ready, cp_t[0], exp_t[0]))
                        cp_t[0] = S.op("vector", lambda v, qt=qt: v.tensor_copy(
                            s_negm_f[0:1, qt * 128:(qt + 1) * 128], pST[0][0:1, 0:128]), tneed=tf)
                    S.op("vector", lambda v: v.tensor_copy(s_negm_bf[:], s_negm_f[:]))
                    nb_ready = S.c
                    exp_t = [max(exp_t[0], nb_ready), exp_t[1]]
                    for ig in range(8):
                        nC = 4 * ig + 4
                        st_tk = {}

                        def emit_st(c):
                            par = c % 2
                            st_tk[c] = S.top_group([
                                (lambda t, c=c, ig=ig, par=par: t.matmul(
                                    pST[par][:], skh[:, c * 128:(c + 1) * 128],
                                    sqh[:, ig * 512:(ig + 1) * 512], start=True, stop=False)),
                                (lambda t, ig=ig, par=par: t.matmul(
                                    pST[par][:], s_ones1b[:],
                                    s_negm_bf[0:1, ig * 512:(ig + 1) * 512], start=False, stop=True))],
                                cneed=max(nb_ready, exp_t[par]))

                        emit_st(0)
                        for c in range(nC):
                            par = c % 2
                            exp_t[par] = S.op("scalar", lambda sc, par=par: sc.activation(
                                s_pt[:, par * 512:(par + 1) * 512], pST[par][:], AF.Exp, scale=SCc),
                                tneed=max(st_tk[c], pv_last[par]))
                            if c + 1 < nC:
                                emit_st(c + 1)
                            if c >= 4 * ig:
                                rel = c - 4 * ig
                                S.op("vector", lambda v, rel=rel, par=par: v.tensor_tensor(
                                    s_pt[:, par * 512:(par + 1) * 512], s_pt[:, par * 512:(par + 1) * 512],
                                    s_dm[:, rel * 512:(rel + 1) * 512], ALU.mult))
                            pt_ready = S.c
                            for sub in range(4):
                                if 4 * ig + sub >= c:
                                    cn = pt_ready if c > 0 else max(pt_ready, fin_t[sub])
                                    pv_last[par] = S.top_group([
                                        (lambda t, sub=sub, c=c, ig=ig, par=par: t.matmul(
                                            pY[sub][0:128, 0:129],
                                            s_pt[:, par * 512 + sub * 128: par * 512 + (sub + 1) * 128],
                                            svh[:, c * 132:c * 132 + 129],
                                            start=(c == 0), stop=(c == 4 * ig + sub)))],
                                        cneed=cn)
                                    if c == 4 * ig + sub:
                                        fin_t[sub] = pv_last[par]
                        for sub in range(4):
                            S.op("vector", lambda v, sub=sub: v.reciprocal(s_l[:], pY[sub][:, 128:129]),
                                 tneed=fin_t[sub])
                            fin_t[sub] = S.op("vector", lambda v, sub=sub: v.tensor_scalar(
                                s_yst[:, sub * 128:(sub + 1) * 128], pY[sub][:, 0:128],
                                s_l[:, 0:1], None, ALU.mult))
                        j2 = 4 * bs + ig // 2
                        base = j2 * 2048 + hsub * TOK + (ig % 2) * 512
                        S.dma("sync",
                              yc[base:base + 512, :].rearrange("(s q) e -> q s e", q=128),
                              s_yst[:].rearrange("p (s e) -> p s e", e=128))
                S.cc("AllToAll", ALU.bypass, [yc], [yg])

            with ExitStack() as ph:
                S.fence()
                sbl = lambda n, s, t: ph.enter_context(nc.sbuf_tensor(n, s, t))
                pGf = ph.enter_context(nc.psum_tensor("pGf", [128, 512], F32))
                s_yT = sbl("s_yT", [128, 16 * TOK], BF16)
                s_wp = sbl("s_wp", [128, 16 * D], BF16)
                s_bp = sbl("s_bp", [1, D], BF16)
                s_gt = sbl("s_gt2", [128, D], F32)
                s_tmp = sbl("s_tmp2", [128, 512], F32)
                s_outg = sbl("s_outg", [128, D], F32)
                S.dma("gpsimd", s_bp[:], b_proj)
                for h in range(NH):
                    S.dma("sync", s_yT[:, h * TOK:(h + 1) * TOK],
                          yg[(h // 2) * 2048 + (h % 2) * TOK: (h // 2) * 2048 + (h % 2 + 1) * TOK, :],
                          transpose=True)
                S.dma("gpsimd", s_wp[:].rearrange("p (cc d) -> p cc d", d=D), wp_h)
                for tt in range(NCHUNK):
                    S.dma("sync", s_xt[:], x1_s[tt * 128:(tt + 1) * 128, :])
                    S.dma("sync", s_gt[:], g1_d[tt * 128:(tt + 1) * 128, :])
                    for eb in range(4):
                        S.mm_group([
                            (lambda t, cc=cc, eb=eb, tt=tt: t.matmul(
                                pGf[:], s_yT[:, cc * TOK + tt * 128: cc * TOK + (tt + 1) * 128],
                                s_wp[:, cc * D + eb * 512: cc * D + eb * 512 + 512],
                                start=(cc == 0), stop=False))
                            for cc in range(16)] + [
                            lambda t, eb=eb: t.matmul(
                                pGf[:], s_ones1b[:], s_bp[:, eb * 512:(eb + 1) * 512], start=False, stop=True)])
                        S.op("vector", lambda v, eb=eb: v.tensor_tensor(
                            s_tmp[:], pGf[:], s_gt[:, eb * 512:(eb + 1) * 512], ALU.mult))
                        S.op("vector", lambda v, eb=eb: v.tensor_tensor(
                            s_outg[:, eb * 512:(eb + 1) * 512], s_tmp[:], s_xt[:, eb * 512:(eb + 1) * 512], ALU.add))
                    S.dma("sync", out_s[tt * 128:(tt + 1) * 128, :], s_outg[:])

        S.finish()
    return nc


def _mk_inputs(inputs, parts="L0L1", x1=None):
    has0 = "L0" in parts
    has1 = "L1" in parts
    ident = np.eye(128, dtype=BF)
    maps = []
    if has0:
        x = np.ascontiguousarray(inputs["L0_attn_hidden_states"].reshape(SEQ * BATCH, D))
        def _colmajor(wT, nb, w):
            return np.ascontiguousarray(wT.reshape(16, 128, nb, w).transpose(2, 1, 0, 3))
        wq_hv = _colmajor(inputs["L0_attn_w_q"].T, 4, 512)
        wk_hv = _colmajor(inputs["L0_attn_w_k"].T, 4, 512)
        wv_hv = _colmajor(inputs["L0_attn_w_v"].T, 4, 512)
        wo_hv = np.ascontiguousarray(inputs["L0_attn_w_o"].T.reshape(16, 128, D).transpose(1, 0, 2))
        memories = inputs["L0_attn_memories"]
        memh2v = np.ascontiguousarray(memories.transpose(2, 1, 0, 3).reshape(HD, NH, 512))
        mnT2v = np.ascontiguousarray(inputs["L0_attn_memory_norms"].transpose(2, 1, 0))
        mnmv = np.ascontiguousarray(inputs["L0_attn_memory_norms"].reshape(4, D))
        gate_cv = np.ascontiguousarray(inputs["L0_attn_gate"].reshape(NH, 1))
        gate_rv = np.ascontiguousarray(inputs["L0_attn_gate"].reshape(1, NH))
        triuv = np.triu(np.ones((128, 128), BF))
    if has1:
        b_attnv = np.ascontiguousarray(inputs["L1_attn_b_attn"].reshape(1, 3 * D))
        def _colmajor(wT, nb, w):
            return np.ascontiguousarray(wT.reshape(16, 128, nb, w).transpose(2, 1, 0, 3))
        wa_hv = _colmajor(inputs["L1_attn_W_attn"], 12, 512)
        gw_hv = _colmajor(inputs["L1_ffn_gate_w"].T, 32, 256)
        up_hv = _colmajor(inputs["L1_ffn_up_w"].T, 32, 256)
        wp_hv = np.ascontiguousarray(inputs["L1_attn_W_proj"].reshape(16, 128, D).transpose(1, 0, 2))
        dwe32v = np.ascontiguousarray(
            inputs["L1_ffn_down_w"].T.reshape(64, 128, 16, 128).transpose(2, 1, 0, 3).reshape(16, 128, FF1))
        gsc2v = np.ascontiguousarray(inputs["L1_ffn_gate_scale"].reshape(64, 128))
        usc2v = np.ascontiguousarray(inputs["L1_ffn_up_scale"].reshape(64, 128))
        dsc2v = np.ascontiguousarray(inputs["L1_ffn_down_scale"].reshape(16, 128))
        b_projv = np.ascontiguousarray(inputs["L1_attn_b_proj"].reshape(1, D))
        dmm = np.zeros((4, 128, 512), BF)
        for r in range(4):
            for tk in range(128):
                dmm[r, tk, r * 128 + tk:] = 1.0
        identfv = np.eye(128, dtype=np.float32)
    for r in range(NCORES):
        m = {"ident": ident}
        if has0:
            b = r // 4
            pm = np.zeros((1, 8), np.float32)
            pm[0, 4 * b:r] = 1.0
            bm = np.zeros((8, 1), np.float32)
            bm[4 * b:4 * b + 4, 0] = 1.0
            m.update({
                "x_s": np.ascontiguousarray(x[r * TOK:(r + 1) * TOK]),
                "w1s": _shard_rows(inputs["L0_ffn_w1"], r),
                "w2s": _shard_rows(inputs["L0_ffn_w2"], r),
                "w3s": _shard_rows(inputs["L0_ffn_w3"], r),
                "wq_h": wq_hv, "wk_h": wk_hv, "wv_h": wv_hv, "wo_h": wo_hv,
                "memh2": memh2v, "mnT2": mnT2v, "mnm": mnmv,
                "gate_c": gate_cv, "gate_r": gate_rv,
                "triu": triuv, "pmask": pm, "bmask": bm,
            })
        else:
            m["x1_s"] = np.ascontiguousarray(x1[r])
        if has1:
            m.update({
                "wa_h": wa_hv, "b_attn": b_attnv,
                "gw_h": gw_hv, "up_h": up_hv, "dwe32": dwe32v,
                "gsc2": gsc2v, "usc2": usc2v, "dsc2": dsc2v,
                "wp_h": wp_hv, "b_proj": b_projv,
                "dm": dmm, "identf": identfv,
            })
        maps.append(m)
    return maps


def kernel_mk1(inputs):
    nc = build_mk("L0L1")
    maps = _mk_inputs(inputs, "L0L1")
    res = run_bass_kernel_spmd(nc, maps, core_ids=list(range(NCORES)))
    rs = res.results
    out = np.concatenate([rs[r]["out_s"] for r in range(NCORES)], axis=0)
    return out.reshape(BATCH, SEQ, D)


def kernel_mk2(inputs):
    nc0 = build_mk("L0")
    res0 = run_bass_kernel_spmd(nc0, _mk_inputs(inputs, "L0"), core_ids=list(range(NCORES)))
    x1 = np.stack([res0.results[r]["x1_s"] for r in range(NCORES)], axis=0)
    nc1 = build_mk("L1")
    res1 = run_bass_kernel_spmd(nc1, _mk_inputs(inputs, "L1", x1=x1), core_ids=list(range(NCORES)))
    out = np.concatenate([res1.results[r]["out_s"] for r in range(NCORES)], axis=0)
    return out.reshape(BATCH, SEQ, D)



def _shard_rows(a, r, n=NCORES):
    rows = a.shape[0] // n
    return np.ascontiguousarray(a[r * rows:(r + 1) * rows])


def run_A(inputs):
    nc = build_A()
    in_maps = []
    for r in range(NCORES):
        in_maps.append({
            "w1s": _shard_rows(inputs["L0_ffn_w1"], r),
            "w2s": _shard_rows(inputs["L0_ffn_w2"], r),
            "w3s": _shard_rows(inputs["L0_ffn_w3"], r),
            "gws": _shard_rows(inputs["L1_ffn_gate_w"], r),
            "uws": _shard_rows(inputs["L1_ffn_up_w"], r),
            "dws": _shard_rows(inputs["L1_ffn_down_w"], r),
            "gsc": _shard_rows(inputs["L1_ffn_gate_scale"], r),
            "usc": _shard_rows(inputs["L1_ffn_up_scale"], r),
            "dsc": _shard_rows(inputs["L1_ffn_down_scale"], r),
        })
    res = run_bass_kernel_spmd(nc, in_maps, core_ids=list(range(NCORES)))
    rs = res.results
    gsums = np.concatenate([rs[r]["gsum"] for r in range(NCORES)], axis=0)
    gwq = np.concatenate([rs[r]["gwq"] for r in range(NCORES)], axis=0)
    uwq = np.concatenate([rs[r]["uwq"] for r in range(NCORES)], axis=0)
    dwq = np.concatenate([rs[r]["dwq"] for r in range(NCORES)], axis=0)
    return gsums, gwq, uwq, dwq


def run_B(inputs, gsums):
    nc = build_B()
    in_maps = []
    for r in range(NCORES):
        in_maps.append({
            "w1s": _shard_rows(inputs["L0_ffn_w1"], r),
            "w2s": _shard_rows(inputs["L0_ffn_w2"], r),
            "w3s": _shard_rows(inputs["L0_ffn_w3"], r),
            "gsums": gsums,
        })
    res = run_bass_kernel_spmd(nc, in_maps, core_ids=list(range(NCORES)))
    rs = res.results
    w1q = np.concatenate([rs[r]["w1q"] for r in range(NCORES)], axis=0)
    w2q = np.concatenate([rs[r]["w2q"] for r in range(NCORES)], axis=0)
    w3q = np.concatenate([rs[r]["w3q"] for r in range(NCORES)], axis=0)
    return w1q, w2q, w3q


def kernel(**inputs):
    return kernel_mk1(inputs)
